# version 04
# baseline (speedup 1.0000x reference)
import contextlib
import numpy as np
import ml_dtypes
import concourse.bass as bass
import concourse.mybir as mybir
from concourse.bass_utils import run_bass_kernel_spmd

F32 = mybir.dt.float32
BF16 = mybir.dt.bfloat16
I32 = mybir.dt.int32
AF = mybir.ActivationFunctionType
ALU = mybir.AluOpType
AX = mybir.AxisListType

D = 1024
H = 16
HD = 64
DH = 4096
C = 64
NB = 2
HPG = 8


class Buf:
    def __init__(self, t, name=""):
        self.t = t
        self.name = name
        self.wtok = None
        self.rtoks = []


class Prog:
    def __init__(self):
        self.steps = []
        self.cnt = {"t": 0, "v": 0, "s": 0, "g": 0, "ld": 0, "st": 0}
        self.esem = {"tensor": "t", "vector": "v", "scalar": "s", "gpsimd": "g"}

    def add(self, eng, fn, reads=(), writes=(), dma=None, n_dma=1):
        waits = {}
        for b in reads:
            if b.wtok is not None:
                s, v = b.wtok
                waits[s] = max(waits.get(s, 0), v)
        for b in writes:
            toks = ([b.wtok] if b.wtok else []) + b.rtoks
            for (s, v) in toks:
                waits[s] = max(waits.get(s, 0), v)
        if dma:
            sem = dma
            self.cnt[sem] += 16 * n_dma
        else:
            sem = self.esem[eng]
            self.cnt[sem] += 1
        self.steps.append((eng, waits, fn, sem,
                           frozenset(id(b) for b in reads),
                           frozenset(id(b) for b in writes)))
        tok = (sem, self.cnt[sem])
        for b in reads:
            b.rtoks.append(tok)
        for b in writes:
            b.wtok = tok
            b.rtoks = []
        return tok

    def wait_all(self, eng, skip=()):
        snap = dict(self.cnt)
        own = self.esem.get(eng)

        def fn(e):
            return None
        fn._waits = {s: v for s, v in snap.items() if s != own and s not in skip and v > 0}
        self.steps.append((eng, fn._waits, None, None, frozenset(), frozenset()))

    def emit_into(self, block, sems):
        def mk(en):
            def run(eng):
                own = self.esem.get(en)
                hw = {}
                prev_w = frozenset()
                for (e, waits, fn, sem, rds, wrs) in self.steps:
                    if e != en:
                        continue
                    for s, v in waits.items():
                        if s == own:
                            continue
                        if hw.get(s, 0) >= v:
                            continue
                        eng.wait_ge(sems[s], v)
                        hw[s] = v
                    if fn is None:
                        continue
                    if en == "vector" and (prev_w & rds):
                        eng.tensor_copy(out=self.junk[:, 0:1], in_=self.junk[:, 1:2])
                    if en == "vector":
                        prev_w = wrs
                    out = fn(eng)
                    if sem in ("ld", "st"):
                        for ins in (out if isinstance(out, (list, tuple)) else [out]):
                            ins.then_inc(sems[sem], 16)
                    else:
                        out.then_inc(sems[sem], 1)
            return run
        block.sync(mk("sync"))
        block.vector(mk("vector"))
        block.scalar(mk("scalar"))
        block.tensor(mk("tensor"))
        block.gpsimd(mk("gpsimd"))


def build_program(T, dbg=(), stop_after=None, max_steps=None):
    ST_A = 128
    ST_B = 256
    NCH = ST_A // C
    NST_A = T // ST_A
    NST_B = T // ST_B
    NTA = 2 * ST_A // 128
    NTB = 2 * ST_B // 128
    WA = 2 * ST_A
    WB = 2 * ST_B
    HBW = ST_B + 4

    nc = bass.Bass("TRN2", target_bir_lowering=False, debug=False)

    x_in = nc.dram_tensor("x", [NB * T, D], F32, kind="ExternalInput").ap()
    ws_in = {}
    for nm, shp in (("Wq", [D, D]), ("Wk", [D, D]), ("Wv", [D, D]), ("Wgamma", [D, H]),
                    ("Wf1", [D, HD]), ("Wf2", [HD, D]), ("Wog1", [D, HD]),
                    ("Wog2", [HD, D]), ("norm_w", [1, D]), ("Wo", [D, D]),
                    ("w1", [DH, D]), ("conv_w", [DH, 3]), ("conv_b", [1, DH]),
                    ("w2", [D, DH]), ("eye128", [128, 128]), ("cumtri", [128, 128]),
                    ("mlow", [128, C]), ("mupp", [128, C]), ("mud", [128, C]),
                    ("eye128f", [128, 128]),
                    ("ipair", [128, C]), ("ones1", [128, 1]), ("onesr", [1, 128]), ("onesrf", [1, 128])):
        dt = BF16 if nm in ("eye128", "cumtri", "ipair", "ones1", "onesr") else F32
        ws_in[nm] = nc.dram_tensor(nm, shp, dt, kind="ExternalInput").ap()

    y_out = nc.dram_tensor("y", [NB * T, D], F32, kind="ExternalOutput").ap()
    x1_sc = nc.dram_tensor("x1sc", [NB * T, D], F32, kind="Internal").ap()
    sc_sc = nc.dram_tensor("scsc", [8, 4], F32, kind="Internal").ap()
    row_sc = nc.dram_tensor("rowsc", [4, 512], F32, kind="Internal").ap()
    w1q_sc = nc.dram_tensor("w1qsc", [D, DH], BF16, kind="Internal").ap()
    y_sc = nc.dram_tensor("ysc", [32 * 2 * 128, 256], F32, kind="Internal").ap()
    w2q_sc = nc.dram_tensor("w2qsc", [DH, D], BF16, kind="Internal").ap()

    dbg_outs = {}
    for name, shape in dbg:
        dbg_outs[name] = nc.dram_tensor("dbg_" + name, list(shape), F32,
                                        kind="ExternalOutput").ap()

    P = Prog()

    rowB = Buf(None, "row_sc")
    scB = Buf(None, "sc_sc")
    woqB = Buf(None, "woq_sc")
    w1qB = Buf(None, "w1q_sc")
    w2qB = Buf(None, "w2q_sc")
    x1B = Buf(None, "x1_sc")
    yscB = Buf(None, "y_sc")

    with contextlib.ExitStack() as ES:
        sems = {s: ES.enter_context(nc.semaphore("sem_" + s))
                for s in ("t", "v", "s", "g", "ld", "st")}

        def sb(name, p, f, dt):
            return Buf(ES.enter_context(nc.sbuf_tensor(name, [p, f], dt)), name)

        P.junk = ES.enter_context(nc.sbuf_tensor("jkraw", [128, 2], F32))

        pb = [Buf(ES.enter_context(nc.psum_tensor(f"pb{i}", [128, 512], F32)), f"pb{i}")
              for i in range(7)]
        pT = Buf(ES.enter_context(nc.psum_tensor("pT", [128, 1024], BF16)), "pT")

        c_eye = sb("c_eye", 128, 128, BF16)
        c_eyef = sb("c_eyef", 128, 128, F32)
        c_cum = sb("c_cum", 128, 128, BF16)
        c_mlow = sb("c_mlow", 128, C, F32)
        c_mupp = sb("c_mupp", 128, C, F32)
        c_mud = sb("c_mud", 128, C, F32)
        c_ipair = sb("c_ipair", 128, C, BF16)
        c_ones = sb("c_ones", 128, 1, BF16)
        c_onesr = sb("c_onesr", 1, 128, BF16)
        c_onesrf = sb("c_onesrf", 1, 128, F32)
        w_f1 = sb("w_f1", 128, 8 * HD, BF16)
        w_og1 = sb("w_og1", 128, 8 * HD, BF16)
        w_f2 = sb("w_f2", 128, D, BF16)
        w_og2 = sb("w_og2", 128, D, BF16)
        w_gam = sb("w_gam", 128, 8 * H, BF16)
        c_cb = sb("c_cb", 128, 32, F32)
        c_cw = sb("c_cw", 128, 96, F32)
        n_w = sb("n_w", 128, 8, F32)
        swsc = sb("swsc", 128, 4, F32)
        s_state = sb("s_state", 128, HPG * 2 * HD, BF16)
        x2f = sb("x2f", 128, D, F32)
        mu1 = sb("mu1", 128, 8, F32)
        cflt = sb("cflt", 128, 2, F32)
        WS = contextlib.ExitStack()

        def sbW(name, p, f, dt):
            return Buf(WS.enter_context(nc.sbuf_tensor(name, [p, f], dt)), name)

        w_q = sbW("w_q", 128, 8 * D, BF16)
        w_k = sbW("w_k", 128, 8 * D, BF16)
        w_v = sbW("w_v", 128, 8 * D, BF16)
        w_ow = sbW("w_ow", 128, 2 * 8 * 128, BF16)

        def dmas(fn, n):
            fn.n_dma = n
            return fn

        P.add("vector", lambda e: e.memset(cflt.t[:, 0:1], 1e-5), writes=[cflt])
        P.add("vector", lambda e: e.memset(cflt.t[:, 1:2], -1.0), writes=[cflt])
        P.add("vector", lambda e: e.memset(w_f2.t[64:128, :], 0.0), writes=[w_f2])
        P.add("vector", lambda e: e.memset(w_og2.t[64:128, :], 0.0), writes=[w_og2])

        def ld_const(e):
            return [e.dma_start(out=c_eye.t[:], in_=ws_in["eye128"]),
                    e.dma_start(out=c_eyef.t[:], in_=ws_in["eye128f"]),
                    e.dma_start(out=c_cum.t[:], in_=ws_in["cumtri"]),
                    e.dma_start(out=c_mlow.t[:], in_=ws_in["mlow"]),
                    e.dma_start(out=c_mupp.t[:], in_=ws_in["mupp"]),
                    e.dma_start(out=c_mud.t[:], in_=ws_in["mud"]),
                    e.dma_start(out=c_ipair.t[:], in_=ws_in["ipair"]),
                    e.dma_start(out=c_ones.t[:], in_=ws_in["ones1"]),
                    e.dma_start(out=c_onesr.t[:], in_=ws_in["onesr"]),
                    e.dma_start(out=c_onesrf.t[:], in_=ws_in["onesrf"]),
                    e.dma_start(out=c_cb.t[:].rearrange("p (a b) -> p a b", b=1),
                                in_=ws_in["conv_b"].rearrange("o (a p b) -> p a (o b)", p=128, b=1),
                                allow_slow_non_contiguous=True),
                    e.dma_start(out=c_cw.t[:].rearrange("p (a b) -> p a b", b=3),
                                in_=ws_in["conv_w"].rearrange("(a p) b -> p a b", p=128),
                                allow_slow_non_contiguous=True),
                    e.dma_start(out=n_w.t[:].rearrange("p (a b) -> p a b", b=1),
                                in_=ws_in["norm_w"].rearrange("o (a p b) -> p a (o b)", p=128, b=1),
                                allow_slow_non_contiguous=True)]
        P.add("sync", dmas(ld_const, 13), dma="ld", n_dma=13,
              writes=[c_eye, c_eyef, c_cum, c_mlow, c_mupp, c_mud, c_ipair, c_ones, c_onesr, c_onesrf, c_cb, c_cw, n_w])

        with contextlib.ExitStack() as PS:
            def psb(name, p, f, dt):
                return Buf(PS.enter_context(nc.sbuf_tensor(name, [p, f], dt)), name)

            stg = psb("stg", 128, 4096, F32)
            tq = psb("tq", 128, 4096, BF16)
            tqt = psb("tqt", 128, 1024, BF16)
            acc1 = psb("acc1", 128, 2, F32)
            accb = psb("accb", 128, 1, F32)
            tr1 = psb("tr1", 1, 128, F32)
            iv0 = psb("iv0", 128, 4096, I32)

            def cast_small(nm, dst, cols, kt):
                def ld(e):
                    return [e.dma_start(out=stg.t[:, k * cols:(k + 1) * cols],
                                        in_=ws_in[nm][k * 128:(k + 1) * 128, 0:cols])
                            for k in range(kt)]
                P.add("sync", dmas(ld, kt), dma="ld", n_dma=kt, writes=[stg])
                P.add("vector", lambda e, cols=cols, kt=kt, dst=dst: e.tensor_copy(
                    out=dst.t[:, 0:kt * cols], in_=stg.t[:, 0:kt * cols]),
                    reads=[stg], writes=[dst])

            cast_small("Wf1", w_f1, HD, 8)
            cast_small("Wog1", w_og1, HD, 8)
            cast_small("Wgamma", w_gam, H, 8)

            def ld_f2(e):
                return [e.dma_start(out=stg.t[0:64, 0:D], in_=ws_in["Wf2"]),
                        e.dma_start(out=stg.t[0:64, D:2 * D], in_=ws_in["Wog2"])]
            P.add("sync", dmas(ld_f2, 2), dma="ld", n_dma=2, writes=[stg])
            P.add("vector", lambda e: e.tensor_copy(out=w_f2.t[0:64, :], in_=stg.t[0:64, 0:D]),
                  reads=[stg], writes=[w_f2])
            P.add("vector", lambda e: e.tensor_copy(out=w_og2.t[0:64, :], in_=stg.t[0:64, D:2 * D]),
                  reads=[stg], writes=[w_og2])

            for (nm, dst) in (("Wq", w_q), ("Wk", w_k), ("Wv", w_v)):
                for k in range(8):
                    half = (k % 2) * 2048
                    def ld(e, nm=nm, k=k, half=half):
                        return [e.dma_start(out=stg.t[:, half:half + D],
                                            in_=ws_in[nm][k * 128:(k + 1) * 128, :])]
                    P.add("sync", dmas(ld, 1), dma="ld", writes=[stg])
                    P.add("vector", lambda e, dst=dst, k=k, half=half: e.tensor_copy(
                        out=dst.t[:, k * D:(k + 1) * D], in_=stg.t[:, half:half + D]),
                        reads=[stg], writes=[dst])

            woq_sc = nc.dram_tensor("woqsc", [D, D], BF16, kind="Internal").ap()
            for k in range(8):
                half = (k % 2) * 2048
                def ld(e, k=k, half=half):
                    return [e.dma_start(out=stg.t[:, half:half + D],
                                        in_=ws_in["Wo"][k * 128:(k + 1) * 128, :])]
                P.add("sync", dmas(ld, 1), dma="ld", writes=[stg])
                P.add("vector", lambda e, k=k, half=half: e.tensor_scalar(
                    out=tq.t[:, 0:D], in0=stg.t[:, half:half + D],
                    scalar1=n_w.t[:, k:k + 1], scalar2=None, op0=ALU.mult),
                    reads=[stg, n_w], writes=[tq])
                def stw(e, k=k):
                    return [e.dma_start(out=woq_sc[k * 128:(k + 1) * 128, :], in_=tq.t[:, 0:D])]
                P.add("gpsimd", dmas(stw, 1), dma="st", reads=[tq], writes=[woqB])

            def wmean(nm, rows, cols, out_col):
                nt = rows // 128
                P.add("vector", lambda e: e.memset(acc1.t[:, 0:1], 0.0), writes=[acc1])
                for i in range(nt):
                    def ld(e, i=i):
                        return [e.dma_start(out=stg.t[:, 0:cols],
                                            in_=ws_in[nm][i * 128:(i + 1) * 128, :])]
                    P.add("sync", dmas(ld, 1), dma="ld", writes=[stg])
                    P.add("vector", lambda e: e.reduce_sum(out=acc1.t[:, 1:2], in_=stg.t[:, 0:cols],
                                                           axis=AX.X, apply_absolute_value=True),
                          reads=[stg], writes=[acc1])
                    P.add("vector", lambda e: e.tensor_add(out=acc1.t[:, 0:1], in0=acc1.t[:, 0:1],
                                                           in1=acc1.t[:, 1:2]),
                          reads=[acc1], writes=[acc1])
                P.add("vector", lambda e: e.tensor_copy(out=accb.t[:], in_=acc1.t[:, 0:1]),
                      reads=[acc1], writes=[accb])
                P.add("tensor", lambda e: e.transpose(pb[0].t[0:1, 0:128], accb.t[:], c_eyef.t[:]),
                      reads=[accb, c_eyef], writes=[pb[0]])
                P.add("vector", lambda e: e.tensor_copy(out=tr1.t[:], in_=pb[0].t[0:1, 0:128]),
                      reads=[pb[0]], writes=[tr1])
                P.add("vector", lambda e: e.reduce_sum(out=tr1.t[0:1, 0:1], in_=tr1.t[:], axis=AX.X),
                      reads=[tr1], writes=[tr1])
                P.add("vector", lambda e, rows=rows, cols=cols: e.tensor_scalar(
                    out=tr1.t[0:1, 1:2], in0=tr1.t[0:1, 0:1],
                    scalar1=1.0 / (rows * cols), scalar2=1e-5, op0=ALU.mult, op1=ALU.max),
                    reads=[tr1], writes=[tr1])
                P.add("vector", lambda e: e.reciprocal(out=tr1.t[0:1, 2:3], in_=tr1.t[0:1, 1:2]),
                      reads=[tr1], writes=[tr1])

                P.add("tensor", lambda e, out_col=out_col: e.matmul(
                    pT.t[:, 0:1] if False else pb[6].t[:, out_col:out_col + 1],
                    c_onesrf.t[:], tr1.t[0:1, 2:3],
                    start=True, stop=True), reads=[tr1, c_onesrf], writes=[pb[6]])
                P.add("vector", lambda e, out_col=out_col: e.tensor_copy(
                    out=swsc.t[:, out_col:out_col + 1],
                    in_=pb[6].t[:, out_col:out_col + 1]),
                    reads=[pb[6]], writes=[swsc])

            wmean("w1", DH, D, 0)
            wmean("w2", D, DH, 1)
            P.add("vector", lambda e: e.reciprocal(out=swsc.t[:, 2:3], in_=swsc.t[:, 0:1]),
                  reads=[swsc], writes=[swsc])
            P.add("vector", lambda e: e.reciprocal(out=swsc.t[:, 3:4], in_=swsc.t[:, 1:2]),
                  reads=[swsc], writes=[swsc])

            def ternarize(nm, dst_sc, dstB, rows, cols, scol):
                nt, nf = rows // 128, cols // 128
                for i in range(nt):
                    def ld(e, i=i):
                        return [e.dma_start(out=stg.t[:, 0:cols],
                                            in_=ws_in[nm][i * 128:(i + 1) * 128, :])]
                    P.add("sync", dmas(ld, 1), dma="ld", writes=[stg])
                    P.add("vector", lambda e, cols=cols, scol=scol: e.tensor_scalar(
                        out=stg.t[:, 0:cols], in0=stg.t[:, 0:cols],
                        scalar1=swsc.t[:, scol:scol + 1], scalar2=None, op0=ALU.mult),
                        reads=[stg, swsc], writes=[stg])
                    P.add("vector", lambda e, cols=cols: e.tensor_copy(out=iv0.t[:, 0:cols],
                                                                      in_=stg.t[:, 0:cols]),
                          reads=[stg], writes=[iv0])
                    P.add("vector", lambda e, cols=cols: e.tensor_scalar(
                        out=tq.t[:, 0:cols], in0=iv0.t[:, 0:cols],
                        scalar1=1.0, scalar2=-1.0, op0=ALU.min, op1=ALU.max),
                        reads=[iv0], writes=[tq])
                    for fj in range(nf):
                        sl = fj % 8
                        P.add("tensor", lambda e, fj=fj, sl=sl: e.transpose(
                            pT.t[:, sl * 128:(sl + 1) * 128],
                            tq.t[:, fj * 128:(fj + 1) * 128], c_eye.t[:]),
                            reads=[tq, c_eye], writes=[pT])
                        P.add("scalar", lambda e, fj=fj, sl=sl: e.copy(
                            out=tqt.t[:, sl * 128:(sl + 1) * 128],
                            in_=pT.t[:, sl * 128:(sl + 1) * 128]),
                            reads=[pT], writes=[tqt])
                        if sl == 7 or fj == nf - 1:
                            n = sl + 1
                            def stf(e, i=i, fj=fj, n=n):
                                return [e.dma_start(
                                    out=dst_sc[(fj - n + 1 + j) * 128:(fj - n + 2 + j) * 128,
                                               i * 128:(i + 1) * 128],
                                    in_=tqt.t[:, j * 128:(j + 1) * 128]) for j in range(n)]
                            P.add("gpsimd", dmas(stf, n), dma="st", n_dma=n, reads=[tqt],
                                  writes=[dstB])

            ternarize("w1", w1q_sc, w1qB, DH, D, 0)
            ternarize("w2", w2q_sc, w2qB, D, DH, 1)
        P.wait_all("vector")
        P.wait_all("scalar")
        P.wait_all("tensor")
        P.wait_all("sync")
        P.wait_all("gpsimd")

        AS = contextlib.ExitStack()

        def sbA(name, p, f, dt):
            return Buf(AS.enter_context(nc.sbuf_tensor(name, [p, f], dt)), name)

        xa = [sbA(f"xa{i}", 128, NTA * D, F32) for i in range(1)]
        lnT = sbA("lnT", 128, NTA * D, BF16)
        lnF = sbA("lnF", 128, 8 * WA, BF16)
        tfq = sbA("tfq", 128, D, F32)
        tfk = sbA("tfk", 128, D, F32)
        tfs = sbA("tfs", 128, D, F32)
        tfc = sbA("tfc", 128, D, F32)
        tfB = sbA("tfB", 128, D, F32)
        tfBi = sbA("tfBi", 128, D, F32)
        tfe = sbA("tfe", 128, D, F32)
        tfkn = sbA("tfkn", 128, D, F32)
        tf1 = sbA("tf1", 128, D, F32)
        qtm = sbA("qtm", 128, D, BF16)
        bhb = sbA("bhb", 128, D, BF16)
        rsq = sbA("rsq", 128, H, F32)
        gamT = sbA("gamT", 128, NCH * H, F32)
        zf1 = sbA("zf1", 128, WA, BF16)
        zg1 = sbA("zg1", 128, WA, BF16)
        tmVC = sbA("tmVC", 128, NCH * H * 128, BF16)
        tmW = sbA("tmW", 128, NCH * H * HD, BF16)
        hdq = sbA("hdq", 128, HPG * NCH * 128, BF16)
        hdw = sbA("hdw", 128, HPG * NCH * 128, BF16)
        hdc = sbA("hdc", 128, HPG * NCH * 128, BF16)
        hdb = sbA("hdb", 128, HPG * NCH * 128, BF16)
        hdq2 = sbA("hdq2", 128, HPG * NCH * 128, BF16)
        hdw2 = sbA("hdw2", 128, HPG * NCH * 128, BF16)
        hdc2 = sbA("hdc2", 128, HPG * NCH * 128, BF16)
        bCt = sbA("bCt", 128, H * NCH, BF16)
        sgog = sbA("sgog", 128, 8 * WA, BF16)
        sL = sbA("sL", 128, 2 * NCH * C, BF16)
        sQl = [sbA(f"sQl{i}", 128, 2 * NCH * C, BF16) for i in range(6)]
        sP = [sbA(f"sP{i}", 128, 2 * NCH * C, BF16) for i in range(2)]
        sAq = sbA("sAq", 128, 2 * NCH * C, BF16)
        sX = [sbA(f"sX{i}", 128, 2 * NCH * 128, BF16) for i in range(2)]
        sXp = sbA("sXp", 128, 2 * NCH * HD, BF16)
        sQE = sbA("sQE", 128, 2 * NCH * HD, BF16)
        oStg = sbA("oStg", 128, NCH * 2 * HD, BF16)
        oFM = sbA("oFM", 128, HPG * NCH * 128, BF16)
        oTM = lnT
        bcA = sbA("bcA", 128, WA, F32)
        bcB = sbA("bcB", 128, WA, F32)
        og2p = sbA("og2p", 128, WA, BF16)
        mrow = sbA("mrow", 1, 4 * WA, F32)
        onrm = sbA("onrm", 128, 8 * WA, BF16)
        aoF = sbA("aoF", 128, 8 * WA, BF16)
        dTM = sbA("dTM", 128, NTA * D, BF16)
        x1T = sbA("x1T", 128, NTA * D, F32)

        def col_hd(hp, ch, b):
            return (hp * NCH + ch) * 128 + b * 64

        def ln_stats(xbuf, ti, center_dst=None):
            xt = lambda: xbuf.t[:, ti * D:(ti + 1) * D]
            P.add("vector", lambda e: e.reduce_sum(out=mu1.t[:, 0:1], in_=xt(), axis=AX.X),
                  reads=[xbuf], writes=[mu1])
            P.add("scalar", lambda e: e.activation(out=x2f.t[:], in_=xt(), func=AF.Square),
                  reads=[xbuf], writes=[x2f])
            P.add("vector", lambda e: e.reduce_sum(out=mu1.t[:, 1:2], in_=x2f.t[:], axis=AX.X),
                  reads=[x2f], writes=[mu1])
            P.add("vector", lambda e: e.tensor_scalar(out=mu1.t[:, 2:3], in0=mu1.t[:, 0:1],
                                                      scalar1=1.0 / D, scalar2=None, op0=ALU.mult),
                  reads=[mu1], writes=[mu1])
            P.add("vector", lambda e: e.tensor_tensor(out=mu1.t[:, 3:4], in0=mu1.t[:, 2:3],
                                                      in1=mu1.t[:, 2:3], op=ALU.mult),
                  reads=[mu1], writes=[mu1])
            P.add("vector", lambda e: e.tensor_scalar(out=mu1.t[:, 4:5], in0=mu1.t[:, 1:2],
                                                      scalar1=1.0 / D, scalar2=None, op0=ALU.mult),
                  reads=[mu1], writes=[mu1])
            P.add("vector", lambda e: e.tensor_tensor(out=mu1.t[:, 4:5], in0=mu1.t[:, 4:5],
                                                      in1=mu1.t[:, 3:4], op=ALU.subtract),
                  reads=[mu1], writes=[mu1])
            P.add("scalar", lambda e: e.activation(out=mu1.t[:, 5:6], in_=mu1.t[:, 4:5],
                                                   func=AF.Sqrt, bias=cflt.t[:, 0:1]),
                  reads=[mu1], writes=[mu1])
            P.add("vector", lambda e: e.reciprocal(out=mu1.t[:, 6:7], in_=mu1.t[:, 5:6]),
                  reads=[mu1], writes=[mu1])
            P.add("vector", lambda e: e.tensor_tensor(out=mu1.t[:, 7:8], in0=mu1.t[:, 2:3],
                                                      in1=mu1.t[:, 6:7], op=ALU.mult),
                  reads=[mu1], writes=[mu1])
            P.add("vector", lambda e: e.tensor_scalar(out=mu1.t[:, 7:8], in0=mu1.t[:, 7:8],
                                                      scalar1=-1.0, scalar2=None, op0=ALU.mult),
                  reads=[mu1], writes=[mu1])

        def tm_to_fm(srcbuf, src_off, dstbuf, W, ti):
            for k in range(8):
                sl = k % 8
                P.add("tensor", lambda e, k=k, sl=sl: e.transpose(
                    pT.t[:, sl * 128:(sl + 1) * 128],
                    srcbuf.t[:, src_off + k * 128:src_off + (k + 1) * 128], c_eye.t[:]),
                    reads=[srcbuf, c_eye], writes=[pT])
            def ev(e):
                return e.tensor_copy(
                    out=dstbuf.t[:].rearrange("p (k c) -> p k c", k=8)[:, :, ti * 128:(ti + 1) * 128],
                    in_=pT.t[:].rearrange("p (a c) -> p a c", a=8))
            P.add("vector", ev, reads=[pT], writes=[dstbuf])

        def fm_ln(srcbuf, dstbuf, W, eps=1e-5):
            def mmsum(e):
                last = None
                for k in range(8):
                    last = e.matmul(pb[5].t[0:1, 0:W], c_ones.t[:],
                                    srcbuf.t[:, k * W:(k + 1) * W],
                                    start=(k == 0), stop=(k == 7))
                return last
            P.add("tensor", mmsum, reads=[srcbuf, c_ones], writes=[pb[5]])
            for k in range(8):
                P.add("scalar", lambda e, k=k: e.activation(out=og2p.t[:, 0:W],
                                                            in_=srcbuf.t[:, k * W:(k + 1) * W],
                                                            func=AF.Square),
                      reads=[srcbuf], writes=[og2p])
                def mm2(e, k=k):
                    return e.matmul(pb[5].t[64:65, 0:W], c_ones.t[:], og2p.t[:, 0:W],
                                    start=(k == 0), stop=(k == 7))
                P.add("tensor", mm2, reads=[og2p, c_ones], writes=[pb[5]])
            P.add("vector", lambda e: e.tensor_scalar(out=mrow.t[0:1, 0:W], in0=pb[5].t[0:1, 0:W],
                                                      scalar1=1.0 / D, scalar2=None, op0=ALU.mult),
                  reads=[pb[5]], writes=[mrow])
            P.add("vector", lambda e: e.tensor_scalar(out=mrow.t[0:1, W:2 * W], in0=pb[5].t[64:65, 0:W],
                                                      scalar1=1.0 / D, scalar2=None, op0=ALU.mult),
                  reads=[pb[5]], writes=[mrow])
            P.add("vector", lambda e: e.tensor_tensor(out=mrow.t[0:1, 2 * W:3 * W],
                                                      in0=mrow.t[0:1, 0:W], in1=mrow.t[0:1, 0:W],
                                                      op=ALU.mult),
                  reads=[mrow], writes=[mrow])
            P.add("vector", lambda e: e.tensor_tensor(out=mrow.t[0:1, W:2 * W],
                                                      in0=mrow.t[0:1, W:2 * W],
                                                      in1=mrow.t[0:1, 2 * W:3 * W], op=ALU.subtract),
                  reads=[mrow], writes=[mrow])
            P.add("scalar", lambda e: e.activation(out=mrow.t[0:1, W:2 * W],
                                                   in_=mrow.t[0:1, W:2 * W], func=AF.Sqrt, bias=cflt.t[0:1, 0:1]),
                  reads=[mrow], writes=[mrow])
            P.add("vector", lambda e: e.reciprocal(out=mrow.t[0:1, W:2 * W],
                                                   in_=mrow.t[0:1, W:2 * W]),
                  reads=[mrow], writes=[mrow])
            def mmbc(e):
                e.matmul(pb[6].t[:, 0:W], c_onesrf.t[:], mrow.t[0:1, 0:W],
                         start=True, stop=True)
                return e.matmul(pb[6].t[:, W:2 * W], c_onesrf.t[:], mrow.t[0:1, W:2 * W],
                                start=True, stop=True)
            P.add("tensor", mmbc, reads=[mrow, c_onesrf], writes=[pb[6]])
            P.add("vector", lambda e: e.tensor_tensor(
                out=dstbuf.t[:].rearrange("p (k c) -> p k c", k=8),
                in0=srcbuf.t[:].rearrange("p (k c) -> p k c", k=8),
                in1=pb[6].t[:, 0:W][:, None, :].broadcast_to((128, 8, W)), op=ALU.subtract),
                reads=[srcbuf, pb[6]], writes=[dstbuf])
            P.add("vector", lambda e: e.tensor_tensor(
                out=dstbuf.t[:].rearrange("p (k c) -> p k c", k=8),
                in0=dstbuf.t[:].rearrange("p (k c) -> p k c", k=8),
                in1=pb[6].t[:, W:2 * W][:, None, :].broadcast_to((128, 8, W)), op=ALU.mult),
                reads=[dstbuf, pb[6]], writes=[dstbuf])

        def emit_recurrence():
            Wp = 2 * NCH * C
            HC = HPG * NCH * 128

            def mkswap(src, dst):
                def sw(e):
                    return [e.dma_start(out=dst.t[0:64, 0:HC], in_=src.t[64:128, 0:HC]),
                            e.dma_start(out=dst.t[64:128, 0:HC], in_=src.t[0:64, 0:HC])]
                P.add("gpsimd", dmas(sw, 2), dma="st", n_dma=2, reads=[src], writes=[dst])
            mkswap(hdq, hdq2)
            mkswap(hdw, hdw2)
            mkswap(hdc, hdc2)
            def bext(e):
                ds = []
                for s2 in range(2):
                    for b in range(2):
                        src = hdb.t[s2 * 64:(s2 + 1) * 64, :].rearrange(
                            "p (hp ch bb t) -> p hp ch bb t", hp=HPG, ch=NCH, bb=2)[
                            :, :, :, b, 63]
                        dst = bCt.t[b * 64:(b + 1) * 64,
                                    s2 * (HPG * NCH):(s2 + 1) * (HPG * NCH)].rearrange(
                            "p (hp ch) -> p hp ch", ch=NCH)
                        ds.append(e.dma_start(out=dst, in_=src,
                                              allow_slow_non_contiguous=True))
                return ds
            P.add("gpsimd", dmas(bext, 4), dma="st", n_dma=4, reads=[hdb], writes=[bCt])

            def hsel(base, swapped, b, s2):
                return base if (s2 == b) else swapped

            for hp in range(HPG):
                for ch in range(NCH):
                    for s2 in range(2):
                        uc = (ch * 2 + s2) * C
                        for (bank, lh, lh2, rh, rh2) in (
                                (pb[0], hdc, hdc2, hdw, hdw2),
                                (pb[1], hdw, hdw2, hdc, hdc2),
                                (pb[2], hdw, hdw2, hdq, hdq2)):
                            def mmx(e, bank=bank, lh=lh, lh2=lh2, rh=rh, rh2=rh2,
                                    s2=s2, ch=ch, uc=uc, hp=hp):
                                last = None
                                for b in range(2):
                                    r0 = b * 64
                                    cb = col_hd(hp, ch, b)
                                    lt = hsel(lh, lh2, b, s2)
                                    rt = hsel(rh, rh2, b, s2)
                                    last = e.matmul(bank.t[r0:r0 + 64, uc:uc + C],
                                                    lt.t[r0:r0 + 64, cb:cb + 64],
                                                    rt.t[r0:r0 + 64, cb:cb + 64],
                                                    start=True, stop=True,
                                                    tile_position=(r0, r0))
                                return last
                            P.add("tensor", mmx, reads=[lh, lh2, rh, rh2], writes=[bank])
                for (dst, bank, msk) in ((sL, pb[0], c_mlow), (sQl[0], pb[1], c_mupp),
                                         (sAq, pb[2], c_mud)):
                    P.add("vector", lambda e, dst=dst, bank=bank, msk=msk: e.tensor_tensor(
                        out=dst.t[:].rearrange("p (u c) -> p u c", c=C),
                        in0=bank.t[:, 0:Wp].rearrange("p (u c) -> p u c", c=C),
                        in1=msk.t[:][:, None, :].broadcast_to((128, Wp // C, C)), op=ALU.mult),
                        reads=[bank, msk], writes=[dst])
                curP = sL
                for lvl in range(1, 6):
                    curQ, dstQ, dstP = sQl[lvl - 1], sQl[lvl], sP[lvl % 2]
                    for u in range(2 * NCH):
                        uc = u * C
                        if lvl < 5:
                            def mmp(e, uc=uc, curP=curP, curQ=curQ):
                                e.matmul(pb[0].t[0:64, uc:uc + C], curQ.t[0:64, uc:uc + C],
                                         curP.t[0:64, uc:uc + C], start=True, stop=True,
                                         tile_position=(0, 0))
                                return e.matmul(pb[0].t[64:128, uc:uc + C],
                                                curQ.t[64:128, uc:uc + C],
                                                curP.t[64:128, uc:uc + C], start=True, stop=True,
                                                tile_position=(64, 64))
                            P.add("tensor", mmp, reads=[curP, curQ], writes=[pb[0]])
                        def mmq(e, uc=uc, curP=curP, curQ=curQ):
                            e.matmul(pb[1].t[0:64, uc:uc + C], curP.t[0:64, uc:uc + C],
                                     curQ.t[0:64, uc:uc + C], start=True, stop=True,
                                     tile_position=(0, 0))
                            return e.matmul(pb[1].t[64:128, uc:uc + C], curP.t[64:128, uc:uc + C],
                                            curQ.t[64:128, uc:uc + C], start=True, stop=True,
                                            tile_position=(64, 64))
                        P.add("tensor", mmq, reads=[curP, curQ], writes=[pb[1]])
                    if lvl < 5:
                        P.add("scalar", lambda e, dstP=dstP: e.copy(out=dstP.t[:, 0:Wp],
                                                                    in_=pb[0].t[:, 0:Wp]),
                              reads=[pb[0]], writes=[dstP])
                    P.add("vector", lambda e, dstQ=dstQ: e.tensor_copy(out=dstQ.t[:, 0:Wp],
                                                                       in_=pb[1].t[:, 0:Wp]),
                          reads=[pb[1]], writes=[dstQ])
                    curP = dstP
                for gen, k in enumerate(range(5, -1, -1)):
                    qk = sQl[k]
                    dstX = sX[gen % 2]
                    prevX = tmVC if gen == 0 else sX[(gen - 1) % 2]
                    for ch in range(NCH):
                        for s2 in range(2):
                            h = 2 * hp + s2
                            uc = (ch * 2 + s2) * C
                            xc = (ch * 2 + s2) * 128
                            def rap(gen=gen, ch=ch, h=h, xc=xc):
                                if gen == 0:
                                    return tmVC.t[:, (ch * H + h) * 128:(ch * H + h + 1) * 128]
                                return sX[(gen - 1) % 2].t[:, xc:xc + 128]
                            def mma(e, qk=qk, uc=uc, xc=xc, rap=rap):
                                e.matmul(pb[3].t[0:64, xc:xc + 128], qk.t[0:64, uc:uc + C],
                                         rap()[0:64, :], start=True, stop=True,
                                         tile_position=(0, 0))
                                return e.matmul(pb[3].t[64:128, xc:xc + 128],
                                                qk.t[64:128, uc:uc + C], rap()[64:128, :],
                                                start=True, stop=True, tile_position=(64, 64))
                            P.add("tensor", mma, reads=[qk, prevX], writes=[pb[3]])
                    def evx(e, gen=gen, dstX=dstX, hp=hp):
                        if gen == 0:
                            src0 = tmVC.t[:].rearrange("p (ch h c) -> p ch h c", ch=NCH, h=H)[
                                :, :, 2 * hp:2 * hp + 2, :]
                        else:
                            src0 = sX[(gen - 1) % 2].t[:, 0:2 * NCH * 128].rearrange(
                                "p (ch h c) -> p ch h c", ch=NCH, h=2)
                        return e.tensor_tensor(
                            out=dstX.t[:, 0:2 * NCH * 128].rearrange(
                                "p (ch h c) -> p ch h c", ch=NCH, h=2),
                            in0=src0,
                            in1=pb[3].t[:, 0:2 * NCH * 128].rearrange(
                                "p (ch h c) -> p ch h c", ch=NCH, h=2),
                            op=ALU.add)
                    P.add("vector", evx, reads=[prevX, pb[3]], writes=[dstX])
                xfin = sX[1]
                for ch in range(NCH):
                    for s2 in range(2):
                        h = 2 * hp + s2
                        xc = (ch * 2 + s2) * 128
                        wc = (ch * H + h) * HD
                        oc = (ch * 2 + s2) * HD
                        def mmxp(e, xc=xc, wc=wc, oc=oc):
                            e.matmul(pb[0].t[0:64, oc:oc + HD],
                                     xfin.t[0:64, xc + 64:xc + 128], tmW.t[0:64, wc:wc + HD],
                                     start=True, stop=True, tile_position=(0, 0))
                            return e.matmul(pb[0].t[64:128, oc:oc + HD],
                                            xfin.t[64:128, xc + 64:xc + 128],
                                            tmW.t[64:128, wc:wc + HD],
                                            start=True, stop=True, tile_position=(64, 64))
                        P.add("tensor", mmxp, reads=[xfin, tmW], writes=[pb[0]])
                P.add("vector", lambda e: e.tensor_tensor(
                    out=sXp.t[:].rearrange("p (u c) -> p u c", c=C),
                    in0=pb[0].t[:, 0:2 * NCH * HD].rearrange("p (u c) -> p u c", c=C),
                    in1=c_ipair.t[:][:, None, :].broadcast_to((128, 2 * NCH, C)), op=ALU.add),
                    reads=[pb[0], c_ipair], writes=[sXp])
                for ch in range(NCH):
                    for s2 in range(2):
                        h = 2 * hp + s2
                        uc = (ch * 2 + s2) * C
                        xc = (ch * 2 + s2) * 128
                        oc = (ch * 2 + s2) * HD
                        def mmE(e, uc=uc, xc=xc, oc=oc):
                            last = None
                            for b in range(2):
                                r0 = b * 64
                                last = e.matmul(pb[4].t[r0:r0 + 64, oc:oc + HD],
                                                xfin.t[r0:r0 + 64, xc + 64:xc + 128],
                                                sAq.t[r0:r0 + 64, uc:uc + C],
                                                start=True, stop=True, tile_position=(r0, r0))
                            return last
                        P.add("tensor", mmE, reads=[xfin, sAq], writes=[pb[4]])
                        for b in range(2):
                            r0 = b * 64
                            cb = col_hd(hp, ch, b)
                            qsrc = hsel(hdq, hdq2, b, s2)
                            def evqe(e, r0=r0, cb=cb, oc=oc, qsrc=qsrc):
                                return e.tensor_tensor(
                                    out=sQE.t[r0:r0 + 64, oc:oc + HD],
                                    in0=qsrc.t[r0:r0 + 64, cb:cb + 64],
                                    in1=pb[4].t[r0:r0 + 64, oc:oc + HD], op=ALU.add)
                            P.add("vector", evqe, reads=[qsrc, pb[4]], writes=[sQE])
                        def mmF(e, uc=uc, xc=xc, oc=oc):
                            last = None
                            for b in range(2):
                                r0 = b * 64
                                last = e.matmul(pb[5].t[r0:r0 + 64, oc:oc + HD],
                                                xfin.t[r0:r0 + 64, xc:xc + 64],
                                                sAq.t[r0:r0 + 64, uc:uc + C],
                                                start=True, stop=False, tile_position=(r0, r0))
                            return last
                        P.add("tensor", mmF, reads=[xfin, sAq], writes=[pb[5]])
                        def mmO(e, oc=oc, hp=hp):
                            last = None
                            for b in range(2):
                                r0 = b * 64
                                last = e.matmul(pb[5].t[r0:r0 + 64, oc:oc + HD],
                                                s_state.t[r0:r0 + 64,
                                                          (2 * hp + (oc // HD) % 2) * HD:
                                                          (2 * hp + (oc // HD) % 2 + 1) * HD]
                                                if False else
                                                s_state.t[r0:r0 + 64, 0:HD],
                                                sQE.t[r0:r0 + 64, oc:oc + HD],
                                                start=False, stop=True, tile_position=(r0, r0))
                            return last
                        def mmO2(e, oc=oc, h=h):
                            last = None
                            for b in range(2):
                                r0 = b * 64
                                last = e.matmul(pb[5].t[r0:r0 + 64, oc:oc + HD],
                                                s_state.t[r0:r0 + 64, h * HD:(h + 1) * HD],
                                                sQE.t[r0:r0 + 64, oc:oc + HD],
                                                start=False, stop=True, tile_position=(r0, r0))
                            return last
                        P.add("tensor", mmO2, reads=[s_state, sQE], writes=[pb[5]])
                        def mmQb(e, ch=ch, h=h, xc=xc, oc=oc):
                            last = None
                            for b in range(2):
                                r0 = b * 64
                                last = e.matmul(pb[6].t[r0:r0 + 64, oc:oc + HD],
                                                tmW.t[r0:r0 + 64, (ch * H + h) * HD:
                                                      (ch * H + h + 1) * HD],
                                                xfin.t[r0:r0 + 64, xc:xc + 64],
                                                start=True, stop=False, tile_position=(r0, r0))
                            return last
                        P.add("tensor", mmQb, reads=[tmW, xfin], writes=[pb[6]])
                        def mmZ(e, oc=oc, h=h):
                            last = None
                            for b in range(2):
                                r0 = b * 64
                                last = e.matmul(pb[6].t[r0:r0 + 64, oc:oc + HD],
                                                sXp.t[r0:r0 + 64, oc:oc + HD],
                                                s_state.t[r0:r0 + 64, h * HD:(h + 1) * HD],
                                                start=False, stop=True, tile_position=(r0, r0))
                            return last
                        P.add("tensor", mmZ, reads=[sXp, s_state], writes=[pb[6]])
                    def evo(e, ch=ch):
                        return e.tensor_copy(out=oStg.t[:, ch * 128:(ch + 1) * 128],
                                             in_=pb[5].t[:, ch * 128:(ch + 1) * 128])
                    P.add("vector", evo, reads=[pb[5]], writes=[oStg])
                    def evs(e, ch=ch, hp=hp):
                        return e.tensor_tensor(
                            out=s_state.t[:, 2 * hp * HD:(2 * hp + 2) * HD].rearrange(
                                "p (s d) -> p s d", s=2),
                            in0=pb[6].t[:, ch * 128:(ch + 1) * 128].rearrange(
                                "p (s d) -> p s d", s=2),
                            in1=bCt.t[:].rearrange("p (s hp c) -> p s hp c", s=2, c=NCH)[
                                :, :, hp, ch][:, :, None].broadcast_to((128, 2, HD)),
                            op=ALU.mult)
                    P.add("vector", evs, reads=[pb[6], bCt], writes=[s_state])
                    for s2 in range(2):
                        h = 2 * hp + s2
                        for b in range(2):
                            r0 = b * 64
                            def mto(e, r0=r0, ch=ch, s2=s2, h=h, hp=hp):
                                return e.transpose(
                                    pT.t[r0:r0 + 64, (2 * hp + s2) * 64:(2 * hp + s2 + 1) * 64],
                                    oStg.t[r0:r0 + 64, (ch * 2 + s2) * HD:(ch * 2 + s2 + 1) * HD],
                                    c_eye.t[r0:r0 + 64, r0:r0 + 64])
                            P.add("tensor", mto, reads=[oStg, c_eye], writes=[pT])
                    def evoT(e, ch=ch, hp=hp):
                        return e.tensor_copy(
                            out=oTM.t[:, ch * D + 2 * hp * HD:ch * D + (2 * hp + 2) * HD],
                            in_=pT.t[:, 2 * hp * 128:(2 * hp + 1) * 128]
                            if False else pT.t[:, (2 * hp) * 64:(2 * hp + 2) * 64])
                    P.add("vector", evoT, reads=[pT], writes=[oTM])
            for m in range(NTA):
                tm_to_fm(oTM, m * D, oFM, WA, m)
        P.add("vector", lambda e: e.memset(s_state.t[:], 0.0), writes=[s_state])
        P.add("vector", lambda e: e.memset(zf1.t[64:128, :], 0.0), writes=[zf1])
        P.add("vector", lambda e: e.memset(zg1.t[64:128, :], 0.0), writes=[zg1])

        def a_supertile(st):
            xbuf = xa[0]
            rb = st * ST_A

            def ldx(e):
                ds = []
                for ti in range(NTA):
                    for b in range(NB):
                        r0 = b * T + rb + ti * 64
                        ds.append(e.dma_start(out=xbuf.t[b * 64:(b + 1) * 64, ti * D:(ti + 1) * D],
                                              in_=x_in[r0:r0 + 64, :]))
                return ds
            P.add("sync", dmas(ldx, NTA * NB), dma="ld", n_dma=NTA * NB, writes=[xbuf])

            for ti in range(NTA):
                ln_stats(xbuf, ti)
                P.add("vector", lambda e, ti=ti: e.tensor_scalar(
                    out=lnT.t[:, ti * D:(ti + 1) * D], in0=xbuf.t[:, ti * D:(ti + 1) * D],
                    scalar1=mu1.t[:, 6:7], scalar2=mu1.t[:, 7:8],
                    op0=ALU.mult, op1=ALU.add),
                    reads=[xbuf, mu1], writes=[lnT])
                tm_to_fm(lnT, ti * D, lnF, WA, ti)

            for (wb, dst) in ((w_f1, zf1), (w_og1, zg1)):
                def mmz(e, wb=wb):
                    last = None
                    for k in range(8):
                        last = e.matmul(pb[2].t[0:64, 0:WA], wb.t[:, k * HD:(k + 1) * HD],
                                        lnF.t[:, k * WA:(k + 1) * WA],
                                        start=(k == 0), stop=(k == 7))
                    return last
                P.add("tensor", mmz, reads=[lnF, wb], writes=[pb[2]])
                P.add("vector", lambda e, dst=dst: e.tensor_copy(out=dst.t[0:64, 0:WA],
                                                                 in_=pb[2].t[0:64, 0:WA]),
                      reads=[pb[2]], writes=[dst])

            for mo in range(8):
                bank = pb[mo % 2]
                def mmo(e, mo=mo, bank=bank):
                    return e.matmul(bank.t[:, 0:WA], w_og2.t[:, mo * 128:(mo + 1) * 128],
                                    zg1.t[:, 0:WA], start=True, stop=True)
                P.add("tensor", mmo, reads=[zg1, w_og2], writes=[bank])
                P.add("scalar", lambda e, mo=mo, bank=bank: e.activation(
                    out=sgog.t[:, mo * WA:(mo + 1) * WA], in_=bank.t[:, 0:WA], func=AF.Sigmoid),
                    reads=[bank], writes=[sgog])

            for m in range(NTA):
                ch = m
                for (wb, dst, act) in ((w_q, tfq, AF.Silu), (w_k, tfk, AF.Silu)):
                    for nn in range(2):
                        bank = pb[nn]
                        def mmp_(e, m=m, nn=nn, wb=wb, bank=bank):
                            last = None
                            for k in range(8):
                                last = e.matmul(bank.t[:, 0:512],
                                                lnF.t[:, k * WA + m * 128:k * WA + (m + 1) * 128],
                                                wb.t[:, k * D + nn * 512:k * D + (nn + 1) * 512],
                                                start=(k == 0), stop=(k == 7))
                            return last
                        P.add("tensor", mmp_, reads=[lnF, wb], writes=[bank])
                        P.add("scalar", lambda e, nn=nn, bank=bank, dst=dst, act=act: e.activation(
                            out=dst.t[:, nn * 512:(nn + 1) * 512], in_=bank.t[:, 0:512], func=act),
                            reads=[bank], writes=[dst])
                for nn in range(2):
                    bank = pb[nn]
                    def mmv(e, m=m, nn=nn, bank=bank):
                        last = None
                        for k in range(8):
                            last = e.matmul(bank.t[:, 0:512],
                                            lnF.t[:, k * WA + m * 128:k * WA + (m + 1) * 128],
                                            w_v.t[:, k * D + nn * 512:k * D + (nn + 1) * 512],
                                            start=(k == 0), stop=(k == 7))
                        return last
                    P.add("tensor", mmv, reads=[lnF, w_v], writes=[bank])
                    def evv(e, m=m, nn=nn, bank=bank):
                        return e.tensor_copy(
                            out=tmVC.t[:].rearrange("p (u c) -> p u c", c=128)[
                                :, m * H + nn * 8:m * H + (nn + 1) * 8, 0:64],
                            in_=bank.t[:, 0:512].rearrange("p (h c) -> p h c", c=64))
                    P.add("vector", evv, reads=[bank], writes=[tmVC])
                for nn in range(2):
                    bank = pb[nn]
                    def mmf2(e, m=m, nn=nn, bank=bank):
                        return e.matmul(bank.t[:, 0:512], zf1.t[:, m * 128:(m + 1) * 128],
                                        w_f2.t[:, nn * 512:(nn + 1) * 512],
                                        start=True, stop=True)
                    P.add("tensor", mmf2, reads=[zf1, w_f2], writes=[bank])
                    P.add("scalar", lambda e, nn=nn, bank=bank: e.activation(
                        out=tfe.t[:, nn * 512:(nn + 1) * 512], in_=bank.t[:, 0:512],
                        func=AF.Sigmoid),
                        reads=[bank], writes=[tfe])
                P.add("scalar", lambda e: e.activation(out=tfs.t[:], in_=tfe.t[:], func=AF.Ln),
                      reads=[tfe], writes=[tfs])
                bank = pb[2]
                def mmg(e, m=m, bank=bank):
                    last = None
                    for k in range(8):
                        last = e.matmul(bank.t[:, 0:H],
                                        lnF.t[:, k * WA + m * 128:k * WA + (m + 1) * 128],
                                        w_gam.t[:, k * H:(k + 1) * H],
                                        start=(k == 0), stop=(k == 7))
                    return last
                P.add("tensor", mmg, reads=[lnF, w_gam], writes=[bank])
                P.add("scalar", lambda e, m=m, bank=bank: e.activation(
                    out=gamT.t[:, m * H:(m + 1) * H], in_=bank.t[:, 0:H], func=AF.Sigmoid),
                    reads=[bank], writes=[gamT])
                P.add("vector", lambda e, m=m: e.tensor_scalar(
                    out=gamT.t[:, m * H:(m + 1) * H], in0=gamT.t[:, m * H:(m + 1) * H],
                    scalar1=-1.0, scalar2=None, op0=ALU.mult),
                    reads=[gamT], writes=[gamT])
                P.add("vector", lambda e: e.tensor_copy(out=qtm.t[:], in_=tfs.t[:]),
                      reads=[tfs], writes=[qtm])
                P.add("vector", lambda e: e.tensor_tensor(out=bhb.t[:], in0=tfs.t[:],
                                                          in1=qtm.t[:], op=ALU.subtract),
                      reads=[tfs, qtm], writes=[bhb])
                for nn in range(2):
                    bank = pb[nn]
                    def mmc(e, nn=nn, bank=bank):
                        e.matmul(bank.t[:, 0:512], c_cum.t[:], qtm.t[:, nn * 512:(nn + 1) * 512],
                                 start=True, stop=False)
                        return e.matmul(bank.t[:, 0:512], c_cum.t[:],
                                        bhb.t[:, nn * 512:(nn + 1) * 512], start=False, stop=True)
                    P.add("tensor", mmc, reads=[qtm, bhb, c_cum], writes=[bank])
                    P.add("vector", lambda e, nn=nn, bank=bank: e.tensor_copy(
                        out=tfc.t[:, nn * 512:(nn + 1) * 512], in_=bank.t[:, 0:512]),
                        reads=[bank], writes=[tfc])
                P.add("scalar", lambda e: e.activation(out=tfB.t[:], in_=tfc.t[:], func=AF.Exp),
                      reads=[tfc], writes=[tfB])
                P.add("scalar", lambda e: e.activation(out=tfBi.t[:], in_=tfc.t[:], func=AF.Exp,
                                                       scale=cflt.t[:, 1:2]),
                      reads=[tfc, cflt], writes=[tfBi])
                P.add("scalar", lambda e: e.activation(out=x2f.t[:], in_=tfk.t[:], func=AF.Square),
                      reads=[tfk], writes=[x2f])
                P.add("vector", lambda e: e.reduce_sum(
                    out=rsq.t[:], in_=x2f.t[:].rearrange("p (h d) -> p h d", h=H), axis=AX.X),
                    reads=[x2f], writes=[rsq])
                P.add("scalar", lambda e: e.activation(out=rsq.t[:], in_=rsq.t[:], func=AF.Sqrt),
                      reads=[rsq], writes=[rsq])
                P.add("vector", lambda e: e.tensor_scalar(out=rsq.t[:], in0=rsq.t[:],
                                                          scalar1=1e-12, scalar2=None, op0=ALU.max),
                      reads=[rsq], writes=[rsq])
                P.add("vector", lambda e: e.reciprocal(out=rsq.t[:], in_=rsq.t[:]),
                      reads=[rsq], writes=[rsq])
                P.add("vector", lambda e: e.tensor_tensor(
                    out=tfkn.t[:].rearrange("p (h d) -> p h d", h=H),
                    in0=tfk.t[:].rearrange("p (h d) -> p h d", h=H),
                    in1=rsq.t[:, :, None].broadcast_to((128, H, HD)), op=ALU.mult),
                    reads=[tfk, rsq], writes=[tfkn])
                P.add("vector", lambda e, ch=ch: e.tensor_tensor(
                    out=tmW.t[:].rearrange("p (u d) -> p u d", d=HD)[:, ch * H:(ch + 1) * H, :],
                    in0=tfkn.t[:].rearrange("p (h d) -> p h d", h=H),
                    in1=tfBi.t[:].rearrange("p (h d) -> p h d", h=H), op=ALU.mult),
                    reads=[tfkn, tfBi], writes=[tmW])
                P.add("vector", lambda e: e.tensor_tensor(out=tf1.t[:], in0=tfe.t[:], in1=tfB.t[:],
                                                          op=ALU.mult),
                      reads=[tfe, tfB], writes=[tf1])
                P.add("vector", lambda e: e.tensor_tensor(out=tf1.t[:], in0=tf1.t[:], in1=tfkn.t[:],
                                                          op=ALU.mult),
                      reads=[tf1, tfkn], writes=[tf1])
                P.add("vector", lambda e, ch=ch, m=m: e.tensor_tensor(
                    out=tmVC.t[:].rearrange("p (u c) -> p u c", c=128)[
                        :, ch * H:(ch + 1) * H, 64:128],
                    in0=tf1.t[:].rearrange("p (h d) -> p h d", h=H),
                    in1=gamT.t[:, m * H:(m + 1) * H][:, :, None].broadcast_to((128, H, HD)),
                    op=ALU.mult),
                    reads=[tf1, gamT], writes=[tmVC])
                P.add("vector", lambda e: e.tensor_tensor(out=qtm.t[:], in0=tfq.t[:], in1=tfB.t[:],
                                                          op=ALU.mult),
                      reads=[tfq, tfB], writes=[qtm])
                P.add("vector", lambda e: e.tensor_copy(out=bhb.t[:], in_=tfB.t[:]),
                      reads=[tfB], writes=[bhb])
                for (srcfn, srcbuf, dst) in (
                        (lambda hp: qtm.t[:, hp * 128:(hp + 1) * 128], qtm, hdq),
                        (lambda hp, ch=ch: tmW.t[:, (ch * H + 2 * hp) * HD:
                                                 (ch * H + 2 * hp + 2) * HD], tmW, hdw),
                        (lambda hp: bhb.t[:, hp * 128:(hp + 1) * 128], bhb, hdb)):
                    for hp in range(HPG):
                        sl = hp % 8
                        P.add("tensor", lambda e, srcfn=srcfn, hp=hp, sl=sl: e.transpose(
                            pT.t[:, sl * 128:(sl + 1) * 128], srcfn(hp), c_eye.t[:]),
                            reads=[srcbuf, c_eye], writes=[pT])
                    def evt(e, dst=dst, ch=ch):
                        return e.tensor_copy(
                            out=dst.t[:].rearrange("p (hp ch c) -> p hp ch c", hp=HPG, ch=NCH)[
                                :, :, ch, :],
                            in_=pT.t[:].rearrange("p (a c) -> p a c", a=8))
                    P.add("vector", evt, reads=[pT], writes=[dst])
                for hp in range(HPG):
                    for s2 in range(2):
                        hcm = 2 * hp + s2
                        P.add("tensor", lambda e, hp=hp, s2=s2, hcm=hcm, ch=ch: e.transpose(
                            pT.t[s2 * 64:s2 * 64 + 64, hp * 128:(hp + 1) * 128],
                            tmVC.t[:, (ch * H + hcm) * 128 + 64:(ch * H + hcm) * 128 + 128],
                            c_eye.t[:]),
                            reads=[tmVC, c_eye], writes=[pT])
                def evtc(e, ch=ch):
                    return e.tensor_copy(
                        out=hdc.t[:].rearrange("p (hp ch c) -> p hp ch c", hp=HPG, ch=NCH)[
                            :, :, ch, :],
                        in_=pT.t[:].rearrange("p (a c) -> p a c", a=8))
                P.add("vector", evtc, reads=[pT], writes=[hdc])

            emit_recurrence()

            P.add("vector", lambda e: e.tensor_tensor(out=oFM.t[:], in0=oFM.t[:], in1=sgog.t[:],
                                                      op=ALU.mult),
                  reads=[oFM, sgog], writes=[oFM])
            fm_ln(oFM, onrm, WA)
            for mo in range(8):
                wslot = (mo % 2) * 1024
                def ldo(e, mo=mo, wslot=wslot):
                    return [e.dma_start(out=w_ow.t[:, wslot + k * 128:wslot + (k + 1) * 128],
                                        in_=woq_sc[k * 128:(k + 1) * 128, mo * 128:(mo + 1) * 128])
                            for k in range(8)]
                P.add("sync", dmas(ldo, 8), dma="ld", n_dma=8, reads=[woqB], writes=[w_ow])
                bank = pb[mo % 2]
                def mmwo(e, mo=mo, wslot=wslot, bank=bank):
                    last = None
                    for k in range(8):
                        last = e.matmul(bank.t[:, 0:WA],
                                        w_ow.t[:, wslot + k * 128:wslot + (k + 1) * 128],
                                        onrm.t[:, k * WA:(k + 1) * WA],
                                        start=(k == 0), stop=(k == 7))
                    return last
                P.add("tensor", mmwo, reads=[onrm, w_ow], writes=[bank])
                P.add("scalar", lambda e, mo=mo, bank=bank: e.activation(
                    out=aoF.t[:, mo * WA:(mo + 1) * WA], in_=bank.t[:, 0:WA], func=AF.Copy),
                    reads=[bank], writes=[aoF])
            fm_ln(aoF, onrm, WA)
            for m in range(NTA):
                for k in range(8):
                    sl = k % 8
                    P.add("tensor", lambda e, m=m, k=k, sl=sl: e.transpose(
                        pT.t[:, sl * 128:(sl + 1) * 128],
                        onrm.t[:, k * WA + m * 128:k * WA + (m + 1) * 128], c_eye.t[:]),
                        reads=[onrm, c_eye], writes=[pT])
                def evd(e, m=m):
                    return e.tensor_copy(
                        out=dTM.t[:].rearrange("p (m k c) -> p m k c", m=NTA, k=8)[:, m, :, :],
                        in_=pT.t[:].rearrange("p (a c) -> p a c", a=8))
                P.add("vector", evd, reads=[pT], writes=[dTM])
            P.add("vector", lambda e: e.tensor_tensor(out=x1T.t[:], in0=xbuf.t[:], in1=dTM.t[:],
                                                      op=ALU.add),
                  reads=[xbuf, dTM], writes=[x1T])

            def stx1(e, rb=rb):
                ds = []
                for ti in range(NTA):
                    for b in range(NB):
                        r0 = b * T + rb + ti * 64
                        ds.append(e.dma_start(out=x1_sc[r0:r0 + 64, :],
                                              in_=x1T.t[b * 64:(b + 1) * 64, ti * D:(ti + 1) * D]))
                return ds
            P.add("gpsimd", dmas(stx1, NTA * NB), dma="st", n_dma=NTA * NB, reads=[x1T],
                  writes=[x1B])

        for st in range(NST_A):
            a_supertile(st)

        def tap(name, buf, rows, cols):
            if name in dbg_outs:
                def stp(e):
                    return [e.dma_start(out=dbg_outs[name], in_=buf.t[0:rows, 0:cols])]
                P.add("gpsimd", dmas(stp, 1), dma="st", reads=[buf])

        tap("lnT", lnT, 128, NTA * D)
        tap("lnF", lnF, 128, 8 * WA)
        tap("tfq", tfq, 128, D)
        tap("tfs", tfs, 128, D)
        tap("tfc", tfc, 128, D)
        tap("tmW", tmW, 128, NCH * H * HD)
        tap("tmVC", tmVC, 128, NCH * H * 128)
        tap("hdq", hdq, 128, HPG * NCH * 128)
        tap("hdw", hdw, 128, HPG * NCH * 128)
        tap("hdb", hdb, 128, HPG * NCH * 128)
        tap("oFM", oFM, 128, HPG * NCH * 128)
        tap("sgog", sgog, 128, 8 * WA)
        tap("aoF", aoF, 128, 8 * WA)
        tap("x1T", x1T, 128, NTA * D)
        tap("state", s_state, 128, HPG * 2 * HD)
        tap("sL", sL, 128, 2 * NCH * C)
        tap("sQ0", sQl[0], 128, 2 * NCH * C)
        tap("sQ5", sQl[5], 128, 2 * NCH * C)
        tap("sAq", sAq, 128, 2 * NCH * C)
        tap("xfin", sX[1], 128, 2 * NCH * 128)
        tap("sXp", sXp, 128, 2 * NCH * HD)
        tap("sQE", sQE, 128, 2 * NCH * HD)
        tap("oStg", oStg, 128, NCH * 2 * HD)
        tap("bCt", bCt, 128, H * NCH)
        tap("onrm", onrm, 128, 8 * WA)
        tap("hdc", hdc, 128, HPG * NCH * 128)

        def tapB(name, buf, rows, cols):
            if name in dbg_outs:
                def stp(e):
                    return [e.dma_start(out=dbg_outs[name], in_=buf.t[0:rows, 0:cols])]
                P.add("gpsimd", dmas(stp, 1), dma="st", reads=[buf])
            return name in dbg_outs

        for _eng in ("vector", "scalar", "tensor", "sync", "gpsimd"):
            P.wait_all(_eng)
        AS.close()
        WS.close()

        x1b = sb("x1b", 128, NTB * D, F32)
        x1s = sb("x1s", 128, NTB * D, F32)
        xqT = sb("xqT", 128, NTB * D, BF16)
        xcT = sb("xcT", 128, D, F32)
        drow = sb("drow", 128, NTB * 2, F32)
        xqF = sb("xqF", 128, 8 * WB, BF16)
        w1w = sb("w1w", 128, 2 * 8 * 128, BF16)
        w2w = sb("w2w", 128, 2 * 32 * 128, BF16)
        hF = sb("hF", 128, 2 * 2 * HBW, F32)
        hhalo = sb("hhalo", 128, 32 * 2 * 2, F32)
        ac0 = sb("ac0", 128, WB, F32)
        ac1 = sb("ac1", 128, WB, F32)
        hprevF = sb("hprevF", 128, 32 * 2 * 2, F32)
        yf = sb("yf", 128, WB, F32)
        yld = sb("yld", 128, 512, F32)
        ivh = sb("ivh", 128, WB, I32)
        hq = sb("hq", 128, 32 * WB, BF16)
        ymx = sb("ymx", 128, WB, F32)
        mxp = sb("mxp", 128, WB, F32)
        outFf = sb("outFf", 128, 8 * WB, F32)
        yrow = sb("yrow", 1, 4 * WB, F32)
        dscb = sb("dscb", 128, 3 * WB, F32)
        outF = sb("outF", 128, 8 * WB, BF16)
        o2pB = sb("o2pB", 128, WB, BF16)
        mrowB = yrow
        bcA2 = ymx
        bcB2 = xcT
        onB = xqF
        d2TM = xqT
        x2T = x1b

        P.add("vector", lambda e: e.memset(hhalo.t[:], 0.0), writes=[hhalo])

        def fm_ln_b(srcbuf, srcnorm, dstbuf, W):
            def mmsum(e):
                last = None
                for k in range(8):
                    last = e.matmul(pb[5].t[0:1, 0:W], c_ones.t[:],
                                    srcbuf.t[:, k * W:(k + 1) * W], start=(k == 0), stop=(k == 7))
                return last
            P.add("tensor", mmsum, reads=[srcbuf, c_ones], writes=[pb[5]])
            for k in range(8):
                P.add("scalar", lambda e, k=k: e.activation(out=o2pB.t[:, 0:W],
                                                            in_=srcbuf.t[:, k * W:(k + 1) * W],
                                                            func=AF.Square),
                      reads=[srcbuf], writes=[o2pB])
                def mm2(e, k=k):
                    return e.matmul(pb[5].t[64:65, 0:W], c_ones.t[:], o2pB.t[:, 0:W],
                                    start=(k == 0), stop=(k == 7))
                P.add("tensor", mm2, reads=[o2pB, c_ones], writes=[pb[5]])
            P.add("vector", lambda e: e.tensor_scalar(out=mrowB.t[0:1, 0:W], in0=pb[5].t[0:1, 0:W],
                                                      scalar1=1.0 / D, scalar2=None, op0=ALU.mult),
                  reads=[pb[5]], writes=[mrowB])
            P.add("vector", lambda e: e.tensor_scalar(out=mrowB.t[0:1, W:2 * W],
                                                      in0=pb[5].t[64:65, 0:W],
                                                      scalar1=1.0 / D, scalar2=None, op0=ALU.mult),
                  reads=[pb[5]], writes=[mrowB])
            P.add("vector", lambda e: e.tensor_tensor(out=mrowB.t[0:1, 2 * W:3 * W],
                                                      in0=mrowB.t[0:1, 0:W], in1=mrowB.t[0:1, 0:W],
                                                      op=ALU.mult),
                  reads=[mrowB], writes=[mrowB])
            P.add("vector", lambda e: e.tensor_tensor(out=mrowB.t[0:1, W:2 * W],
                                                      in0=mrowB.t[0:1, W:2 * W],
                                                      in1=mrowB.t[0:1, 2 * W:3 * W],
                                                      op=ALU.subtract),
                  reads=[mrowB], writes=[mrowB])
            P.add("scalar", lambda e: e.activation(out=mrowB.t[0:1, W:2 * W],
                                                   in_=mrowB.t[0:1, W:2 * W], func=AF.Sqrt,
                                                   bias=cflt.t[0:1, 0:1]),
                  reads=[mrowB], writes=[mrowB])
            P.add("vector", lambda e: e.reciprocal(out=mrowB.t[0:1, W:2 * W],
                                                   in_=mrowB.t[0:1, W:2 * W]),
                  reads=[mrowB], writes=[mrowB])
            def mmbc(e):
                e.matmul(pb[6].t[:, 0:W], c_onesrf.t[:], mrowB.t[0:1, 0:W],
                         start=True, stop=True)
                return e.matmul(pb[4].t[:, 0:W], c_onesrf.t[:], mrowB.t[0:1, W:2 * W],
                                start=True, stop=True)
            P.add("tensor", mmbc, reads=[mrowB, c_onesrf], writes=[pb[6], pb[4]])
            P.add("vector", lambda e: e.tensor_tensor(
                out=dstbuf.t[:].rearrange("p (k c) -> p k c", k=8),
                in0=srcnorm.t[:].rearrange("p (k c) -> p k c", k=8),
                in1=pb[6].t[:, 0:W][:, None, :].broadcast_to((128, 8, W)), op=ALU.subtract),
                reads=[srcnorm, pb[6]], writes=[dstbuf])
            P.add("vector", lambda e: e.tensor_tensor(
                out=dstbuf.t[:].rearrange("p (k c) -> p k c", k=8),
                in0=dstbuf.t[:].rearrange("p (k c) -> p k c", k=8),
                in1=pb[4].t[:, 0:W][:, None, :].broadcast_to((128, 8, W)), op=ALU.mult),
                reads=[dstbuf, pb[4]], writes=[dstbuf])

        colbf = sb("colbf", 128, 8, F32)

        def row_from_cols(colbuf, coloff, colstride, n, dst_off):
            if colstride > 1:
                src_ap = colbuf.t[:].rearrange("p (a b) -> p a b", b=colstride)[:, 0:n, coloff]
            else:
                src_ap = colbuf.t[:, coloff:coloff + n]
            P.add("vector", lambda e, src_ap=src_ap: e.tensor_copy(
                out=colbf.t[:, 0:n], in_=src_ap),
                reads=[colbuf], writes=[colbf])
            for ti in range(n):
                P.add("tensor", lambda e, ti=ti: e.transpose(
                    pb[3].t[0:1, ti * 128:(ti + 1) * 128],
                    colbf.t[:, ti:ti + 1], c_eyef.t[:]),
                    reads=[colbf, c_eyef], writes=[pb[3]])
            P.add("vector", lambda e: e.tensor_copy(out=yrow.t[0:1, dst_off:dst_off + n * 128],
                                                    in_=pb[3].t[0:1, 0:n * 128]),
                  reads=[pb[3]], writes=[yrow])

        def bcast_row(src_off, W, dst, dst_off, scale_src=None):
            P.add("tensor", lambda e: e.matmul(
                pb[4].t[:, 0:W], c_onesrf.t[:], yrow.t[0:1, src_off:src_off + W],
                start=True, stop=True), reads=[yrow, c_onesrf], writes=[pb[4]])
            P.add("vector", lambda e: e.tensor_copy(out=dst.t[:, dst_off:dst_off + W],
                                                    in_=pb[4].t[:, 0:W]),
                  reads=[pb[4]], writes=[dst])

        def b_supertile(st):
            rb = st * ST_B

            def ldx1(e):
                ds = []
                for ti in range(NTB):
                    for b in range(NB):
                        r0 = b * T + rb + ti * 64
                        ds.append(e.dma_start(out=x1b.t[b * 64:(b + 1) * 64, ti * D:(ti + 1) * D],
                                              in_=x1_sc[r0:r0 + 64, :]))
                return ds
            P.add("sync", dmas(ldx1, NTB * NB), dma="ld", n_dma=NTB * NB, reads=[x1B],
                  writes=[x1b])
            def ldx1s(e, st=st):
                ds = []
                for ti in range(NTB):
                    for b in range(NB):
                        p0 = b * 64
                        if st == 0 and ti == 0:
                            ds.append(e.dma_start(out=x1s.t[p0 + 1:p0 + 64, 0:D],
                                                  in_=x1_sc[b * T:b * T + 63, :]))
                        else:
                            r0 = b * T + rb - 1 + ti * 64
                            ds.append(e.dma_start(out=x1s.t[p0:p0 + 64, ti * D:(ti + 1) * D],
                                                  in_=x1_sc[r0:r0 + 64, :]))
                return ds
            P.add("sync", dmas(ldx1s, NTB * NB), dma="ld", n_dma=NTB * NB, reads=[x1B],
                  writes=[x1s])

            for ti in range(NTB):
                xt = lambda ti=ti: x1b.t[:, ti * D:(ti + 1) * D]
                P.add("vector", lambda e, xt=xt: e.reduce_sum(out=mu1.t[:, 0:1], in_=xt(), axis=AX.X),
                      reads=[x1b], writes=[mu1])
                P.add("vector", lambda e: e.tensor_scalar(out=mu1.t[:, 2:3], in0=mu1.t[:, 0:1],
                                                          scalar1=1.0 / D, scalar2=None,
                                                          op0=ALU.mult),
                      reads=[mu1], writes=[mu1])
                P.add("vector", lambda e: e.tensor_scalar(out=mu1.t[:, 3:4], in0=mu1.t[:, 2:3],
                                                          scalar1=-1.0, scalar2=None, op0=ALU.mult),
                      reads=[mu1], writes=[mu1])
                P.add("vector", lambda e, xt=xt: e.tensor_scalar(
                    out=xcT.t[:], in0=xt(), scalar1=mu1.t[:, 3:4], scalar2=None, op0=ALU.add),
                    reads=[x1b, mu1], writes=[xcT])
                P.add("scalar", lambda e: e.activation(out=x2f.t[:], in_=xcT.t[:], func=AF.Square),
                      reads=[xcT], writes=[x2f])
                P.add("vector", lambda e: e.reduce_sum(out=mu1.t[:, 4:5], in_=x2f.t[:], axis=AX.X),
                      reads=[x2f], writes=[mu1])
                P.add("vector", lambda e: e.tensor_scalar(out=mu1.t[:, 4:5], in0=mu1.t[:, 4:5],
                                                          scalar1=1.0 / D, scalar2=None,
                                                          op0=ALU.mult),
                      reads=[mu1], writes=[mu1])
                P.add("scalar", lambda e: e.activation(out=mu1.t[:, 5:6], in_=mu1.t[:, 4:5],
                                                       func=AF.Sqrt, bias=cflt.t[:, 0:1]),
                      reads=[mu1], writes=[mu1])
                P.add("vector", lambda e: e.reciprocal(out=mu1.t[:, 6:7], in_=mu1.t[:, 5:6]),
                      reads=[mu1], writes=[mu1])
                P.add("vector", lambda e: e.reduce_max(out=mu1.t[:, 0:1], in_=xcT.t[:], axis=AX.X,
                                                       apply_absolute_value=True),
                      reads=[xcT], writes=[mu1])
                P.add("vector", lambda e: e.tensor_tensor(out=mu1.t[:, 1:2], in0=mu1.t[:, 0:1],
                                                          in1=mu1.t[:, 6:7], op=ALU.mult),
                      reads=[mu1], writes=[mu1])
                P.add("vector", lambda e: e.tensor_scalar(out=mu1.t[:, 1:2], in0=mu1.t[:, 1:2],
                                                          scalar1=1e-5, scalar2=None, op0=ALU.max),
                      reads=[mu1], writes=[mu1])
                P.add("vector", lambda e: e.reciprocal(out=mu1.t[:, 3:4], in_=mu1.t[:, 1:2]),
                      reads=[mu1], writes=[mu1])
                P.add("vector", lambda e: e.tensor_tensor(out=mu1.t[:, 3:4], in0=mu1.t[:, 3:4],
                                                          in1=mu1.t[:, 6:7], op=ALU.mult),
                      reads=[mu1], writes=[mu1])
                P.add("vector", lambda e, ti=ti: e.tensor_scalar(
                    out=drow.t[:, ti * 2:ti * 2 + 1], in0=mu1.t[:, 3:4],
                    scalar1=127.0, scalar2=None, op0=ALU.mult),
                    reads=[mu1], writes=[drow])
                P.add("vector", lambda e: e.tensor_scalar(
                    out=mu1.t[:, 4:5], in0=mu1.t[:, 1:2],
                    scalar1=1.0 / 127.0, scalar2=None, op0=ALU.mult),
                    reads=[mu1], writes=[mu1])
                P.add("vector", lambda e, ti=ti: e.tensor_tensor(
                    out=drow.t[:, ti * 2 + 1:ti * 2 + 2], in0=mu1.t[:, 4:5],
                    in1=swsc.t[:, 2:3], op=ALU.mult),
                    reads=[mu1, swsc], writes=[drow])
                P.add("vector", lambda e, ti=ti: e.tensor_scalar(
                    out=x2f.t[:], in0=xcT.t[:], scalar1=drow.t[:, ti * 2:ti * 2 + 1],
                    scalar2=None, op0=ALU.mult),
                    reads=[xcT, drow], writes=[x2f])
                for hh in range(2):
                    P.add("vector", lambda e, hh=hh: e.tensor_copy(
                        out=ivh.t[:], in_=x2f.t[:, hh * 512:(hh + 1) * 512]),
                        reads=[x2f], writes=[ivh])
                    P.add("vector", lambda e, ti=ti, hh=hh: e.tensor_copy(
                        out=xqT.t[:, ti * D + hh * 512:ti * D + (hh + 1) * 512],
                        in_=ivh.t[:]),
                        reads=[ivh], writes=[xqT])
                tm_to_fm(xqT, ti * D, xqF, WB, ti)

            row_from_cols(drow, 1, 2, NTB, 0)
            bcast_row(0, WB, dscb, 0)

            P.add("vector", lambda e: e.memset(mxp.t[:], 0.0), writes=[mxp])
            for tt in range(32):
                wslot = (tt % 2) * 1024
                def ldw1(e, tt=tt, wslot=wslot):
                    return [e.dma_start(out=w1w.t[:, wslot + k * 128:wslot + (k + 1) * 128],
                                        in_=w1q_sc[k * 128:(k + 1) * 128, tt * 128:(tt + 1) * 128])
                            for k in range(8)]
                P.add("sync", dmas(ldw1, 8), dma="ld", n_dma=8, reads=[w1qB], writes=[w1w])
                bank = pb[tt % 2]
                def mmh(e, tt=tt, wslot=wslot, bank=bank):
                    last = None
                    for k in range(8):
                        last = e.matmul(bank.t[:, 0:WB],
                                        w1w.t[:, wslot + k * 128:wslot + (k + 1) * 128],
                                        xqF.t[:, k * WB:(k + 1) * WB],
                                        start=(k == 0), stop=(k == 7))
                    return last
                P.add("tensor", mmh, reads=[w1w, xqF], writes=[bank])
                if st == NST_B - 1:
                    for b in range(2):
                        c0 = 3 * 128 + b * 64 + 62
                        def evhf(e, tt=tt, b=b, bank=bank, c0=c0):
                            return e.tensor_tensor(
                                out=hprevF.t[:, tt * 4 + b * 2:tt * 4 + b * 2 + 2],
                                in0=bank.t[:, c0:c0 + 2], in1=dscb.t[:, c0:c0 + 2],
                                op=ALU.mult)
                        P.add("vector", evhf, reads=[bank, dscb], writes=[hprevF])
                slot = (tt % 2) * (2 * HBW)
                def evhal(e, tt=tt, slot=slot):
                    return e.tensor_copy(
                        out=hF.t[:, slot:slot + 2 * HBW].rearrange(
                            "p (b w) -> p b w", b=2)[:, :, 0:2],
                        in_=hhalo.t[:, tt * 4:tt * 4 + 4].rearrange("p (b c) -> p b c", b=2))
                P.add("vector", evhal, reads=[hhalo], writes=[hF])
                def evh(e, tt=tt, bank=bank, slot=slot):
                    out_ap = hF.t[:, slot:slot + 2 * HBW].rearrange(
                        "p (b w) -> p b w", b=2)[:, :, 2:2 + ST_B].rearrange(
                        "p b (i j) -> p i b j", j=64)
                    return e.tensor_tensor(
                        out=out_ap,
                        in0=bank.t[:, 0:WB].rearrange("p (i b j) -> p i b j", b=2, j=64),
                        in1=dscb.t[:, 0:WB].rearrange("p (i b j) -> p i b j", b=2, j=64),
                        op=ALU.mult)
                P.add("vector", evh, reads=[bank, dscb], writes=[hF])
                def evsav(e, tt=tt, slot=slot):
                    return e.tensor_copy(
                        out=hhalo.t[:, tt * 4:tt * 4 + 4].rearrange("p (b c) -> p b c", b=2),
                        in_=hF.t[:, slot:slot + 2 * HBW].rearrange(
                            "p (b w) -> p b w", b=2)[:, :, ST_B:ST_B + 2])
                P.add("vector", evsav, reads=[hF], writes=[hhalo])
                def hv(tp, slot=slot):
                    return hF.t[:, slot:slot + 2 * HBW].rearrange(
                        "p (b w) -> p b w", b=2)[:, :, tp:tp + ST_B]
                def cv0(e, tt=tt, slot=slot):
                    return e.tensor_scalar(
                        out=ac0.t[:].rearrange("p (b j) -> p b j", b=2),
                        in0=hv(0, slot), scalar1=c_cw.t[:, tt * 3:tt * 3 + 1],
                        scalar2=None, op0=ALU.mult)
                P.add("vector", cv0, reads=[hF, c_cw], writes=[ac0])
                def cv1(e, tt=tt, slot=slot):
                    return e.scalar_tensor_tensor(
                        out=ac1.t[:].rearrange("p (b j) -> p b j", b=2),
                        in0=hv(1, slot), scalar=c_cw.t[:, tt * 3 + 1:tt * 3 + 2],
                        in1=ac0.t[:].rearrange("p (b j) -> p b j", b=2),
                        op0=ALU.mult, op1=ALU.add)
                P.add("vector", cv1, reads=[hF, c_cw, ac0], writes=[ac1])
                def cv2(e, tt=tt, slot=slot):
                    return e.scalar_tensor_tensor(
                        out=ac0.t[:].rearrange("p (b j) -> p b j", b=2),
                        in0=hv(2, slot), scalar=c_cw.t[:, tt * 3 + 2:tt * 3 + 3],
                        in1=ac1.t[:].rearrange("p (b j) -> p b j", b=2),
                        op0=ALU.mult, op1=ALU.add)
                P.add("vector", cv2, reads=[hF, c_cw, ac1], writes=[ac0])
                def evy(e, tt=tt):
                    return e.activation(out=yf.t[:], in_=ac0.t[:], func=AF.Silu,
                                        bias=c_cb.t[:, tt:tt + 1])
                P.add("scalar", evy, reads=[ac0, c_cb], writes=[yf])
                def evyb(e, tt=tt):
                    return e.copy(out=hq.t[:, tt * WB:(tt + 1) * WB], in_=yf.t[:])
                P.add("scalar", evyb, reads=[yf], writes=[hq])
                def styf(e, tt=tt):
                    r0 = tt * 2 * 128
                    return [e.dma_start(out=y_sc[r0:r0 + 128, :], in_=yf.t[:, 0:256]),
                            e.dma_start(out=y_sc[r0 + 128:r0 + 256, :], in_=yf.t[:, 256:512])]
                P.add("gpsimd", dmas(styf, 2), dma="st", n_dma=2, reads=[yf], writes=[yscB])
                def evab(e):
                    return e.scalar_tensor_tensor(out=ac1.t[:], in0=yf.t[:], scalar=-1.0,
                                                  in1=yf.t[:], op0=ALU.mult, op1=ALU.max)
                P.add("vector", evab, reads=[yf], writes=[ac1])
                def evmx(e):
                    return e.tensor_tensor(out=mxp.t[:], in0=mxp.t[:], in1=ac1.t[:],
                                           op=ALU.max)
                P.add("vector", evmx, reads=[ac1, mxp], writes=[mxp])

            for blk in range(4):
                P.add("tensor", lambda e, blk=blk: e.transpose(
                    pb[2].t[:, blk * 128:(blk + 1) * 128],
                    mxp.t[:, blk * 128:(blk + 1) * 128], c_eyef.t[:]),
                    reads=[mxp, c_eyef], writes=[pb[2]])
            P.add("vector", lambda e: e.tensor_copy(out=dscb.t[:, WB:2 * WB],
                                                    in_=pb[2].t[:, 0:512]),
                  reads=[pb[2]], writes=[dscb])
            P.add("vector", lambda e: e.reduce_max(
                out=drow.t[:, 0:4], in_=dscb.t[:, WB:2 * WB].rearrange("p (a c) -> p a c", a=4),
                axis=AX.X, apply_absolute_value=True),
                reads=[dscb], writes=[drow])
            row_from_cols(drow, 0, 1, 4, WB)
            P.add("vector", lambda e: e.tensor_scalar(out=yrow.t[0:1, WB:2 * WB],
                                                      in0=yrow.t[0:1, WB:2 * WB],
                                                      scalar1=1e-5, scalar2=None, op0=ALU.max),
                  reads=[yrow], writes=[yrow])
            P.add("vector", lambda e: e.reciprocal(out=yrow.t[0:1, 2 * WB:3 * WB],
                                                   in_=yrow.t[0:1, WB:2 * WB]),
                  reads=[yrow], writes=[yrow])
            P.add("vector", lambda e: e.tensor_scalar(out=yrow.t[0:1, 2 * WB:3 * WB],
                                                      in0=yrow.t[0:1, 2 * WB:3 * WB],
                                                      scalar1=127.0, scalar2=None, op0=ALU.mult),
                  reads=[yrow], writes=[yrow])
            P.add("vector", lambda e: e.tensor_scalar(out=yrow.t[0:1, 3 * WB:4 * WB],
                                                      in0=yrow.t[0:1, WB:2 * WB],
                                                      scalar1=1.0 / 127.0, scalar2=None,
                                                      op0=ALU.mult),
                  reads=[yrow], writes=[yrow])
            bcast_row(2 * WB, WB, dscb, WB)
            P.add("vector", lambda e: e.tensor_copy(
                out=yrow.t[0:1, 0:WB].rearrange("o (i b j) -> o i b j", b=2, j=64),
                in_=yrow.t[0:1, 3 * WB:4 * WB].rearrange("o (b i j) -> o i b j", b=2, j=64)),
                reads=[yrow], writes=[yrow])
            bcast_row(0, WB, dscb, 2 * WB)
            for tt in range(32):
                ls = 0
                def ldyf(e, tt=tt, ls=ls):
                    r0 = tt * 2 * 128
                    return [e.dma_start(out=yld.t[:, ls:ls + 256], in_=y_sc[r0:r0 + 128, :]),
                            e.dma_start(out=yld.t[:, ls + 256:ls + 512],
                                        in_=y_sc[r0 + 128:r0 + 256, :])]
                P.add("sync", dmas(ldyf, 2), dma="ld", n_dma=2, reads=[yscB], writes=[yld])
                def q1(e, tt=tt, ls=ls):
                    return e.tensor_tensor(
                        out=ivh.t[:],
                        in0=yld.t[:, ls:ls + 512],
                        in1=dscb.t[:, WB:2 * WB],
                        op=ALU.mult)
                P.add("vector", q1, reads=[yld, dscb], writes=[ivh])
                def q2(e, tt=tt):
                    return e.tensor_copy(
                        out=hq.t[:, tt * WB:(tt + 1) * WB].rearrange(
                            "p (i b j) -> p i b j", b=2, j=64),
                        in_=ivh.t[:].rearrange("p (b i j) -> p i b j", b=2, j=64))
                P.add("vector", q2, reads=[ivh], writes=[hq])
            P.add("vector", lambda e: e.tensor_scalar(out=dscb.t[:, 2 * WB:3 * WB],
                                                      in0=dscb.t[:, 2 * WB:3 * WB],
                                                      scalar1=swsc.t[:, 3:4], scalar2=None,
                                                      op0=ALU.mult),
                  reads=[dscb, swsc], writes=[dscb])
            for mo in range(8):
                wslot = (mo % 2) * 4096
                def ldw2(e, mo=mo, wslot=wslot):
                    return [e.dma_start(out=w2w.t[:, wslot + k * 128:wslot + (k + 1) * 128],
                                        in_=w2q_sc[k * 128:(k + 1) * 128, mo * 128:(mo + 1) * 128])
                            for k in range(32)]
                P.add("sync", dmas(ldw2, 32), dma="ld", n_dma=32, reads=[w2qB], writes=[w2w])
                bank = pb[mo % 2]
                def mmw2(e, mo=mo, wslot=wslot, bank=bank):
                    last = None
                    for k in range(32):
                        last = e.matmul(bank.t[:, 0:WB],
                                        w2w.t[:, wslot + k * 128:wslot + (k + 1) * 128],
                                        hq.t[:, k * WB:(k + 1) * WB],
                                        start=(k == 0), stop=(k == 31))
                    return last
                P.add("tensor", mmw2, reads=[w2w, hq], writes=[bank])
                def evw2(e, mo=mo, bank=bank):
                    return e.tensor_tensor(out=outF.t[:, mo * WB:(mo + 1) * WB],
                                           in0=bank.t[:, 0:WB], in1=dscb.t[:, 2 * WB:3 * WB],
                                           op=ALU.mult)
                P.add("vector", evw2, reads=[bank, dscb], writes=[outF])
                def evw2f(e, mo=mo, bank=bank):
                    return e.tensor_tensor(out=outFf.t[:, mo * WB:(mo + 1) * WB],
                                           in0=bank.t[:, 0:WB], in1=dscb.t[:, 2 * WB:3 * WB],
                                           op=ALU.mult)
                P.add("vector", evw2f, reads=[bank, dscb], writes=[outFf])

            fm_ln_b(outF, outFf, onB, WB)
            for m in range(NTB):
                for k in range(8):
                    sl = k % 8
                    P.add("tensor", lambda e, m=m, k=k, sl=sl: e.transpose(
                        pT.t[:, sl * 128:(sl + 1) * 128],
                        onB.t[:, k * WB + m * 128:k * WB + (m + 1) * 128], c_eye.t[:]),
                        reads=[onB, c_eye], writes=[pT])
                def evd(e, m=m):
                    return e.tensor_copy(
                        out=d2TM.t[:].rearrange("p (m k c) -> p m k c", m=NTB, k=8)[:, m, :, :],
                        in_=pT.t[:].rearrange("p (a c) -> p a c", a=8))
                P.add("vector", evd, reads=[pT], writes=[d2TM])
            P.add("vector", lambda e: e.tensor_tensor(out=x2T.t[:], in0=x1s.t[:], in1=d2TM.t[:],
                                                      op=ALU.add),
                  reads=[x1s, d2TM], writes=[x2T])

            def sty(e, st=st, rb=rb):
                ds = []
                for ti in range(NTB):
                    for b in range(NB):
                        r0 = b * T + rb - 1 + ti * 64
                        p0 = b * 64
                        if st == 0 and ti == 0:
                            ds.append(e.dma_start(out=y_out[b * T + 0:b * T + 63, :],
                                                  in_=x2T.t[p0 + 1:p0 + 64, ti * D:(ti + 1) * D]))
                        else:
                            ds.append(e.dma_start(out=y_out[r0:r0 + 64, :],
                                                  in_=x2T.t[p0:p0 + 64, ti * D:(ti + 1) * D]))
                return ds
            P.add("gpsimd", dmas(sty, NTB * NB), dma="st", n_dma=NTB * NB, reads=[x2T])



        if stop_after is None:
            for st in range(NST_B):
                b_supertile(st)
        tapB("Bdscb", dscb, 128, 3 * WB)
        tapB("Bdrow", drow, 128, NTB * 2)
        tapB("Bhq", hq, 128, 32 * WB)
        tapB("BoutF", outF, 128, 8 * WB)
        tapB("BxqF", xqF, 128, 8 * WB)
        tapB("Byrow", yrow, 1, 4 * WB)
        tapB("Bswsc", swsc, 128, 4)

        def final_token():
            for tp2 in range(2):
                def cv(e, tp2=tp2):
                    return e.tensor_tensor(
                        out=ymx.t[:, 64 + tp2 * 64:128 + tp2 * 64].rearrange(
                            "p (t b) -> p t b", b=2),
                        in0=hprevF.t[:].rearrange("p (t b c) -> p t b c", b=2, c=2)[:, :, :, tp2],
                        in1=c_cw.t[:].rearrange("p (t k) -> p t k", k=3)[:, :, tp2:tp2 + 1]
                        .broadcast_to((128, 32, 2)),
                        op=ALU.mult)
                P.add("vector", cv, reads=[hprevF, c_cw], writes=[ymx])
            P.add("vector", lambda e: e.tensor_tensor(out=ymx.t[:, 0:64], in0=ymx.t[:, 64:128],
                                                      in1=ymx.t[:, 128:192], op=ALU.add),
                  reads=[ymx], writes=[ymx])
            P.add("vector", lambda e: e.tensor_tensor(
                out=ymx.t[:, 0:64].rearrange("p (t b) -> p t b", b=2),
                in0=ymx.t[:, 0:64].rearrange("p (t b) -> p t b", b=2),
                in1=c_cb.t[:, :, None].broadcast_to((128, 32, 2)), op=ALU.add),
                reads=[ymx, c_cb], writes=[ymx])
            P.add("scalar", lambda e: e.activation(out=ymx.t[:, 0:64], in_=ymx.t[:, 0:64],
                                                   func=AF.Silu),
                  reads=[ymx], writes=[ymx])
            P.add("vector", lambda e: e.reduce_max(
                out=drow.t[:, 0:2], in_=ymx.t[:, 0:64].rearrange("p (t b) -> p b t", b=2),
                axis=AX.X, apply_absolute_value=True),
                reads=[ymx], writes=[drow])
            row_from_cols(drow, 0, 1, 2, 0)
            P.add("vector", lambda e: e.reduce_max(out=yrow.t[0:1, 300:301],
                                                   in_=yrow.t[0:1, 0:128], axis=AX.X),
                  reads=[yrow], writes=[yrow])
            P.add("vector", lambda e: e.reduce_max(out=yrow.t[0:1, 301:302],
                                                   in_=yrow.t[0:1, 128:256], axis=AX.X),
                  reads=[yrow], writes=[yrow])
            P.add("vector", lambda e: e.tensor_scalar(out=yrow.t[0:1, 300:302],
                                                      in0=yrow.t[0:1, 300:302],
                                                      scalar1=1e-5, scalar2=None, op0=ALU.max),
                  reads=[yrow], writes=[yrow])
            P.add("vector", lambda e: e.reciprocal(out=yrow.t[0:1, 302:304],
                                                   in_=yrow.t[0:1, 300:302]),
                  reads=[yrow], writes=[yrow])
            P.add("vector", lambda e: e.tensor_scalar(out=yrow.t[0:1, 302:304],
                                                      in0=yrow.t[0:1, 302:304],
                                                      scalar1=127.0, scalar2=None, op0=ALU.mult),
                  reads=[yrow], writes=[yrow])

            def mmbf(e):
                e.matmul(pb[4].t[:, 0:2], c_onesrf.t[:], yrow.t[0:1, 302:304],
                         start=True, stop=True)
                return e.matmul(pb[4].t[:, 2:4], c_onesrf.t[:], yrow.t[0:1, 300:302],
                                start=True, stop=True)
            P.add("tensor", mmbf, reads=[yrow, c_onesrf], writes=[pb[4]])
            P.add("vector", lambda e: e.tensor_copy(out=drow.t[:, 2:6], in_=pb[4].t[:, 0:4]),
                  reads=[pb[4]], writes=[drow])
            P.add("vector", lambda e: e.tensor_tensor(
                out=ivh.t[:, 0:64].rearrange("p (t b) -> p t b", b=2),
                in0=ymx.t[:, 0:64].rearrange("p (t b) -> p t b", b=2),
                in1=drow.t[:, 2:4][:, None, :].broadcast_to((128, 32, 2)), op=ALU.mult),
                reads=[ymx, drow], writes=[ivh])
            P.add("vector", lambda e: e.tensor_copy(out=hq.t[:, 0:64], in_=ivh.t[:, 0:64]),
                  reads=[ivh], writes=[hq])
            for mo in range(8):
                wslot = (mo % 2) * 4096
                def ldw2(e, mo=mo, wslot=wslot):
                    return [e.dma_start(out=w2w.t[:, wslot + k * 128:wslot + (k + 1) * 128],
                                        in_=w2q_sc[k * 128:(k + 1) * 128, mo * 128:(mo + 1) * 128])
                            for k in range(32)]
                P.add("sync", dmas(ldw2, 32), dma="ld", n_dma=32, reads=[w2qB], writes=[w2w])
                bank = pb[mo % 2]
                def mmw2(e, mo=mo, wslot=wslot, bank=bank):
                    last = None
                    for k in range(32):
                        last = e.matmul(bank.t[:, 0:2],
                                        w2w.t[:, wslot + k * 128:wslot + (k + 1) * 128],
                                        hq.t[:, k * 2:(k + 1) * 2],
                                        start=(k == 0), stop=(k == 31))
                    return last
                P.add("tensor", mmw2, reads=[w2w, hq], writes=[bank])
                def evw2(e, mo=mo, bank=bank):
                    return e.tensor_tensor(out=outF.t[:, mo * 2:(mo + 1) * 2],
                                           in0=bank.t[:, 0:2], in1=drow.t[:, 4:6], op=ALU.mult)
                P.add("vector", evw2, reads=[bank, drow], writes=[outF])
            P.add("vector", lambda e: e.tensor_scalar(out=outF.t[:, 0:16], in0=outF.t[:, 0:16],
                                                      scalar1=1.0 / 127.0, scalar2=None,
                                                      op0=ALU.mult),
                  reads=[outF], writes=[outF])
            P.add("vector", lambda e: e.tensor_scalar(out=outF.t[:, 0:16], in0=outF.t[:, 0:16],
                                                      scalar1=swsc.t[:, 3:4], scalar2=None,
                                                      op0=ALU.mult),
                  reads=[outF, swsc], writes=[outF])
            tapB("FoutFpre", outF, 128, 16)
            def mmsum(e):
                last = None
                for k in range(8):
                    last = e.matmul(pb[5].t[0:1, 0:2], c_ones.t[:], outF.t[:, k * 2:(k + 1) * 2],
                                    start=(k == 0), stop=(k == 7))
                return last
            P.add("tensor", mmsum, reads=[outF, c_ones], writes=[pb[5]])
            for k in range(8):
                P.add("scalar", lambda e, k=k: e.activation(out=o2pB.t[:, 0:2],
                                                            in_=outF.t[:, k * 2:(k + 1) * 2],
                                                            func=AF.Square),
                      reads=[outF], writes=[o2pB])
                def mm2(e, k=k):
                    return e.matmul(pb[5].t[64:65, 0:2], c_ones.t[:], o2pB.t[:, 0:2],
                                    start=(k == 0), stop=(k == 7))
                P.add("tensor", mm2, reads=[o2pB, c_ones], writes=[pb[5]])
            P.add("vector", lambda e: e.tensor_scalar(out=yrow.t[0:1, 310:312],
                                                      in0=pb[5].t[0:1, 0:2],
                                                      scalar1=1.0 / D, scalar2=None, op0=ALU.mult),
                  reads=[pb[5]], writes=[yrow])
            P.add("vector", lambda e: e.tensor_scalar(out=yrow.t[0:1, 312:314],
                                                      in0=pb[5].t[64:65, 0:2],
                                                      scalar1=1.0 / D, scalar2=None, op0=ALU.mult),
                  reads=[pb[5]], writes=[yrow])
            P.add("vector", lambda e: e.tensor_tensor(out=yrow.t[0:1, 314:316],
                                                      in0=yrow.t[0:1, 310:312],
                                                      in1=yrow.t[0:1, 310:312], op=ALU.mult),
                  reads=[yrow], writes=[yrow])
            P.add("vector", lambda e: e.tensor_tensor(out=yrow.t[0:1, 312:314],
                                                      in0=yrow.t[0:1, 312:314],
                                                      in1=yrow.t[0:1, 314:316], op=ALU.subtract),
                  reads=[yrow], writes=[yrow])
            P.add("scalar", lambda e: e.activation(out=yrow.t[0:1, 312:314],
                                                   in_=yrow.t[0:1, 312:314], func=AF.Sqrt,
                                                   bias=cflt.t[0:1, 0:1]),
                  reads=[yrow], writes=[yrow])
            P.add("vector", lambda e: e.reciprocal(out=yrow.t[0:1, 312:314],
                                                   in_=yrow.t[0:1, 312:314]),
                  reads=[yrow], writes=[yrow])
            def mmbm(e):
                e.matmul(pb[4].t[:, 4:6], c_onesrf.t[:], yrow.t[0:1, 310:312],
                         start=True, stop=True)
                return e.matmul(pb[4].t[:, 6:8], c_onesrf.t[:], yrow.t[0:1, 312:314],
                                start=True, stop=True)
            P.add("tensor", mmbm, reads=[yrow, c_onesrf], writes=[pb[4]])
            P.add("vector", lambda e: e.tensor_copy(out=drow.t[:, 6:8], in_=pb[4].t[:, 4:6]),
                  reads=[pb[4]], writes=[drow])
            P.add("vector", lambda e: e.tensor_copy(out=drow.t[:, 0:2], in_=pb[4].t[:, 6:8]),
                  reads=[pb[4]], writes=[drow])
            for k in range(8):
                P.add("vector", lambda e, k=k: e.tensor_tensor(out=outF.t[:, k * 2:(k + 1) * 2],
                                                               in0=outF.t[:, k * 2:(k + 1) * 2],
                                                               in1=drow.t[:, 6:8], op=ALU.subtract),
                      reads=[outF, drow], writes=[outF])
                P.add("vector", lambda e, k=k: e.tensor_tensor(out=outF.t[:, k * 2:(k + 1) * 2],
                                                               in0=outF.t[:, k * 2:(k + 1) * 2],
                                                               in1=drow.t[:, 0:2], op=ALU.mult),
                      reads=[outF, drow], writes=[outF])
                P.add("tensor", lambda e, k=k: e.transpose(pT.t[0:2, k * 128:(k + 1) * 128],
                                                           outF.t[:, k * 2:(k + 1) * 2],
                                                           c_eye.t[:]),
                      reads=[outF, c_eye], writes=[pT])
            P.add("vector", lambda e: e.tensor_copy(out=x2T.t[0:2, 0:D], in_=pT.t[0:2, 0:1024]),
                  reads=[pT], writes=[x2T])
            def ldlast(e):
                return [e.dma_start(out=x1s.t[0:1, 0:D], in_=x1_sc[T - 1:T, :]),
                        e.dma_start(out=x1s.t[1:2, 0:D], in_=x1_sc[2 * T - 1:2 * T, :])]
            P.add("sync", dmas(ldlast, 2), dma="ld", reads=[x1B], writes=[x1s])
            P.add("vector", lambda e: e.tensor_tensor(out=x2T.t[0:2, 0:D], in0=x1s.t[0:2, 0:D],
                                                      in1=x2T.t[0:2, 0:D], op=ALU.add),
                  reads=[x1s, x2T], writes=[x2T])
            def styl(e):
                return [e.dma_start(out=y_out[T - 1:T, :], in_=x2T.t[0:1, 0:D]),
                        e.dma_start(out=y_out[2 * T - 1:2 * T, :], in_=x2T.t[1:2, 0:D])]
            P.add("gpsimd", dmas(styl, 2), dma="st", reads=[x2T])

        if stop_after is None:
            final_token()
        tapB("Fymx", ymx, 128, 192)
        tapB("Fhprevf", hprevF, 128, 32 * 2 * 2)
        tapB("Fhq", hq, 128, 64)
        tapB("FoutF", outF, 128, 16)
        tapB("Fyrow", yrow, 1, 320)
        tapB("Fdrow", drow, 128, 8)
        tapB("Fx2T", x1b, 128, D)

        P.wait_all("gpsimd")

        if max_steps is not None:
            P.steps = P.steps[:max_steps]
        with nc.Block() as block:
            P.emit_into(block, sems)

    return nc


def _constants():
    e = np.eye(128, dtype=ml_dtypes.bfloat16)
    tri = np.triu(np.ones((C, C), np.float32))
    cum = np.zeros((128, 128), np.float32)
    cum[0:64, 0:64] = tri
    cum[64:128, 64:128] = tri
    mlow = np.tril(np.ones((C, C), np.float32), -1)
    mupp = np.triu(np.ones((C, C), np.float32), 1)
    mud = np.triu(np.ones((C, C), np.float32), 0)
    ip = np.eye(C, dtype=np.float32)
    ones1 = np.ones((128, 1), np.float32)
    stack2 = lambda m: np.concatenate([m, m], axis=0).astype(np.float32)
    return dict(eye128=e, eye128f=np.eye(128, dtype=np.float32),
                cumtri=cum.astype(ml_dtypes.bfloat16),
                mlow=stack2(mlow), mupp=stack2(mupp), mud=stack2(mud),
                ipair=np.concatenate([ip, ip], axis=0).astype(ml_dtypes.bfloat16),
                ones1=ones1.astype(ml_dtypes.bfloat16),
                onesr=np.ones((1, 128), ml_dtypes.bfloat16),
                onesrf=np.ones((1, 128), np.float32))


def kernel(L0_attn_x, L0_attn_Wq, L0_attn_Wk, L0_attn_Wv, L0_attn_Wgamma,
           L0_attn_Wf1, L0_attn_Wf2, L0_attn_Wog1, L0_attn_Wog2, L0_attn_norm_w,
           L0_attn_Wo, L0_ffn_w1, L0_ffn_conv_w, L0_ffn_conv_b, L0_ffn_w2):
    T = 4096
    x = np.ascontiguousarray(L0_attn_x, dtype=np.float32)
    consts = _constants()
    common = dict(Wq=np.ascontiguousarray(L0_attn_Wq),
                  Wk=np.ascontiguousarray(L0_attn_Wk),
                  Wv=np.ascontiguousarray(L0_attn_Wv),
                  Wgamma=np.ascontiguousarray(L0_attn_Wgamma),
                  Wf1=np.ascontiguousarray(L0_attn_Wf1),
                  Wf2=np.ascontiguousarray(L0_attn_Wf2),
                  Wog1=np.ascontiguousarray(L0_attn_Wog1),
                  Wog2=np.ascontiguousarray(L0_attn_Wog2),
                  norm_w=np.ascontiguousarray(L0_attn_norm_w).reshape(1, D),
                  Wo=np.ascontiguousarray(L0_attn_Wo),
                  w1=np.ascontiguousarray(L0_ffn_w1),
                  conv_w=np.ascontiguousarray(L0_ffn_conv_w).reshape(DH, 3),
                  conv_b=np.ascontiguousarray(L0_ffn_conv_b).reshape(1, DH),
                  w2=np.ascontiguousarray(L0_ffn_w2),
                  **consts)
    nc = build_program(T)
    in_maps = []
    for c in range(8):
        xs = x[2 * c:2 * c + 2].reshape(NB * T, D)
        in_maps.append(dict(x=np.ascontiguousarray(xs), **common))
    res = run_bass_kernel_spmd(nc, in_maps, core_ids=list(range(8)))
    out = np.empty((16, T, D), np.float32)
    for c in range(8):
        out[2 * c:2 * c + 2] = res.results[c]["y"].reshape(NB, T, D)
    return out


# version 08 (final)
# speedup vs baseline: 1.5511x; 1.5511x over previous
import contextlib
import numpy as np
import ml_dtypes
import concourse.bass as bass
import concourse.mybir as mybir
from concourse.bass_utils import run_bass_kernel_spmd

F32 = mybir.dt.float32
BF16 = mybir.dt.bfloat16
I32 = mybir.dt.int32
AF = mybir.ActivationFunctionType
ALU = mybir.AluOpType
AX = mybir.AxisListType

D = 1024
H = 16
HD = 64
DH = 4096
C = 64
NB = 2
HPG = 8


class Buf:
    def __init__(self, t, name=""):
        self.t = t
        self.name = name
        self.wtok = None
        self.rtoks = []


class Prog:
    DSEMS = [f"{q}{i}" for q in ("ld", "st") for i in range(4)]

    def __init__(self):
        self.steps = []
        self.cnt = {"t": 0, "v": 0, "s": 0, "g": 0}
        for k in self.DSEMS:
            self.cnt[k] = 0
        self.dseq = {"ld": 0, "st": 0}
        self.esem = {"tensor": "t", "vector": "v", "scalar": "s", "gpsimd": "g"}

    def add(self, eng, fn, reads=(), writes=(), dma=None, n_dma=1):
        waits = {}
        for b in reads:
            if b.wtok is not None:
                s, v = b.wtok
                waits[s] = max(waits.get(s, 0), v)
        for b in writes:
            toks = ([b.wtok] if b.wtok else []) + b.rtoks
            for (s, v) in toks:
                waits[s] = max(waits.get(s, 0), v)
        if dma:
            sem = f"{dma}{self.dseq[dma] % 4}"
            self.dseq[dma] += 1
            self.cnt[sem] += 16 * n_dma
        else:
            sem = self.esem[eng]
            self.cnt[sem] += 1
        self.steps.append((eng, waits, fn, sem,
                           frozenset(id(b) for b in reads),
                           frozenset(id(b) for b in writes)))
        tok = (sem, self.cnt[sem])
        for b in reads:
            b.rtoks.append(tok)
        for b in writes:
            b.wtok = tok
            b.rtoks = []
        return tok

    def wait_all(self, eng, skip=()):
        snap = dict(self.cnt)
        own = self.esem.get(eng)

        def fn(e):
            return None
        fn._waits = {s: v for s, v in snap.items() if s != own and s not in skip and v > 0}
        self.steps.append((eng, fn._waits, None, None, frozenset(), frozenset()))

    def emit_into(self, block, sems):
        def mk(en):
            def run(eng):
                own = self.esem.get(en)
                hw = {}
                prev_w = frozenset()
                for (e, waits, fn, sem, rds, wrs) in self.steps:
                    if e != en:
                        continue
                    for s, v in waits.items():
                        if s == own:
                            continue
                        if hw.get(s, 0) >= v:
                            continue
                        eng.wait_ge(sems[s], v)
                        hw[s] = v
                    if fn is None:
                        continue
                    if en == "vector" and (prev_w & rds):
                        eng.tensor_copy(out=self.junk[:, 0:1], in_=self.junk[:, 1:2])
                    if en == "vector":
                        prev_w = wrs
                    out = fn(eng)
                    if sem is not None and sem not in ("t", "v", "s", "g"):
                        for ins in (out if isinstance(out, (list, tuple)) else [out]):
                            ins.then_inc(sems[sem], 16)
                    else:
                        out.then_inc(sems[sem], 1)
            return run
        block.sync(mk("sync"))
        block.vector(mk("vector"))
        block.scalar(mk("scalar"))
        block.tensor(mk("tensor"))
        block.gpsimd(mk("gpsimd"))


def build_program(T, dbg=(), stop_after=None, max_steps=None):
    ST_A = 128
    ST_B = 256
    NCH = ST_A // C
    NST_A = T // ST_A
    NST_B = T // ST_B
    NTA = 2 * ST_A // 128
    NTB = 2 * ST_B // 128
    WA = 2 * ST_A
    WB = 2 * ST_B
    HBW = ST_B + 4

    nc = bass.Bass("TRN2", target_bir_lowering=False, debug=False)

    x_in = nc.dram_tensor("x", [NB * T, D], F32, kind="ExternalInput").ap()
    ws_in = {}
    for nm, shp in (("Wq", [D, D]), ("Wk", [D, D]), ("Wv", [D, D]), ("Wgamma", [D, H]),
                    ("Wf1", [D, HD]), ("Wf2", [HD, D]), ("Wog1", [D, HD]),
                    ("Wog2", [HD, D]), ("norm_w", [1, D]), ("Wo", [D, D]),
                    ("w1", [DH, D]), ("conv_w", [DH, 3]), ("conv_b", [1, DH]),
                    ("w2", [D, DH]), ("eye128", [128, 128]), ("cumtri", [128, 128]),
                    ("mlow", [128, C]), ("mupp", [128, C]), ("mud", [128, C]),
                    ("eye128f", [128, 128]),
                    ("ipair", [128, C]), ("ones1", [128, 1]), ("onesr", [1, 128]), ("onesrf", [1, 128])):
        dt = BF16 if nm in ("eye128", "cumtri", "ipair", "ones1", "onesr") else F32
        ws_in[nm] = nc.dram_tensor(nm, shp, dt, kind="ExternalInput").ap()

    y_out = nc.dram_tensor("y", [NB * T, D], F32, kind="ExternalOutput").ap()
    x1_sc = nc.dram_tensor("x1sc", [NB * T, D], F32, kind="Internal").ap()
    sc_sc = nc.dram_tensor("scsc", [8, 4], F32, kind="Internal").ap()
    row_sc = nc.dram_tensor("rowsc", [4, 512], F32, kind="Internal").ap()
    w1q_sc = nc.dram_tensor("w1qsc", [DH, D], BF16, kind="Internal").ap()
    y_sc = nc.dram_tensor("ysc", [32 * 2 * 128, 256], F32, kind="Internal").ap()
    w2q_sc = nc.dram_tensor("w2qsc", [D, DH], BF16, kind="Internal").ap()

    dbg_outs = {}
    for name, shape in dbg:
        dbg_outs[name] = nc.dram_tensor("dbg_" + name, list(shape), F32,
                                        kind="ExternalOutput").ap()

    P = Prog()

    rowB = Buf(None, "row_sc")
    scB = Buf(None, "sc_sc")
    woqB = Buf(None, "woq_sc")
    w1qB = Buf(None, "w1q_sc")
    w2qB = Buf(None, "w2q_sc")
    x1B = Buf(None, "x1_sc")
    yscB4 = [Buf(None, f"y_sc{i}") for i in range(4)]

    with contextlib.ExitStack() as ES:
        sems = {s: ES.enter_context(nc.semaphore("sem_" + s))
                for s in ["t", "v", "s", "g"] + Prog.DSEMS}

        def sb(name, p, f, dt):
            return Buf(ES.enter_context(nc.sbuf_tensor(name, [p, f], dt)), name)

        P.junk = ES.enter_context(nc.sbuf_tensor("jkraw", [128, 4], F32))

        pb = [Buf(ES.enter_context(nc.psum_tensor(f"pb{i}", [128, 512], F32)), f"pb{i}")
              for i in range(7)]
        pT = Buf(ES.enter_context(nc.psum_tensor("pT", [128, 1024], BF16)), "pT")
        pT_p = [pT, Buf(pT.t, "pT_odd")]
        pTB = pT_p

        c_eye = sb("c_eye", 128, 128, BF16)
        c_eyef = sb("c_eyef", 128, 128, F32)
        c_cum = sb("c_cum", 128, 128, BF16)
        c_mlow = sb("c_mlow", 128, C, F32)
        c_mupp = sb("c_mupp", 128, C, F32)
        c_mud = sb("c_mud", 128, C, F32)
        c_ipair = sb("c_ipair", 128, C, BF16)
        c_ones = sb("c_ones", 128, 1, BF16)
        c_onesr = sb("c_onesr", 1, 128, BF16)
        c_onesrf = sb("c_onesrf", 1, 128, F32)
        w_f1 = sb("w_f1", 128, 8 * HD, BF16)
        w_og1 = sb("w_og1", 128, 8 * HD, BF16)
        w_f2 = sb("w_f2", 128, D, BF16)
        w_og2 = sb("w_og2", 128, D, BF16)
        w_gam = sb("w_gam", 128, 8 * H, BF16)
        c_cb = sb("c_cb", 128, 32, F32)
        c_cw = sb("c_cw", 128, 96, F32)
        n_w = sb("n_w", 128, 8, F32)
        swsc = sb("swsc", 128, 4, F32)
        s_state = sb("s_state", 128, HPG * 2 * HD, BF16)
        x2f = sb("x2f", 128, D, F32)
        mu1 = sb("mu1", 128, 8, F32)
        cflt = sb("cflt", 128, 2, F32)
        WS = contextlib.ExitStack()

        def sbW(name, p, f, dt):
            return Buf(WS.enter_context(nc.sbuf_tensor(name, [p, f], dt)), name)

        w_q = sbW("w_q", 128, 8 * D, BF16)
        w_k = sbW("w_k", 128, 8 * D, BF16)
        w_v = sbW("w_v", 128, 8 * D, BF16)
        w_ow = sbW("w_ow", 128, 2 * 8 * 128, BF16)
        w_ow_p = [w_ow, Buf(w_ow.t, "w_ow_odd")]

        def dmas(fn, n):
            fn.n_dma = n
            return fn

        P.add("vector", lambda e: e.memset(cflt.t[:, 0:1], 1e-5), writes=[cflt])
        P.add("vector", lambda e: e.memset(cflt.t[:, 1:2], -1.0), writes=[cflt])
        P.add("vector", lambda e: e.memset(w_f2.t[64:128, :], 0.0), writes=[w_f2])
        P.add("vector", lambda e: e.memset(w_og2.t[64:128, :], 0.0), writes=[w_og2])

        def ld_const(e):
            return [e.dma_start(out=c_eye.t[:], in_=ws_in["eye128"]),
                    e.dma_start(out=c_eyef.t[:], in_=ws_in["eye128f"]),
                    e.dma_start(out=c_cum.t[:], in_=ws_in["cumtri"]),
                    e.dma_start(out=c_mlow.t[:], in_=ws_in["mlow"]),
                    e.dma_start(out=c_mupp.t[:], in_=ws_in["mupp"]),
                    e.dma_start(out=c_mud.t[:], in_=ws_in["mud"]),
                    e.dma_start(out=c_ipair.t[:], in_=ws_in["ipair"]),
                    e.dma_start(out=c_ones.t[:], in_=ws_in["ones1"]),
                    e.dma_start(out=c_onesr.t[:], in_=ws_in["onesr"]),
                    e.dma_start(out=c_onesrf.t[:], in_=ws_in["onesrf"]),
                    e.dma_start(out=c_cb.t[:].rearrange("p (a b) -> p a b", b=1),
                                in_=ws_in["conv_b"].rearrange("o (a p b) -> p a (o b)", p=128, b=1),
                                allow_slow_non_contiguous=True),
                    e.dma_start(out=c_cw.t[:].rearrange("p (a b) -> p a b", b=3),
                                in_=ws_in["conv_w"].rearrange("(a p) b -> p a b", p=128),
                                allow_slow_non_contiguous=True),
                    e.dma_start(out=n_w.t[:].rearrange("p (a b) -> p a b", b=1),
                                in_=ws_in["norm_w"].rearrange("o (a p b) -> p a (o b)", p=128, b=1),
                                allow_slow_non_contiguous=True)]
        P.add("sync", dmas(ld_const, 13), dma="ld", n_dma=13,
              writes=[c_eye, c_eyef, c_cum, c_mlow, c_mupp, c_mud, c_ipair, c_ones, c_onesr, c_onesrf, c_cb, c_cw, n_w])

        with contextlib.ExitStack() as PS:
            def psb(name, p, f, dt):
                return Buf(PS.enter_context(nc.sbuf_tensor(name, [p, f], dt)), name)

            stg = psb("stg", 128, 4096, F32)
            tq = psb("tq", 128, 4096, BF16)
            tqt = psb("tqt", 128, 1024, BF16)
            acc1 = psb("acc1", 128, 2, F32)
            accb = psb("accb", 128, 1, F32)
            tr1 = psb("tr1", 1, 128, F32)
            iv0 = psb("iv0", 128, 4096, I32)

            def cast_small(nm, dst, cols, kt):
                def ld(e):
                    return [e.dma_start(out=stg.t[:, k * cols:(k + 1) * cols],
                                        in_=ws_in[nm][k * 128:(k + 1) * 128, 0:cols])
                            for k in range(kt)]
                P.add("sync", dmas(ld, kt), dma="ld", n_dma=kt, writes=[stg])
                P.add("vector", lambda e, cols=cols, kt=kt, dst=dst: e.tensor_copy(
                    out=dst.t[:, 0:kt * cols], in_=stg.t[:, 0:kt * cols]),
                    reads=[stg], writes=[dst])

            cast_small("Wf1", w_f1, HD, 8)
            cast_small("Wog1", w_og1, HD, 8)
            cast_small("Wgamma", w_gam, H, 8)

            def ld_f2(e):
                return [e.dma_start(out=stg.t[0:64, 0:D], in_=ws_in["Wf2"]),
                        e.dma_start(out=stg.t[0:64, D:2 * D], in_=ws_in["Wog2"])]
            P.add("sync", dmas(ld_f2, 2), dma="ld", n_dma=2, writes=[stg])
            P.add("vector", lambda e: e.tensor_copy(out=w_f2.t[0:64, :], in_=stg.t[0:64, 0:D]),
                  reads=[stg], writes=[w_f2])
            P.add("vector", lambda e: e.tensor_copy(out=w_og2.t[0:64, :], in_=stg.t[0:64, D:2 * D]),
                  reads=[stg], writes=[w_og2])

            for (nm, dst) in (("Wq", w_q), ("Wk", w_k), ("Wv", w_v)):
                for k in range(8):
                    half = (k % 2) * 2048
                    def ld(e, nm=nm, k=k, half=half):
                        return [e.dma_start(out=stg.t[:, half:half + D],
                                            in_=ws_in[nm][k * 128:(k + 1) * 128, :])]
                    P.add("sync", dmas(ld, 1), dma="ld", writes=[stg])
                    P.add("vector", lambda e, dst=dst, k=k, half=half: e.tensor_copy(
                        out=dst.t[:, k * D:(k + 1) * D], in_=stg.t[:, half:half + D]),
                        reads=[stg], writes=[dst])

            woq_sc = nc.dram_tensor("woqsc", [D, D], BF16, kind="Internal").ap()
            for k in range(8):
                half = (k % 2) * 2048
                def ld(e, k=k, half=half):
                    return [e.dma_start(out=stg.t[:, half:half + D],
                                        in_=ws_in["Wo"][k * 128:(k + 1) * 128, :])]
                P.add("sync", dmas(ld, 1), dma="ld", writes=[stg])
                P.add("vector", lambda e, k=k, half=half: e.tensor_scalar(
                    out=tq.t[:, 0:D], in0=stg.t[:, half:half + D],
                    scalar1=n_w.t[:, k:k + 1], scalar2=None, op0=ALU.mult),
                    reads=[stg, n_w], writes=[tq])
                def stw(e, k=k):
                    return [e.dma_start(out=woq_sc[mo * 128:(mo + 1) * 128, k * 128:(k + 1) * 128],
                                        in_=tq.t[:, mo * 128:(mo + 1) * 128]) for mo in range(8)]
                P.add("gpsimd", dmas(stw, 8), dma="st", n_dma=8, reads=[tq], writes=[woqB])

            def wmean(nm, rows, cols, out_col):
                nt = rows // 128
                P.add("vector", lambda e: e.memset(acc1.t[:, 0:1], 0.0), writes=[acc1])
                for i in range(nt):
                    def ld(e, i=i):
                        return [e.dma_start(out=stg.t[:, 0:cols],
                                            in_=ws_in[nm][i * 128:(i + 1) * 128, :])]
                    P.add("sync", dmas(ld, 1), dma="ld", writes=[stg])
                    P.add("vector", lambda e: e.reduce_sum(out=acc1.t[:, 1:2], in_=stg.t[:, 0:cols],
                                                           axis=AX.X, apply_absolute_value=True),
                          reads=[stg], writes=[acc1])
                    P.add("vector", lambda e: e.tensor_add(out=acc1.t[:, 0:1], in0=acc1.t[:, 0:1],
                                                           in1=acc1.t[:, 1:2]),
                          reads=[acc1], writes=[acc1])
                P.add("vector", lambda e: e.tensor_copy(out=accb.t[:], in_=acc1.t[:, 0:1]),
                      reads=[acc1], writes=[accb])
                P.add("tensor", lambda e: e.transpose(pb[0].t[0:1, 0:128], accb.t[:], c_eyef.t[:]),
                      reads=[accb, c_eyef], writes=[pb[0]])
                P.add("vector", lambda e: e.tensor_copy(out=tr1.t[:], in_=pb[0].t[0:1, 0:128]),
                      reads=[pb[0]], writes=[tr1])
                P.add("vector", lambda e: e.reduce_sum(out=tr1.t[0:1, 0:1], in_=tr1.t[:], axis=AX.X),
                      reads=[tr1], writes=[tr1])
                P.add("vector", lambda e, rows=rows, cols=cols: e.tensor_scalar(
                    out=tr1.t[0:1, 1:2], in0=tr1.t[0:1, 0:1],
                    scalar1=1.0 / (rows * cols), scalar2=1e-5, op0=ALU.mult, op1=ALU.max),
                    reads=[tr1], writes=[tr1])
                P.add("vector", lambda e: e.reciprocal(out=tr1.t[0:1, 2:3], in_=tr1.t[0:1, 1:2]),
                      reads=[tr1], writes=[tr1])

                P.add("tensor", lambda e, out_col=out_col: e.matmul(
                    pT.t[:, 0:1] if False else pb[6].t[:, out_col:out_col + 1],
                    c_onesrf.t[:], tr1.t[0:1, 2:3],
                    start=True, stop=True), reads=[tr1, c_onesrf], writes=[pb[6]])
                P.add("vector", lambda e, out_col=out_col: e.tensor_copy(
                    out=swsc.t[:, out_col:out_col + 1],
                    in_=pb[6].t[:, out_col:out_col + 1]),
                    reads=[pb[6]], writes=[swsc])

            wmean("w1", DH, D, 0)
            wmean("w2", D, DH, 1)
            P.add("vector", lambda e: e.reciprocal(out=swsc.t[:, 2:3], in_=swsc.t[:, 0:1]),
                  reads=[swsc], writes=[swsc])
            P.add("vector", lambda e: e.reciprocal(out=swsc.t[:, 3:4], in_=swsc.t[:, 1:2]),
                  reads=[swsc], writes=[swsc])

            def ternarize(nm, dst_sc, dstB, rows, cols, scol):
                nt, nf = rows // 128, cols // 128
                for i in range(nt):
                    def ld(e, i=i):
                        return [e.dma_start(out=stg.t[:, 0:cols],
                                            in_=ws_in[nm][i * 128:(i + 1) * 128, :])]
                    P.add("sync", dmas(ld, 1), dma="ld", writes=[stg])
                    P.add("vector", lambda e, cols=cols, scol=scol: e.tensor_scalar(
                        out=stg.t[:, 0:cols], in0=stg.t[:, 0:cols],
                        scalar1=swsc.t[:, scol:scol + 1], scalar2=None, op0=ALU.mult),
                        reads=[stg, swsc], writes=[stg])
                    P.add("vector", lambda e, cols=cols: e.tensor_copy(out=iv0.t[:, 0:cols],
                                                                      in_=stg.t[:, 0:cols]),
                          reads=[stg], writes=[iv0])
                    P.add("vector", lambda e, cols=cols: e.tensor_scalar(
                        out=tq.t[:, 0:cols], in0=iv0.t[:, 0:cols],
                        scalar1=1.0, scalar2=-1.0, op0=ALU.min, op1=ALU.max),
                        reads=[iv0], writes=[tq])
                    for fj in range(nf):
                        sl = fj % 8
                        P.add("tensor", lambda e, fj=fj, sl=sl: e.transpose(
                            pT.t[:, sl * 128:(sl + 1) * 128],
                            tq.t[:, fj * 128:(fj + 1) * 128], c_eye.t[:]),
                            reads=[tq, c_eye], writes=pTB)
                        P.add("scalar", lambda e, fj=fj, sl=sl: e.copy(
                            out=tqt.t[:, sl * 128:(sl + 1) * 128],
                            in_=pT.t[:, sl * 128:(sl + 1) * 128]),
                            reads=pTB, writes=[tqt])
                        if sl == 7 or fj == nf - 1:
                            n = sl + 1
                            def stf(e, i=i, fj=fj, n=n):
                                return [e.dma_start(
                                    out=dst_sc[i * 128:(i + 1) * 128,
                                               (fj - n + 1) * 128:(fj + 1) * 128],
                                    in_=tqt.t[:, 0:n * 128])]
                            P.add("gpsimd", dmas(stf, 1), dma="st", n_dma=1, reads=[tqt],
                                  writes=[dstB])

            ternarize("w1", w1q_sc, w1qB, DH, D, 0)
            ternarize("w2", w2q_sc, w2qB, D, DH, 1)
        P.wait_all("vector")
        P.wait_all("scalar")
        P.wait_all("tensor")
        P.wait_all("sync")
        P.wait_all("gpsimd")

        AS = contextlib.ExitStack()

        def sbA(name, p, f, dt):
            return Buf(AS.enter_context(nc.sbuf_tensor(name, [p, f], dt)), name)

        xa = [sbA(f"xa{i}", 128, NTA * D, F32) for i in range(1)]
        lnT = sbA("lnT", 128, NTA * D, BF16)
        lnF = sbA("lnF", 128, 8 * WA, BF16)
        tfq = sbA("tfq", 128, D, F32)
        tfk = sbA("tfk", 128, D, F32)
        tfs = sbA("tfs", 128, D, F32)
        tfc = sbA("tfc", 128, D, F32)
        tfB = sbA("tfB", 128, D, F32)
        tfBi = sbA("tfBi", 128, D, F32)
        tfe = sbA("tfe", 128, D, F32)
        tfkn = sbA("tfkn", 128, D, F32)
        tf1 = sbA("tf1", 128, D, F32)
        qtm = sbA("qtm", 128, D, BF16)
        bhb = sbA("bhb", 128, D, BF16)
        rsq = sbA("rsq", 128, H, F32)
        gamT = sbA("gamT", 128, NCH * H, F32)
        zf1 = sbA("zf1", 128, WA, BF16)
        zg1 = sbA("zg1", 128, WA, BF16)
        tmVC = sbA("tmVC", 128, NCH * H * 128, BF16)
        tmW = sbA("tmW", 128, NCH * H * HD, BF16)
        hdq = sbA("hdq", 128, HPG * NCH * 128, BF16)
        hdw = sbA("hdw", 128, HPG * NCH * 128, BF16)
        hdc = sbA("hdc", 128, HPG * NCH * 128, BF16)
        hdb = sbA("hdb", 128, HPG * NCH * 128, BF16)
        hdq2 = sbA("hdq2", 128, HPG * NCH * 128, BF16)
        hdw2 = sbA("hdw2", 128, HPG * NCH * 128, BF16)
        hdc2 = sbA("hdc2", 128, HPG * NCH * 128, BF16)
        bCt = sbA("bCt", 128, H * NCH, BF16)
        sgog = sbA("sgog", 128, 8 * WA, BF16)
        sL = [sbA(f"sL_{pp}", 128, 2 * NCH * C, BF16) for pp in range(2)]
        sQl = [[sbA(f"sQl{i}_{pp}", 128, 2 * NCH * C, BF16) for i in range(6)]
               for pp in range(2)]
        sP = [[sbA(f"sP{i}_{pp}", 128, 2 * NCH * C, BF16) for i in range(2)]
              for pp in range(2)]
        sAq = [sbA(f"sAq_{pp}", 128, 2 * NCH * C, BF16) for pp in range(2)]
        sX = [[sbA(f"sX{i}_{pp}", 128, 2 * NCH * 128, BF16) for i in range(2)]
              for pp in range(2)]
        sXp = [sbA(f"sXp_{pp}", 128, 2 * NCH * HD, BF16) for pp in range(2)]
        sQE = [sbA(f"sQE_{pp}", 128, 2 * NCH * HD, BF16) for pp in range(2)]
        oStg = [sbA(f"oStg_{pp}", 128, NCH * 2 * HD, BF16) for pp in range(2)]
        oFM = sbA("oFM", 128, HPG * NCH * 128, BF16)
        oTM = lnT
        og2p = sbA("og2p", 128, WA, BF16)
        mrow = sbA("mrow", 1, 4 * WA, F32)
        onrm = sbA("onrm", 128, 8 * WA, BF16)
        aoF = sbA("aoF", 128, 8 * WA, BF16)
        dTM = sbA("dTM", 128, NTA * D, BF16)
        x1T = xa[0]

        ssB = [Buf(s_state.t, f"ss{h_}") for h_ in range(HPG)]

        def col_hd(hp, ch, b):
            return (hp * NCH + ch) * 128 + b * 64

        def ln_stats(xbuf, ti, center_dst=None):
            xt = lambda: xbuf.t[:, ti * D:(ti + 1) * D]
            P.add("vector", lambda e: e.reduce_sum(out=mu1.t[:, 0:1], in_=xt(), axis=AX.X),
                  reads=[xbuf], writes=[mu1])
            P.add("scalar", lambda e: e.activation(out=x2f.t[:], in_=xt(), func=AF.Square),
                  reads=[xbuf], writes=[x2f])
            P.add("vector", lambda e: e.reduce_sum(out=mu1.t[:, 1:2], in_=x2f.t[:], axis=AX.X),
                  reads=[x2f], writes=[mu1])
            P.add("vector", lambda e: e.tensor_scalar(out=mu1.t[:, 2:3], in0=mu1.t[:, 0:1],
                                                      scalar1=1.0 / D, scalar2=None, op0=ALU.mult),
                  reads=[mu1], writes=[mu1])
            P.add("vector", lambda e: e.tensor_tensor(out=mu1.t[:, 3:4], in0=mu1.t[:, 2:3],
                                                      in1=mu1.t[:, 2:3], op=ALU.mult),
                  reads=[mu1], writes=[mu1])
            P.add("vector", lambda e: e.tensor_scalar(out=mu1.t[:, 4:5], in0=mu1.t[:, 1:2],
                                                      scalar1=1.0 / D, scalar2=None, op0=ALU.mult),
                  reads=[mu1], writes=[mu1])
            P.add("vector", lambda e: e.tensor_tensor(out=mu1.t[:, 4:5], in0=mu1.t[:, 4:5],
                                                      in1=mu1.t[:, 3:4], op=ALU.subtract),
                  reads=[mu1], writes=[mu1])
            P.add("scalar", lambda e: e.activation(out=mu1.t[:, 5:6], in_=mu1.t[:, 4:5],
                                                   func=AF.Sqrt, bias=cflt.t[:, 0:1]),
                  reads=[mu1], writes=[mu1])
            P.add("vector", lambda e: e.reciprocal(out=mu1.t[:, 6:7], in_=mu1.t[:, 5:6]),
                  reads=[mu1], writes=[mu1])
            P.add("vector", lambda e: e.tensor_tensor(out=mu1.t[:, 7:8], in0=mu1.t[:, 2:3],
                                                      in1=mu1.t[:, 6:7], op=ALU.mult),
                  reads=[mu1], writes=[mu1])
            P.add("vector", lambda e: e.tensor_scalar(out=mu1.t[:, 7:8], in0=mu1.t[:, 7:8],
                                                      scalar1=-1.0, scalar2=None, op0=ALU.mult),
                  reads=[mu1], writes=[mu1])

        def tm_to_fm(srcbuf, src_off, dstbuf, W, ti):
            for k in range(8):
                sl = k % 8
                P.add("tensor", lambda e, k=k, sl=sl: e.transpose(
                    pT.t[:, sl * 128:(sl + 1) * 128],
                    srcbuf.t[:, src_off + k * 128:src_off + (k + 1) * 128], c_eye.t[:]),
                    reads=[srcbuf, c_eye], writes=pTB)
            def ev(e):
                return e.tensor_copy(
                    out=dstbuf.t[:].rearrange("p (k c) -> p k c", k=8)[:, :, ti * 128:(ti + 1) * 128],
                    in_=pT.t[:].rearrange("p (a c) -> p a c", a=8))
            P.add("vector", ev, reads=pTB, writes=[dstbuf])

        def fm_ln(srcbuf, dstbuf, W, eps=1e-5):
            def mmsum(e):
                last = None
                for k in range(8):
                    last = e.matmul(pb[5].t[0:1, 0:W], c_ones.t[:],
                                    srcbuf.t[:, k * W:(k + 1) * W],
                                    start=(k == 0), stop=(k == 7))
                return last
            P.add("tensor", mmsum, reads=[srcbuf, c_ones], writes=[pb[5]])
            for k in range(8):
                P.add("scalar", lambda e, k=k: e.activation(out=og2p.t[:, 0:W],
                                                            in_=srcbuf.t[:, k * W:(k + 1) * W],
                                                            func=AF.Square),
                      reads=[srcbuf], writes=[og2p])
                def mm2(e, k=k):
                    return e.matmul(pb[5].t[64:65, 0:W], c_ones.t[:], og2p.t[:, 0:W],
                                    start=(k == 0), stop=(k == 7))
                P.add("tensor", mm2, reads=[og2p, c_ones], writes=[pb[5]])
            P.add("vector", lambda e: e.tensor_scalar(out=mrow.t[0:1, 0:W], in0=pb[5].t[0:1, 0:W],
                                                      scalar1=1.0 / D, scalar2=None, op0=ALU.mult),
                  reads=[pb[5]], writes=[mrow])
            P.add("vector", lambda e: e.tensor_scalar(out=mrow.t[0:1, W:2 * W], in0=pb[5].t[64:65, 0:W],
                                                      scalar1=1.0 / D, scalar2=None, op0=ALU.mult),
                  reads=[pb[5]], writes=[mrow])
            P.add("vector", lambda e: e.tensor_tensor(out=mrow.t[0:1, 2 * W:3 * W],
                                                      in0=mrow.t[0:1, 0:W], in1=mrow.t[0:1, 0:W],
                                                      op=ALU.mult),
                  reads=[mrow], writes=[mrow])
            P.add("vector", lambda e: e.tensor_tensor(out=mrow.t[0:1, W:2 * W],
                                                      in0=mrow.t[0:1, W:2 * W],
                                                      in1=mrow.t[0:1, 2 * W:3 * W], op=ALU.subtract),
                  reads=[mrow], writes=[mrow])
            P.add("scalar", lambda e: e.activation(out=mrow.t[0:1, W:2 * W],
                                                   in_=mrow.t[0:1, W:2 * W], func=AF.Sqrt, bias=cflt.t[0:1, 0:1]),
                  reads=[mrow], writes=[mrow])
            P.add("vector", lambda e: e.reciprocal(out=mrow.t[0:1, W:2 * W],
                                                   in_=mrow.t[0:1, W:2 * W]),
                  reads=[mrow], writes=[mrow])
            def mmbc(e):
                e.matmul(pb[6].t[:, 0:W], c_onesrf.t[:], mrow.t[0:1, 0:W],
                         start=True, stop=True)
                return e.matmul(pb[6].t[:, W:2 * W], c_onesrf.t[:], mrow.t[0:1, W:2 * W],
                                start=True, stop=True)
            P.add("tensor", mmbc, reads=[mrow, c_onesrf], writes=[pb[6]])
            P.add("vector", lambda e: e.tensor_tensor(
                out=dstbuf.t[:].rearrange("p (k c) -> p k c", k=8),
                in0=srcbuf.t[:].rearrange("p (k c) -> p k c", k=8),
                in1=pb[6].t[:, 0:W][:, None, :].broadcast_to((128, 8, W)), op=ALU.subtract),
                reads=[srcbuf, pb[6]], writes=[dstbuf])
            P.add("vector", lambda e: e.tensor_tensor(
                out=dstbuf.t[:].rearrange("p (k c) -> p k c", k=8),
                in0=dstbuf.t[:].rearrange("p (k c) -> p k c", k=8),
                in1=pb[6].t[:, W:2 * W][:, None, :].broadcast_to((128, 8, W)), op=ALU.mult),
                reads=[dstbuf, pb[6]], writes=[dstbuf])

        def emit_recurrence():
            Wp = 2 * NCH * C
            HC = HPG * NCH * 128

            def mkswap(src, dst):
                def sw(e):
                    return [e.dma_start(out=dst.t[0:64, 0:HC], in_=src.t[64:128, 0:HC]),
                            e.dma_start(out=dst.t[64:128, 0:HC], in_=src.t[0:64, 0:HC])]
                P.add("gpsimd", dmas(sw, 2), dma="st", n_dma=2, reads=[src], writes=[dst])
            mkswap(hdq, hdq2)
            mkswap(hdw, hdw2)
            mkswap(hdc, hdc2)
            def bext(e):
                ds = []
                for s2 in range(2):
                    for b in range(2):
                        src = hdb.t[s2 * 64:(s2 + 1) * 64, :].rearrange(
                            "p (hp ch bb t) -> p hp ch bb t", hp=HPG, ch=NCH, bb=2)[
                            :, :, :, b, 63]
                        dst = bCt.t[b * 64:(b + 1) * 64,
                                    s2 * (HPG * NCH):(s2 + 1) * (HPG * NCH)].rearrange(
                            "p (hp ch) -> p hp ch", ch=NCH)
                        ds.append(e.dma_start(out=dst, in_=src,
                                              allow_slow_non_contiguous=True))
                return ds
            P.add("gpsimd", dmas(bext, 4), dma="st", n_dma=4, reads=[hdb], writes=[bCt])

            def hsel(base, swapped, b, s2):
                return base if (s2 == b) else swapped

            BK = [(pb[0], pb[1], pb[2]), (pb[3], pb[4], pb[5])]

            def st_mmx(hp):
                pp = hp % 2
                bkA, bkB, bkC = BK[pp]
                for ch in range(NCH):
                    for s2 in range(2):
                        uc = (ch * 2 + s2) * C
                        for (bank, lh, lh2, rh, rh2) in (
                                (bkA, hdc, hdc2, hdw, hdw2),
                                (bkB, hdw, hdw2, hdc, hdc2),
                                (bkC, hdw, hdw2, hdq, hdq2)):
                            def mmx(e, bank=bank, lh=lh, lh2=lh2, rh=rh, rh2=rh2,
                                    s2=s2, ch=ch, uc=uc, hp=hp):
                                last = None
                                for b in range(2):
                                    r0 = b * 64
                                    cb = col_hd(hp, ch, b)
                                    lt = hsel(lh, lh2, b, s2)
                                    rt = hsel(rh, rh2, b, s2)
                                    last = e.matmul(bank.t[r0:r0 + 64, uc:uc + C],
                                                    lt.t[r0:r0 + 64, cb:cb + 64],
                                                    rt.t[r0:r0 + 64, cb:cb + 64],
                                                    start=True, stop=True,
                                                    tile_position=(r0, r0))
                                return last
                            P.add("tensor", mmx, reads=[lh, lh2, rh, rh2], writes=[bank])

            def st_mask(hp):
                pp = hp % 2
                bkA, bkB, bkC = BK[pp]
                for (dst, bank, msk) in ((sL[pp], bkA, c_mlow), (sQl[pp][0], bkB, c_mupp),
                                         (sAq[pp], bkC, c_mud)):
                    P.add("vector", lambda e, dst=dst, bank=bank, msk=msk: e.tensor_tensor(
                        out=dst.t[:].rearrange("p (u c) -> p u c", c=C),
                        in0=bank.t[:, 0:Wp].rearrange("p (u c) -> p u c", c=C),
                        in1=msk.t[:][:, None, :].broadcast_to((128, Wp // C, C)), op=ALU.mult),
                        reads=[bank, msk], writes=[dst])

            def st_power(hp, lvl):
                pp = hp % 2
                bkA, bkB, bkC = BK[pp]
                curP = sL[pp] if lvl == 1 else sP[pp][(lvl - 1) % 2]
                curQ, dstQ, dstP = sQl[pp][lvl - 1], sQl[pp][lvl], sP[pp][lvl % 2]
                for u in range(2 * NCH):
                    uc = u * C
                    if lvl < 5:
                        def mmp(e, uc=uc, curP=curP, curQ=curQ, bkA=bkA):
                            e.matmul(bkA.t[0:64, uc:uc + C], curQ.t[0:64, uc:uc + C],
                                     curP.t[0:64, uc:uc + C], start=True, stop=True,
                                     tile_position=(0, 0))
                            return e.matmul(bkA.t[64:128, uc:uc + C],
                                            curQ.t[64:128, uc:uc + C],
                                            curP.t[64:128, uc:uc + C], start=True, stop=True,
                                            tile_position=(64, 64))
                        P.add("tensor", mmp, reads=[curP, curQ], writes=[bkA])
                    def mmq(e, uc=uc, curP=curP, curQ=curQ, bkB=bkB):
                        e.matmul(bkB.t[0:64, uc:uc + C], curP.t[0:64, uc:uc + C],
                                 curQ.t[0:64, uc:uc + C], start=True, stop=True,
                                 tile_position=(0, 0))
                        return e.matmul(bkB.t[64:128, uc:uc + C], curP.t[64:128, uc:uc + C],
                                        curQ.t[64:128, uc:uc + C], start=True, stop=True,
                                        tile_position=(64, 64))
                    P.add("tensor", mmq, reads=[curP, curQ], writes=[bkB])
                if lvl < 5:
                    P.add("scalar", lambda e, dstP=dstP, bkA=bkA: e.copy(
                        out=dstP.t[:, 0:Wp], in_=bkA.t[:, 0:Wp]),
                        reads=[bkA], writes=[dstP])
                P.add("vector", lambda e, dstQ=dstQ, bkB=bkB: e.tensor_copy(
                    out=dstQ.t[:, 0:Wp], in_=bkB.t[:, 0:Wp]),
                    reads=[bkB], writes=[dstQ])

            def st_apply(hp, gen):
                pp = hp % 2
                bkA, bkB, bkC = BK[pp]
                k = 5 - gen
                qk = sQl[pp][k]
                dstX = sX[pp][gen % 2]
                prevX = tmVC if gen == 0 else sX[pp][(gen - 1) % 2]
                for ch in range(NCH):
                    for s2 in range(2):
                        h = 2 * hp + s2
                        uc = (ch * 2 + s2) * C
                        xc = (ch * 2 + s2) * 128
                        def rap(gen=gen, ch=ch, h=h, xc=xc, pp=pp):
                            if gen == 0:
                                return tmVC.t[:, (ch * H + h) * 128:(ch * H + h + 1) * 128]
                            return sX[pp][(gen - 1) % 2].t[:, xc:xc + 128]
                        def mma(e, qk=qk, uc=uc, xc=xc, rap=rap, bkC=bkC):
                            e.matmul(bkC.t[0:64, xc:xc + 128], qk.t[0:64, uc:uc + C],
                                     rap()[0:64, :], start=True, stop=True,
                                     tile_position=(0, 0))
                            return e.matmul(bkC.t[64:128, xc:xc + 128],
                                            qk.t[64:128, uc:uc + C], rap()[64:128, :],
                                            start=True, stop=True, tile_position=(64, 64))
                        P.add("tensor", mma, reads=[qk, prevX], writes=[bkC])
                def evx(e, gen=gen, dstX=dstX, hp=hp, pp=pp, bkC=bkC):
                    if gen == 0:
                        src0 = tmVC.t[:].rearrange("p (ch h c) -> p ch h c", ch=NCH, h=H)[
                            :, :, 2 * hp:2 * hp + 2, :]
                    else:
                        src0 = sX[pp][(gen - 1) % 2].t[:, 0:2 * NCH * 128].rearrange(
                            "p (ch h c) -> p ch h c", ch=NCH, h=2)
                    return e.tensor_tensor(
                        out=dstX.t[:, 0:2 * NCH * 128].rearrange(
                            "p (ch h c) -> p ch h c", ch=NCH, h=2),
                        in0=src0,
                        in1=bkC.t[:, 0:2 * NCH * 128].rearrange(
                            "p (ch h c) -> p ch h c", ch=NCH, h=2),
                        op=ALU.add)
                P.add("vector", evx, reads=[prevX, bkC], writes=[dstX])

            def st_xprod(hp):
                pp = hp % 2
                bkA, bkB, bkC = BK[pp]
                xfin = sX[pp][1]
                for ch in range(NCH):
                    for s2 in range(2):
                        h = 2 * hp + s2
                        xc = (ch * 2 + s2) * 128
                        wc = (ch * H + h) * HD
                        oc = (ch * 2 + s2) * HD
                        def mmxp(e, xc=xc, wc=wc, oc=oc, xfin=xfin, bkA=bkA):
                            e.matmul(bkA.t[0:64, oc:oc + HD],
                                     xfin.t[0:64, xc + 64:xc + 128], tmW.t[0:64, wc:wc + HD],
                                     start=True, stop=True, tile_position=(0, 0))
                            return e.matmul(bkA.t[64:128, oc:oc + HD],
                                            xfin.t[64:128, xc + 64:xc + 128],
                                            tmW.t[64:128, wc:wc + HD],
                                            start=True, stop=True, tile_position=(64, 64))
                        P.add("tensor", mmxp, reads=[xfin, tmW], writes=[bkA])
                P.add("vector", lambda e, pp=pp, bkA=bkA: e.tensor_tensor(
                    out=sXp[pp].t[:].rearrange("p (u c) -> p u c", c=C),
                    in0=bkA.t[:, 0:2 * NCH * HD].rearrange("p (u c) -> p u c", c=C),
                    in1=c_ipair.t[:][:, None, :].broadcast_to((128, 2 * NCH, C)), op=ALU.add),
                    reads=[bkA, c_ipair], writes=[sXp[pp]])

            def st_state(hp, ch):
                pp = hp % 2
                bkA, bkB, bkC = BK[pp]
                xfin = sX[pp][1]
                ss = ssB[hp]
                for s2 in range(2):
                    h = 2 * hp + s2
                    uc = (ch * 2 + s2) * C
                    xc = (ch * 2 + s2) * 128
                    oc = (ch * 2 + s2) * HD
                    def mmE(e, uc=uc, xc=xc, oc=oc, xfin=xfin, bkA=bkA, pp=pp):
                        last = None
                        for b in range(2):
                            r0 = b * 64
                            last = e.matmul(bkA.t[r0:r0 + 64, oc:oc + HD],
                                            xfin.t[r0:r0 + 64, xc + 64:xc + 128],
                                            sAq[pp].t[r0:r0 + 64, uc:uc + C],
                                            start=True, stop=True, tile_position=(r0, r0))
                        return last
                    P.add("tensor", mmE, reads=[xfin, sAq[pp]], writes=[bkA])
                    for b in range(2):
                        r0 = b * 64
                        cb = col_hd(hp, ch, b)
                        qsrc = hsel(hdq, hdq2, b, s2)
                        def evqe(e, r0=r0, cb=cb, oc=oc, qsrc=qsrc, pp=pp, bkA=bkA):
                            return e.tensor_tensor(
                                out=sQE[pp].t[r0:r0 + 64, oc:oc + HD],
                                in0=qsrc.t[r0:r0 + 64, cb:cb + 64],
                                in1=bkA.t[r0:r0 + 64, oc:oc + HD], op=ALU.add)
                        P.add("vector", evqe, reads=[qsrc, bkA], writes=[sQE[pp]])
                    def mmF(e, uc=uc, xc=xc, oc=oc, xfin=xfin, bkB=bkB, pp=pp):
                        last = None
                        for b in range(2):
                            r0 = b * 64
                            last = e.matmul(bkB.t[r0:r0 + 64, oc:oc + HD],
                                            xfin.t[r0:r0 + 64, xc:xc + 64],
                                            sAq[pp].t[r0:r0 + 64, uc:uc + C],
                                            start=True, stop=False, tile_position=(r0, r0))
                        return last
                    P.add("tensor", mmF, reads=[xfin, sAq[pp]], writes=[bkB])
                    def mmO2(e, oc=oc, h=h, ss=ss, bkB=bkB, pp=pp):
                        last = None
                        for b in range(2):
                            r0 = b * 64
                            last = e.matmul(bkB.t[r0:r0 + 64, oc:oc + HD],
                                            s_state.t[r0:r0 + 64, h * HD:(h + 1) * HD],
                                            sQE[pp].t[r0:r0 + 64, oc:oc + HD],
                                            start=False, stop=True, tile_position=(r0, r0))
                        return last
                    P.add("tensor", mmO2, reads=[ss, sQE[pp]], writes=[bkB])
                    def mmQb(e, ch=ch, h=h, xc=xc, oc=oc, xfin=xfin, bkC=bkC):
                        last = None
                        for b in range(2):
                            r0 = b * 64
                            last = e.matmul(bkC.t[r0:r0 + 64, oc:oc + HD],
                                            tmW.t[r0:r0 + 64, (ch * H + h) * HD:
                                                  (ch * H + h + 1) * HD],
                                            xfin.t[r0:r0 + 64, xc:xc + 64],
                                            start=True, stop=False, tile_position=(r0, r0))
                        return last
                    P.add("tensor", mmQb, reads=[tmW, xfin], writes=[bkC])
                    def mmZ(e, oc=oc, h=h, ss=ss, bkC=bkC, pp=pp):
                        last = None
                        for b in range(2):
                            r0 = b * 64
                            last = e.matmul(bkC.t[r0:r0 + 64, oc:oc + HD],
                                            sXp[pp].t[r0:r0 + 64, oc:oc + HD],
                                            s_state.t[r0:r0 + 64, h * HD:(h + 1) * HD],
                                            start=False, stop=True, tile_position=(r0, r0))
                        return last
                    P.add("tensor", mmZ, reads=[sXp[pp], ss], writes=[bkC])
                def evo(e, ch=ch, pp=pp, bkB=bkB):
                    return e.tensor_copy(out=oStg[pp].t[:, ch * 128:(ch + 1) * 128],
                                         in_=bkB.t[:, ch * 128:(ch + 1) * 128])
                P.add("vector", evo, reads=[bkB], writes=[oStg[pp]])
                def evs(e, ch=ch, hp=hp, bkC=bkC):
                    return e.tensor_tensor(
                        out=s_state.t[:, 2 * hp * HD:(2 * hp + 2) * HD].rearrange(
                            "p (s d) -> p s d", s=2),
                        in0=bkC.t[:, ch * 128:(ch + 1) * 128].rearrange(
                            "p (s d) -> p s d", s=2),
                        in1=bCt.t[:].rearrange("p (s hp c) -> p s hp c", s=2, c=NCH)[
                            :, :, hp, ch][:, :, None].broadcast_to((128, 2, HD)),
                        op=ALU.mult)
                P.add("vector", evs, reads=[bkC, bCt], writes=[ss])
                for s2 in range(2):
                    for b in range(2):
                        r0 = b * 64
                        def mto(e, r0=r0, ch=ch, s2=s2, hp=hp, pp=pp):
                            return e.transpose(
                                pT.t[r0:r0 + 64, (2 * hp + s2) * 64:(2 * hp + s2 + 1) * 64],
                                oStg[pp].t[r0:r0 + 64,
                                           (ch * 2 + s2) * HD:(ch * 2 + s2 + 1) * HD],
                                c_eye.t[r0:r0 + 64, r0:r0 + 64])
                        P.add("tensor", mto, reads=[oStg[pp], c_eye], writes=[pT_p[pp]])
                def evoT(e, ch=ch, hp=hp, pp=pp):
                    return e.tensor_copy(
                        out=oTM.t[:, ch * D + 2 * hp * HD:ch * D + (2 * hp + 2) * HD],
                        in_=pT.t[:, (2 * hp) * 64:(2 * hp + 2) * 64])
                P.add("vector", evoT, reads=[pT_p[pp]], writes=[oTM])

            stages = ([st_mmx, st_mask]
                      + [lambda hp, lvl=lvl: st_power(hp, lvl) for lvl in range(1, 6)]
                      + [lambda hp, gen=gen: st_apply(hp, gen) for gen in range(6)]
                      + [st_xprod]
                      + [lambda hp, ch=ch: st_state(hp, ch) for ch in range(NCH)])
            import os as _os
            if _os.environ.get("REC_SERIAL", "0") == "1":
                for hp_ in range(HPG):
                    for stg in stages:
                        stg(hp_)
            else:
                for pair in range(HPG // 2):
                    for stg in stages:
                        stg(2 * pair)
                        stg(2 * pair + 1)
            for m in range(NTA):
                tm_to_fm(oTM, m * D, oFM, WA, m)
        P.add("vector", lambda e: e.memset(s_state.t[:], 0.0), writes=ssB)
        P.add("vector", lambda e: e.memset(zf1.t[64:128, :], 0.0), writes=[zf1])
        P.add("vector", lambda e: e.memset(zg1.t[64:128, :], 0.0), writes=[zg1])

        def a_supertile(st):
            xbuf = xa[0]
            rb = st * ST_A

            def ldx(e):
                ds = []
                for b in range(NB):
                    r0 = b * T + rb
                    ds.append(e.dma_start(
                        out=xbuf.t[b * 64:(b + 1) * 64, :].rearrange("p (t d) -> p t d", t=NTA),
                        in_=x_in[r0:r0 + NTA * 64, :].rearrange("(t p) d -> p t d", p=64)))
                return ds
            P.add("sync", dmas(ldx, NB), dma="ld", n_dma=NB, writes=[xbuf])

            for ti in range(NTA):
                ln_stats(xbuf, ti)
                P.add("vector", lambda e, ti=ti: e.tensor_scalar(
                    out=lnT.t[:, ti * D:(ti + 1) * D], in0=xbuf.t[:, ti * D:(ti + 1) * D],
                    scalar1=mu1.t[:, 6:7], scalar2=mu1.t[:, 7:8],
                    op0=ALU.mult, op1=ALU.add),
                    reads=[xbuf, mu1], writes=[lnT])
                tm_to_fm(lnT, ti * D, lnF, WA, ti)

            for (wb, dst) in ((w_f1, zf1), (w_og1, zg1)):
                def mmz(e, wb=wb):
                    last = None
                    for k in range(8):
                        last = e.matmul(pb[2].t[0:64, 0:WA], wb.t[:, k * HD:(k + 1) * HD],
                                        lnF.t[:, k * WA:(k + 1) * WA],
                                        start=(k == 0), stop=(k == 7))
                    return last
                P.add("tensor", mmz, reads=[lnF, wb], writes=[pb[2]])
                P.add("vector", lambda e, dst=dst: e.tensor_copy(out=dst.t[0:64, 0:WA],
                                                                 in_=pb[2].t[0:64, 0:WA]),
                      reads=[pb[2]], writes=[dst])

            for mo in range(8):
                bank = pb[mo % 2]
                def mmo(e, mo=mo, bank=bank):
                    return e.matmul(bank.t[:, 0:WA], w_og2.t[:, mo * 128:(mo + 1) * 128],
                                    zg1.t[:, 0:WA], start=True, stop=True)
                P.add("tensor", mmo, reads=[zg1, w_og2], writes=[bank])
                P.add("scalar", lambda e, mo=mo, bank=bank: e.activation(
                    out=sgog.t[:, mo * WA:(mo + 1) * WA], in_=bank.t[:, 0:WA], func=AF.Sigmoid),
                    reads=[bank], writes=[sgog])

            for m in range(NTA):
                ch = m
                for (wb, dst, act) in ((w_q, tfq, AF.Silu), (w_k, tfk, AF.Silu)):
                    for nn in range(2):
                        bank = pb[nn]
                        def mmp_(e, m=m, nn=nn, wb=wb, bank=bank):
                            last = None
                            for k in range(8):
                                last = e.matmul(bank.t[:, 0:512],
                                                lnF.t[:, k * WA + m * 128:k * WA + (m + 1) * 128],
                                                wb.t[:, k * D + nn * 512:k * D + (nn + 1) * 512],
                                                start=(k == 0), stop=(k == 7))
                            return last
                        P.add("tensor", mmp_, reads=[lnF, wb], writes=[bank])
                        P.add("scalar", lambda e, nn=nn, bank=bank, dst=dst, act=act: e.activation(
                            out=dst.t[:, nn * 512:(nn + 1) * 512], in_=bank.t[:, 0:512], func=act),
                            reads=[bank], writes=[dst])
                for nn in range(2):
                    bank = pb[nn]
                    def mmv(e, m=m, nn=nn, bank=bank):
                        last = None
                        for k in range(8):
                            last = e.matmul(bank.t[:, 0:512],
                                            lnF.t[:, k * WA + m * 128:k * WA + (m + 1) * 128],
                                            w_v.t[:, k * D + nn * 512:k * D + (nn + 1) * 512],
                                            start=(k == 0), stop=(k == 7))
                        return last
                    P.add("tensor", mmv, reads=[lnF, w_v], writes=[bank])
                    def evv(e, m=m, nn=nn, bank=bank):
                        return e.tensor_copy(
                            out=tmVC.t[:].rearrange("p (u c) -> p u c", c=128)[
                                :, m * H + nn * 8:m * H + (nn + 1) * 8, 0:64],
                            in_=bank.t[:, 0:512].rearrange("p (h c) -> p h c", c=64))
                    P.add("vector", evv, reads=[bank], writes=[tmVC])
                for nn in range(2):
                    bank = pb[nn]
                    def mmf2(e, m=m, nn=nn, bank=bank):
                        return e.matmul(bank.t[:, 0:512], zf1.t[:, m * 128:(m + 1) * 128],
                                        w_f2.t[:, nn * 512:(nn + 1) * 512],
                                        start=True, stop=True)
                    P.add("tensor", mmf2, reads=[zf1, w_f2], writes=[bank])
                    P.add("scalar", lambda e, nn=nn, bank=bank: e.activation(
                        out=tfe.t[:, nn * 512:(nn + 1) * 512], in_=bank.t[:, 0:512],
                        func=AF.Sigmoid),
                        reads=[bank], writes=[tfe])
                P.add("scalar", lambda e: e.activation(out=tfs.t[:], in_=tfe.t[:], func=AF.Ln),
                      reads=[tfe], writes=[tfs])
                bank = pb[2]
                def mmg(e, m=m, bank=bank):
                    last = None
                    for k in range(8):
                        last = e.matmul(bank.t[:, 0:H],
                                        lnF.t[:, k * WA + m * 128:k * WA + (m + 1) * 128],
                                        w_gam.t[:, k * H:(k + 1) * H],
                                        start=(k == 0), stop=(k == 7))
                    return last
                P.add("tensor", mmg, reads=[lnF, w_gam], writes=[bank])
                P.add("scalar", lambda e, m=m, bank=bank: e.activation(
                    out=gamT.t[:, m * H:(m + 1) * H], in_=bank.t[:, 0:H], func=AF.Sigmoid),
                    reads=[bank], writes=[gamT])
                P.add("vector", lambda e, m=m: e.tensor_scalar(
                    out=gamT.t[:, m * H:(m + 1) * H], in0=gamT.t[:, m * H:(m + 1) * H],
                    scalar1=-1.0, scalar2=None, op0=ALU.mult),
                    reads=[gamT], writes=[gamT])
                P.add("vector", lambda e: e.tensor_copy(out=qtm.t[:], in_=tfs.t[:]),
                      reads=[tfs], writes=[qtm])
                P.add("vector", lambda e: e.tensor_tensor(out=bhb.t[:], in0=tfs.t[:],
                                                          in1=qtm.t[:], op=ALU.subtract),
                      reads=[tfs, qtm], writes=[bhb])
                for nn in range(2):
                    bank = pb[nn]
                    def mmc(e, nn=nn, bank=bank):
                        e.matmul(bank.t[:, 0:512], c_cum.t[:], qtm.t[:, nn * 512:(nn + 1) * 512],
                                 start=True, stop=False)
                        return e.matmul(bank.t[:, 0:512], c_cum.t[:],
                                        bhb.t[:, nn * 512:(nn + 1) * 512], start=False, stop=True)
                    P.add("tensor", mmc, reads=[qtm, bhb, c_cum], writes=[bank])
                    P.add("vector", lambda e, nn=nn, bank=bank: e.tensor_copy(
                        out=tfc.t[:, nn * 512:(nn + 1) * 512], in_=bank.t[:, 0:512]),
                        reads=[bank], writes=[tfc])
                P.add("scalar", lambda e: e.activation(out=tfB.t[:], in_=tfc.t[:], func=AF.Exp),
                      reads=[tfc], writes=[tfB])
                P.add("scalar", lambda e: e.activation(out=tfBi.t[:], in_=tfc.t[:], func=AF.Exp,
                                                       scale=cflt.t[:, 1:2]),
                      reads=[tfc, cflt], writes=[tfBi])
                P.add("scalar", lambda e: e.activation(out=x2f.t[:], in_=tfk.t[:], func=AF.Square),
                      reads=[tfk], writes=[x2f])
                P.add("vector", lambda e: e.reduce_sum(
                    out=rsq.t[:], in_=x2f.t[:].rearrange("p (h d) -> p h d", h=H), axis=AX.X),
                    reads=[x2f], writes=[rsq])
                P.add("scalar", lambda e: e.activation(out=rsq.t[:], in_=rsq.t[:], func=AF.Sqrt),
                      reads=[rsq], writes=[rsq])
                P.add("vector", lambda e: e.tensor_scalar(out=rsq.t[:], in0=rsq.t[:],
                                                          scalar1=1e-12, scalar2=None, op0=ALU.max),
                      reads=[rsq], writes=[rsq])
                P.add("vector", lambda e: e.reciprocal(out=rsq.t[:], in_=rsq.t[:]),
                      reads=[rsq], writes=[rsq])
                P.add("vector", lambda e: e.tensor_tensor(
                    out=tfkn.t[:].rearrange("p (h d) -> p h d", h=H),
                    in0=tfk.t[:].rearrange("p (h d) -> p h d", h=H),
                    in1=rsq.t[:, :, None].broadcast_to((128, H, HD)), op=ALU.mult),
                    reads=[tfk, rsq], writes=[tfkn])
                P.add("vector", lambda e, ch=ch: e.tensor_tensor(
                    out=tmW.t[:].rearrange("p (u d) -> p u d", d=HD)[:, ch * H:(ch + 1) * H, :],
                    in0=tfkn.t[:].rearrange("p (h d) -> p h d", h=H),
                    in1=tfBi.t[:].rearrange("p (h d) -> p h d", h=H), op=ALU.mult),
                    reads=[tfkn, tfBi], writes=[tmW])
                P.add("vector", lambda e: e.tensor_tensor(out=tf1.t[:], in0=tfe.t[:], in1=tfB.t[:],
                                                          op=ALU.mult),
                      reads=[tfe, tfB], writes=[tf1])
                P.add("vector", lambda e: e.tensor_tensor(out=tf1.t[:], in0=tf1.t[:], in1=tfkn.t[:],
                                                          op=ALU.mult),
                      reads=[tf1, tfkn], writes=[tf1])
                P.add("vector", lambda e, ch=ch, m=m: e.tensor_tensor(
                    out=tmVC.t[:].rearrange("p (u c) -> p u c", c=128)[
                        :, ch * H:(ch + 1) * H, 64:128],
                    in0=tf1.t[:].rearrange("p (h d) -> p h d", h=H),
                    in1=gamT.t[:, m * H:(m + 1) * H][:, :, None].broadcast_to((128, H, HD)),
                    op=ALU.mult),
                    reads=[tf1, gamT], writes=[tmVC])
                P.add("vector", lambda e: e.tensor_tensor(out=qtm.t[:], in0=tfq.t[:], in1=tfB.t[:],
                                                          op=ALU.mult),
                      reads=[tfq, tfB], writes=[qtm])
                P.add("vector", lambda e: e.tensor_copy(out=bhb.t[:], in_=tfB.t[:]),
                      reads=[tfB], writes=[bhb])
                for (srcfn, srcbuf, dst) in (
                        (lambda hp: qtm.t[:, hp * 128:(hp + 1) * 128], qtm, hdq),
                        (lambda hp, ch=ch: tmW.t[:, (ch * H + 2 * hp) * HD:
                                                 (ch * H + 2 * hp + 2) * HD], tmW, hdw),
                        (lambda hp: bhb.t[:, hp * 128:(hp + 1) * 128], bhb, hdb)):
                    for hp in range(HPG):
                        sl = hp % 8
                        P.add("tensor", lambda e, srcfn=srcfn, hp=hp, sl=sl: e.transpose(
                            pT.t[:, sl * 128:(sl + 1) * 128], srcfn(hp), c_eye.t[:]),
                            reads=[srcbuf, c_eye], writes=pTB)
                    def evt(e, dst=dst, ch=ch):
                        return e.tensor_copy(
                            out=dst.t[:].rearrange("p (hp ch c) -> p hp ch c", hp=HPG, ch=NCH)[
                                :, :, ch, :],
                            in_=pT.t[:].rearrange("p (a c) -> p a c", a=8))
                    P.add("vector", evt, reads=pTB, writes=[dst])
                for hp in range(HPG):
                    for s2 in range(2):
                        hcm = 2 * hp + s2
                        P.add("tensor", lambda e, hp=hp, s2=s2, hcm=hcm, ch=ch: e.transpose(
                            pT.t[s2 * 64:s2 * 64 + 64, hp * 128:(hp + 1) * 128],
                            tmVC.t[:, (ch * H + hcm) * 128 + 64:(ch * H + hcm) * 128 + 128],
                            c_eye.t[:]),
                            reads=[tmVC, c_eye], writes=pTB)
                def evtc(e, ch=ch):
                    return e.tensor_copy(
                        out=hdc.t[:].rearrange("p (hp ch c) -> p hp ch c", hp=HPG, ch=NCH)[
                            :, :, ch, :],
                        in_=pT.t[:].rearrange("p (a c) -> p a c", a=8))
                P.add("vector", evtc, reads=pTB, writes=[hdc])

            emit_recurrence()

            P.add("vector", lambda e: e.tensor_tensor(out=oFM.t[:], in0=oFM.t[:], in1=sgog.t[:],
                                                      op=ALU.mult),
                  reads=[oFM, sgog], writes=[oFM])
            fm_ln(oFM, onrm, WA)
            def emit_ldo(mo):
                wslot = (mo % 2) * 1024
                def ldo(e, mo=mo, wslot=wslot):
                    return [e.dma_start(out=w_ow.t[:, wslot:wslot + 1024],
                                        in_=woq_sc[mo * 128:(mo + 1) * 128, :])]
                P.add("sync", dmas(ldo, 1), dma="ld", n_dma=1, reads=[woqB],
                      writes=[w_ow_p[mo % 2]])

            emit_ldo(0)
            for mo in range(8):
                if mo + 1 < 8:
                    emit_ldo(mo + 1)
                wslot = (mo % 2) * 1024
                bank = pb[mo % 2]
                def mmwo(e, mo=mo, wslot=wslot, bank=bank):
                    last = None
                    for k in range(8):
                        last = e.matmul(bank.t[:, 0:WA],
                                        w_ow.t[:, wslot + k * 128:wslot + (k + 1) * 128],
                                        onrm.t[:, k * WA:(k + 1) * WA],
                                        start=(k == 0), stop=(k == 7))
                    return last
                P.add("tensor", mmwo, reads=[onrm, w_ow_p[mo % 2]], writes=[bank])
                P.add("scalar", lambda e, mo=mo, bank=bank: e.activation(
                    out=aoF.t[:, mo * WA:(mo + 1) * WA], in_=bank.t[:, 0:WA], func=AF.Copy),
                    reads=[bank], writes=[aoF])
            fm_ln(aoF, onrm, WA)
            for m in range(NTA):
                for k in range(8):
                    sl = k % 8
                    P.add("tensor", lambda e, m=m, k=k, sl=sl: e.transpose(
                        pT.t[:, sl * 128:(sl + 1) * 128],
                        onrm.t[:, k * WA + m * 128:k * WA + (m + 1) * 128], c_eye.t[:]),
                        reads=[onrm, c_eye], writes=pTB)
                def evd(e, m=m):
                    return e.tensor_copy(
                        out=dTM.t[:].rearrange("p (m k c) -> p m k c", m=NTA, k=8)[:, m, :, :],
                        in_=pT.t[:].rearrange("p (a c) -> p a c", a=8))
                P.add("vector", evd, reads=pTB, writes=[dTM])
            P.add("vector", lambda e: e.tensor_tensor(out=x1T.t[:], in0=xbuf.t[:], in1=dTM.t[:],
                                                      op=ALU.add),
                  reads=[dTM], writes=[x1T])

            def stx1(e, rb=rb):
                ds = []
                for b in range(NB):
                    r0 = b * T + rb
                    ds.append(e.dma_start(
                        out=x1_sc[r0:r0 + NTA * 64, :].rearrange("(t p) d -> p t d", p=64),
                        in_=x1T.t[b * 64:(b + 1) * 64, :].rearrange("p (t d) -> p t d", t=NTA)))
                return ds
            P.add("gpsimd", dmas(stx1, NB), dma="st", n_dma=NB, reads=[x1T],
                  writes=[x1B])

        for st in range(NST_A):
            a_supertile(st)

        def tap(name, buf, rows, cols):
            if name in dbg_outs:
                def stp(e):
                    return [e.dma_start(out=dbg_outs[name], in_=buf.t[0:rows, 0:cols])]
                P.add("gpsimd", dmas(stp, 1), dma="st", reads=[buf])

        tap("lnT", lnT, 128, NTA * D)
        tap("lnF", lnF, 128, 8 * WA)
        tap("tfq", tfq, 128, D)
        tap("tfs", tfs, 128, D)
        tap("tfc", tfc, 128, D)
        tap("tmW", tmW, 128, NCH * H * HD)
        tap("tmVC", tmVC, 128, NCH * H * 128)
        tap("hdq", hdq, 128, HPG * NCH * 128)
        tap("hdw", hdw, 128, HPG * NCH * 128)
        tap("hdb", hdb, 128, HPG * NCH * 128)
        tap("oFM", oFM, 128, HPG * NCH * 128)
        tap("sgog", sgog, 128, 8 * WA)
        tap("aoF", aoF, 128, 8 * WA)
        tap("x1T", x1T, 128, NTA * D)
        tap("state", s_state, 128, HPG * 2 * HD)
        tap("sL", sL[1], 128, 2 * NCH * C)
        tap("sQ0", sQl[1][0], 128, 2 * NCH * C)
        tap("sQ5", sQl[1][5], 128, 2 * NCH * C)
        tap("sAq", sAq[1], 128, 2 * NCH * C)
        tap("xfin", sX[1][1], 128, 2 * NCH * 128)
        tap("sXp", sXp[1], 128, 2 * NCH * HD)
        tap("sQE", sQE[1], 128, 2 * NCH * HD)
        tap("oStg", oStg[1], 128, NCH * 2 * HD)
        tap("bCt", bCt, 128, H * NCH)
        tap("onrm", onrm, 128, 8 * WA)
        tap("hdc", hdc, 128, HPG * NCH * 128)

        def tapB(name, buf, rows, cols):
            if name in dbg_outs:
                def stp(e):
                    return [e.dma_start(out=dbg_outs[name], in_=buf.t[0:rows, 0:cols])]
                P.add("gpsimd", dmas(stp, 1), dma="st", reads=[buf])
            return name in dbg_outs

        for _eng in ("vector", "scalar", "tensor", "sync", "gpsimd"):
            P.wait_all(_eng)
        AS.close()
        WS.close()

        x1b = sb("x1b", 128, NTB * D, F32)
        x1s = sb("x1s", 128, NTB * D, F32)
        xqT = sb("xqT", 128, NTB * D, BF16)
        xcT = sb("xcT", 128, D, F32)
        drow = sb("drow", 128, NTB * 2, F32)
        xqF = sb("xqF", 128, 8 * WB, BF16)
        w1w = sb("w1w", 128, 2 * 8 * 128, BF16)
        w1w_p = [w1w, Buf(w1w.t, "w1w_odd")]
        w2w = sb("w2w", 128, 2 * 32 * 128, BF16)
        w2w_p = [w2w, Buf(w2w.t, "w2w_odd")]
        hF = sb("hF", 128, 2 * 2 * HBW, F32)
        hF_p = [hF, Buf(hF.t, "hF_odd")]
        hhalo = sb("hhalo", 128, 32 * 2 * 2, F32)
        hhalo_p = [hhalo, Buf(hhalo.t, "hhalo_odd")]
        ac0 = sb("ac0", 128, WB, F32)
        ac1 = sb("ac1", 128, WB, F32)
        ac0b = sb("ac0b", 128, WB, F32)
        ac1b = sb("ac1b", 128, WB, F32)
        yfb = sb("yfb", 128, WB, F32)
        mxp2 = sb("mxp2", 128, WB, F32)
        hprevF = sb("hprevF", 128, 32 * 2 * 2, F32)
        yf = sb("yf", 128, WB, F32)
        yld = sb("yld", 128, 2 * 512, F32)
        yld_p = [yld, Buf(yld.t, "yld_odd")]
        ivh = sb("ivh", 128, WB, I32)
        ivhb = sb("ivhb", 128, WB, I32)
        hq = sb("hq", 128, 32 * WB, BF16)
        hq_p = [hq, Buf(hq.t, "hq_odd")]
        ymx = sb("ymx", 128, WB, F32)
        mxp = sb("mxp", 128, WB, F32)
        outFf = sb("outFf", 128, 8 * WB, F32)
        yrow = sb("yrow", 1, 4 * WB, F32)
        dscb = sb("dscb", 128, 3 * WB, F32)
        outF = sb("outF", 128, 8 * WB, BF16)
        o2pB = sb("o2pB", 128, WB, BF16)
        mrowB = yrow
        bcA2 = ymx
        bcB2 = xcT
        onB = xqF
        d2TM = xqT
        x2T = x1b

        P.add("vector", lambda e: e.memset(hhalo.t[:], 0.0), writes=[hhalo])

        def fm_ln_b(srcbuf, srcnorm, dstbuf, W):
            def mmsum(e):
                last = None
                for k in range(8):
                    last = e.matmul(pb[5].t[0:1, 0:W], c_ones.t[:],
                                    srcbuf.t[:, k * W:(k + 1) * W], start=(k == 0), stop=(k == 7))
                return last
            P.add("tensor", mmsum, reads=[srcbuf, c_ones], writes=[pb[5]])
            for k in range(8):
                P.add("scalar", lambda e, k=k: e.activation(out=o2pB.t[:, 0:W],
                                                            in_=srcbuf.t[:, k * W:(k + 1) * W],
                                                            func=AF.Square),
                      reads=[srcbuf], writes=[o2pB])
                def mm2(e, k=k):
                    return e.matmul(pb[5].t[64:65, 0:W], c_ones.t[:], o2pB.t[:, 0:W],
                                    start=(k == 0), stop=(k == 7))
                P.add("tensor", mm2, reads=[o2pB, c_ones], writes=[pb[5]])
            P.add("vector", lambda e: e.tensor_scalar(out=mrowB.t[0:1, 0:W], in0=pb[5].t[0:1, 0:W],
                                                      scalar1=1.0 / D, scalar2=None, op0=ALU.mult),
                  reads=[pb[5]], writes=[mrowB])
            P.add("vector", lambda e: e.tensor_scalar(out=mrowB.t[0:1, W:2 * W],
                                                      in0=pb[5].t[64:65, 0:W],
                                                      scalar1=1.0 / D, scalar2=None, op0=ALU.mult),
                  reads=[pb[5]], writes=[mrowB])
            P.add("vector", lambda e: e.tensor_tensor(out=mrowB.t[0:1, 2 * W:3 * W],
                                                      in0=mrowB.t[0:1, 0:W], in1=mrowB.t[0:1, 0:W],
                                                      op=ALU.mult),
                  reads=[mrowB], writes=[mrowB])
            P.add("vector", lambda e: e.tensor_tensor(out=mrowB.t[0:1, W:2 * W],
                                                      in0=mrowB.t[0:1, W:2 * W],
                                                      in1=mrowB.t[0:1, 2 * W:3 * W],
                                                      op=ALU.subtract),
                  reads=[mrowB], writes=[mrowB])
            P.add("scalar", lambda e: e.activation(out=mrowB.t[0:1, W:2 * W],
                                                   in_=mrowB.t[0:1, W:2 * W], func=AF.Sqrt,
                                                   bias=cflt.t[0:1, 0:1]),
                  reads=[mrowB], writes=[mrowB])
            P.add("vector", lambda e: e.reciprocal(out=mrowB.t[0:1, W:2 * W],
                                                   in_=mrowB.t[0:1, W:2 * W]),
                  reads=[mrowB], writes=[mrowB])
            def mmbc(e):
                e.matmul(pb[6].t[:, 0:W], c_onesrf.t[:], mrowB.t[0:1, 0:W],
                         start=True, stop=True)
                return e.matmul(pb[4].t[:, 0:W], c_onesrf.t[:], mrowB.t[0:1, W:2 * W],
                                start=True, stop=True)
            P.add("tensor", mmbc, reads=[mrowB, c_onesrf], writes=[pb[6], pb[4]])
            P.add("vector", lambda e: e.tensor_tensor(
                out=dstbuf.t[:].rearrange("p (k c) -> p k c", k=8),
                in0=srcnorm.t[:].rearrange("p (k c) -> p k c", k=8),
                in1=pb[6].t[:, 0:W][:, None, :].broadcast_to((128, 8, W)), op=ALU.subtract),
                reads=[srcnorm, pb[6]], writes=[dstbuf])
            P.add("vector", lambda e: e.tensor_tensor(
                out=dstbuf.t[:].rearrange("p (k c) -> p k c", k=8),
                in0=dstbuf.t[:].rearrange("p (k c) -> p k c", k=8),
                in1=pb[4].t[:, 0:W][:, None, :].broadcast_to((128, 8, W)), op=ALU.mult),
                reads=[dstbuf, pb[4]], writes=[dstbuf])

        colbf = sb("colbf", 128, 8, F32)

        def row_from_cols(colbuf, coloff, colstride, n, dst_off):
            if colstride > 1:
                src_ap = colbuf.t[:].rearrange("p (a b) -> p a b", b=colstride)[:, 0:n, coloff]
            else:
                src_ap = colbuf.t[:, coloff:coloff + n]
            P.add("vector", lambda e, src_ap=src_ap: e.tensor_copy(
                out=colbf.t[:, 0:n], in_=src_ap),
                reads=[colbuf], writes=[colbf])
            for ti in range(n):
                P.add("tensor", lambda e, ti=ti: e.transpose(
                    pb[3].t[0:1, ti * 128:(ti + 1) * 128],
                    colbf.t[:, ti:ti + 1], c_eyef.t[:]),
                    reads=[colbf, c_eyef], writes=[pb[3]])
            P.add("vector", lambda e: e.tensor_copy(out=yrow.t[0:1, dst_off:dst_off + n * 128],
                                                    in_=pb[3].t[0:1, 0:n * 128]),
                  reads=[pb[3]], writes=[yrow])

        def bcast_row(src_off, W, dst, dst_off, scale_src=None):
            P.add("tensor", lambda e: e.matmul(
                pb[4].t[:, 0:W], c_onesrf.t[:], yrow.t[0:1, src_off:src_off + W],
                start=True, stop=True), reads=[yrow, c_onesrf], writes=[pb[4]])
            P.add("vector", lambda e: e.tensor_copy(out=dst.t[:, dst_off:dst_off + W],
                                                    in_=pb[4].t[:, 0:W]),
                  reads=[pb[4]], writes=[dst])

        def b_supertile(st):
            rb = st * ST_B

            def ldx1(e):
                ds = []
                for b in range(NB):
                    r0 = b * T + rb
                    ds.append(e.dma_start(
                        out=x1b.t[b * 64:(b + 1) * 64, :].rearrange("p (t d) -> p t d", t=NTB),
                        in_=x1_sc[r0:r0 + NTB * 64, :].rearrange("(t p) d -> p t d", p=64)))
                return ds
            P.add("sync", dmas(ldx1, NB), dma="ld", n_dma=NB, reads=[x1B],
                  writes=[x1b])
            def ldx1s(e, st=st):
                ds = []
                for b in range(NB):
                    p0 = b * 64
                    if st == 0:
                        ds.append(e.dma_start(out=x1s.t[p0 + 1:p0 + 64, 0:D],
                                              in_=x1_sc[b * T:b * T + 63, :]))
                        ds.append(e.dma_start(
                            out=x1s.t[p0:p0 + 64, D:NTB * D].rearrange(
                                "p (t d) -> p t d", t=NTB - 1),
                            in_=x1_sc[b * T + 63:b * T + 63 + (NTB - 1) * 64, :].rearrange(
                                "(t p) d -> p t d", p=64)))
                    else:
                        r0 = b * T + rb - 1
                        ds.append(e.dma_start(
                            out=x1s.t[p0:p0 + 64, :].rearrange("p (t d) -> p t d", t=NTB),
                            in_=x1_sc[r0:r0 + NTB * 64, :].rearrange("(t p) d -> p t d", p=64)))
                return ds
            nst = 4 if st == 0 else 2
            P.add("sync", dmas(ldx1s, nst), dma="ld", n_dma=nst, reads=[x1B],
                  writes=[x1s])

            for ti in range(NTB):
                xt = lambda ti=ti: x1b.t[:, ti * D:(ti + 1) * D]
                P.add("vector", lambda e, xt=xt: e.reduce_sum(out=mu1.t[:, 0:1], in_=xt(), axis=AX.X),
                      reads=[x1b], writes=[mu1])
                P.add("vector", lambda e: e.tensor_scalar(out=mu1.t[:, 2:3], in0=mu1.t[:, 0:1],
                                                          scalar1=1.0 / D, scalar2=None,
                                                          op0=ALU.mult),
                      reads=[mu1], writes=[mu1])
                P.add("vector", lambda e: e.tensor_scalar(out=mu1.t[:, 3:4], in0=mu1.t[:, 2:3],
                                                          scalar1=-1.0, scalar2=None, op0=ALU.mult),
                      reads=[mu1], writes=[mu1])
                P.add("vector", lambda e, xt=xt: e.tensor_scalar(
                    out=xcT.t[:], in0=xt(), scalar1=mu1.t[:, 3:4], scalar2=None, op0=ALU.add),
                    reads=[x1b, mu1], writes=[xcT])
                P.add("scalar", lambda e: e.activation(out=x2f.t[:], in_=xcT.t[:], func=AF.Square),
                      reads=[xcT], writes=[x2f])
                P.add("vector", lambda e: e.reduce_sum(out=mu1.t[:, 4:5], in_=x2f.t[:], axis=AX.X),
                      reads=[x2f], writes=[mu1])
                P.add("vector", lambda e: e.tensor_scalar(out=mu1.t[:, 4:5], in0=mu1.t[:, 4:5],
                                                          scalar1=1.0 / D, scalar2=None,
                                                          op0=ALU.mult),
                      reads=[mu1], writes=[mu1])
                P.add("scalar", lambda e: e.activation(out=mu1.t[:, 5:6], in_=mu1.t[:, 4:5],
                                                       func=AF.Sqrt, bias=cflt.t[:, 0:1]),
                      reads=[mu1], writes=[mu1])
                P.add("vector", lambda e: e.reciprocal(out=mu1.t[:, 6:7], in_=mu1.t[:, 5:6]),
                      reads=[mu1], writes=[mu1])
                P.add("vector", lambda e: e.reduce_max(out=mu1.t[:, 0:1], in_=xcT.t[:], axis=AX.X,
                                                       apply_absolute_value=True),
                      reads=[xcT], writes=[mu1])
                P.add("vector", lambda e: e.tensor_tensor(out=mu1.t[:, 1:2], in0=mu1.t[:, 0:1],
                                                          in1=mu1.t[:, 6:7], op=ALU.mult),
                      reads=[mu1], writes=[mu1])
                P.add("vector", lambda e: e.tensor_scalar(out=mu1.t[:, 1:2], in0=mu1.t[:, 1:2],
                                                          scalar1=1e-5, scalar2=None, op0=ALU.max),
                      reads=[mu1], writes=[mu1])
                P.add("vector", lambda e: e.reciprocal(out=mu1.t[:, 3:4], in_=mu1.t[:, 1:2]),
                      reads=[mu1], writes=[mu1])
                P.add("vector", lambda e: e.tensor_tensor(out=mu1.t[:, 3:4], in0=mu1.t[:, 3:4],
                                                          in1=mu1.t[:, 6:7], op=ALU.mult),
                      reads=[mu1], writes=[mu1])
                P.add("vector", lambda e, ti=ti: e.tensor_scalar(
                    out=drow.t[:, ti * 2:ti * 2 + 1], in0=mu1.t[:, 3:4],
                    scalar1=127.0, scalar2=None, op0=ALU.mult),
                    reads=[mu1], writes=[drow])
                P.add("vector", lambda e: e.tensor_scalar(
                    out=mu1.t[:, 4:5], in0=mu1.t[:, 1:2],
                    scalar1=1.0 / 127.0, scalar2=None, op0=ALU.mult),
                    reads=[mu1], writes=[mu1])
                P.add("vector", lambda e, ti=ti: e.tensor_tensor(
                    out=drow.t[:, ti * 2 + 1:ti * 2 + 2], in0=mu1.t[:, 4:5],
                    in1=swsc.t[:, 2:3], op=ALU.mult),
                    reads=[mu1, swsc], writes=[drow])
                P.add("vector", lambda e, ti=ti: e.tensor_scalar(
                    out=x2f.t[:], in0=xcT.t[:], scalar1=drow.t[:, ti * 2:ti * 2 + 1],
                    scalar2=None, op0=ALU.mult),
                    reads=[xcT, drow], writes=[x2f])
                for hh in range(2):
                    P.add("vector", lambda e, hh=hh: e.tensor_copy(
                        out=ivh.t[:], in_=x2f.t[:, hh * 512:(hh + 1) * 512]),
                        reads=[x2f], writes=[ivh])
                    P.add("vector", lambda e, ti=ti, hh=hh: e.tensor_copy(
                        out=xqT.t[:, ti * D + hh * 512:ti * D + (hh + 1) * 512],
                        in_=ivh.t[:]),
                        reads=[ivh], writes=[xqT])
                tm_to_fm(xqT, ti * D, xqF, WB, ti)

            row_from_cols(drow, 1, 2, NTB, 0)
            bcast_row(0, WB, dscb, 0)

            P.add("vector", lambda e: e.memset(mxp.t[:], 0.0), writes=[mxp])
            P.add("vector", lambda e: e.memset(mxp2.t[:], 0.0), writes=[mxp2])

            def emit_ldw1(tt):
                wslot = (tt % 2) * 1024
                def ldw1(e, tt=tt, wslot=wslot):
                    return [e.dma_start(out=w1w.t[:, wslot:wslot + 1024],
                                        in_=w1q_sc[tt * 128:(tt + 1) * 128, :])]
                P.add("sync", dmas(ldw1, 1), dma="ld", n_dma=1, reads=[w1qB],
                      writes=[w1w_p[tt % 2]])

            def emit_mmh(tt):
                wslot = (tt % 2) * 1024
                bank = pb[tt % 2]
                def mmh(e, tt=tt, wslot=wslot, bank=bank):
                    last = None
                    for k in range(8):
                        last = e.matmul(bank.t[:, 0:WB],
                                        w1w.t[:, wslot + k * 128:wslot + (k + 1) * 128],
                                        xqF.t[:, k * WB:(k + 1) * WB],
                                        start=(k == 0), stop=(k == 7))
                    return last
                P.add("tensor", mmh, reads=[w1w_p[tt % 2], xqF], writes=[bank])
                if st == NST_B - 1:
                    for b in range(2):
                        c0 = 3 * 128 + b * 64 + 62
                        def evhf(e, tt=tt, b=b, bank=bank, c0=c0):
                            return e.tensor_tensor(
                                out=hprevF.t[:, tt * 4 + b * 2:tt * 4 + b * 2 + 2],
                                in0=bank.t[:, c0:c0 + 2], in1=dscb.t[:, c0:c0 + 2],
                                op=ALU.mult)
                        P.add("vector", evhf, reads=[bank, dscb], writes=[hprevF])

            def emit_hev(tt):
                bank = pb[tt % 2]
                hFp, hhp = hF_p[tt % 2], hhalo_p[tt % 2]
                slot = (tt % 2) * (2 * HBW)
                def evhal(e, tt=tt, slot=slot):
                    return e.tensor_copy(
                        out=hF.t[:, slot:slot + 2 * HBW].rearrange(
                            "p (b w) -> p b w", b=2)[:, :, 0:2],
                        in_=hhalo.t[:, tt * 4:tt * 4 + 4].rearrange("p (b c) -> p b c", b=2))
                P.add("vector", evhal, reads=[hhp], writes=[hFp])
                def evh(e, tt=tt, bank=bank, slot=slot):
                    out_ap = hF.t[:, slot:slot + 2 * HBW].rearrange(
                        "p (b w) -> p b w", b=2)[:, :, 2:2 + ST_B].rearrange(
                        "p b (i j) -> p i b j", j=64)
                    return e.tensor_tensor(
                        out=out_ap,
                        in0=bank.t[:, 0:WB].rearrange("p (i b j) -> p i b j", b=2, j=64),
                        in1=dscb.t[:, 0:WB].rearrange("p (i b j) -> p i b j", b=2, j=64),
                        op=ALU.mult)
                P.add("vector", evh, reads=[bank, dscb], writes=[hFp])
                def evsav(e, tt=tt, slot=slot):
                    return e.tensor_copy(
                        out=hhalo.t[:, tt * 4:tt * 4 + 4].rearrange("p (b c) -> p b c", b=2),
                        in_=hF.t[:, slot:slot + 2 * HBW].rearrange(
                            "p (b w) -> p b w", b=2)[:, :, ST_B:ST_B + 2])
                P.add("vector", evsav, reads=[hFp], writes=[hhp])

            def emit_conv(tt):
                hFp = hF_p[tt % 2]
                a0, a1, yfs = (ac0, ac1, yf) if tt % 2 == 0 else (ac0b, ac1b, yfb)
                mxs = mxp if tt % 2 == 0 else mxp2
                slot = (tt % 2) * (2 * HBW)
                def hv(tp, slot=slot):
                    return hF.t[:, slot:slot + 2 * HBW].rearrange(
                        "p (b w) -> p b w", b=2)[:, :, tp:tp + ST_B]
                def cv0(e, tt=tt, slot=slot, a0=a0):
                    return e.tensor_scalar(
                        out=a0.t[:].rearrange("p (b j) -> p b j", b=2),
                        in0=hv(0, slot), scalar1=c_cw.t[:, tt * 3:tt * 3 + 1],
                        scalar2=None, op0=ALU.mult)
                P.add("vector", cv0, reads=[hFp, c_cw], writes=[a0])
                def cv1(e, tt=tt, slot=slot, a0=a0, a1=a1):
                    return e.scalar_tensor_tensor(
                        out=a1.t[:].rearrange("p (b j) -> p b j", b=2),
                        in0=hv(1, slot), scalar=c_cw.t[:, tt * 3 + 1:tt * 3 + 2],
                        in1=a0.t[:].rearrange("p (b j) -> p b j", b=2),
                        op0=ALU.mult, op1=ALU.add)
                P.add("vector", cv1, reads=[hFp, c_cw, a0], writes=[a1])
                def cv2(e, tt=tt, slot=slot, a0=a0, a1=a1):
                    return e.scalar_tensor_tensor(
                        out=a0.t[:].rearrange("p (b j) -> p b j", b=2),
                        in0=hv(2, slot), scalar=c_cw.t[:, tt * 3 + 2:tt * 3 + 3],
                        in1=a1.t[:].rearrange("p (b j) -> p b j", b=2),
                        op0=ALU.mult, op1=ALU.add)
                P.add("vector", cv2, reads=[hFp, c_cw, a1], writes=[a0])
                def evy(e, tt=tt, a0=a0, yfs=yfs):
                    return e.activation(out=yfs.t[:], in_=a0.t[:], func=AF.Silu,
                                        bias=c_cb.t[:, tt:tt + 1])
                P.add("scalar", evy, reads=[a0, c_cb], writes=[yfs])
                def evyb(e, tt=tt, yfs=yfs):
                    return e.copy(out=hq.t[:, tt * WB:(tt + 1) * WB], in_=yfs.t[:])
                P.add("scalar", evyb, reads=[yfs], writes=[hq_p[tt % 2]])
                def styf(e, tt=tt, yfs=yfs):
                    r0 = tt * 2 * 128
                    return [e.dma_start(out=y_sc[r0:r0 + 128, :], in_=yfs.t[:, 0:256]),
                            e.dma_start(out=y_sc[r0 + 128:r0 + 256, :], in_=yfs.t[:, 256:512])]
                P.add("gpsimd", dmas(styf, 2), dma="st", n_dma=2, reads=[yfs],
                      writes=[yscB4[tt % 4]])
                def evab(e, yfs=yfs, a1=a1):
                    return e.scalar_tensor_tensor(out=a1.t[:], in0=yfs.t[:], scalar=-1.0,
                                                  in1=yfs.t[:], op0=ALU.mult, op1=ALU.max)
                P.add("vector", evab, reads=[yfs], writes=[a1])
                def evmx(e, a1=a1, mxs=mxs):
                    return e.tensor_tensor(out=mxs.t[:], in0=mxs.t[:], in1=a1.t[:],
                                           op=ALU.max)
                P.add("vector", evmx, reads=[a1, mxs], writes=[mxs])

            emit_ldw1(0)
            for tt in range(32):
                if tt + 1 < 32:
                    emit_ldw1(tt + 1)
                emit_mmh(tt)
                if tt >= 1:
                    emit_conv(tt - 1)
                emit_hev(tt)
            emit_conv(31)
            P.add("vector", lambda e: e.tensor_tensor(out=mxp.t[:], in0=mxp.t[:],
                                                      in1=mxp2.t[:], op=ALU.max),
                  reads=[mxp, mxp2], writes=[mxp])

            for blk in range(4):
                P.add("tensor", lambda e, blk=blk: e.transpose(
                    pb[2].t[:, blk * 128:(blk + 1) * 128],
                    mxp.t[:, blk * 128:(blk + 1) * 128], c_eyef.t[:]),
                    reads=[mxp, c_eyef], writes=[pb[2]])
            P.add("vector", lambda e: e.tensor_copy(out=dscb.t[:, WB:2 * WB],
                                                    in_=pb[2].t[:, 0:512]),
                  reads=[pb[2]], writes=[dscb])
            P.add("vector", lambda e: e.reduce_max(
                out=drow.t[:, 0:4], in_=dscb.t[:, WB:2 * WB].rearrange("p (a c) -> p a c", a=4),
                axis=AX.X, apply_absolute_value=True),
                reads=[dscb], writes=[drow])
            row_from_cols(drow, 0, 1, 4, WB)
            P.add("vector", lambda e: e.tensor_scalar(out=yrow.t[0:1, WB:2 * WB],
                                                      in0=yrow.t[0:1, WB:2 * WB],
                                                      scalar1=1e-5, scalar2=None, op0=ALU.max),
                  reads=[yrow], writes=[yrow])
            P.add("vector", lambda e: e.reciprocal(out=yrow.t[0:1, 2 * WB:3 * WB],
                                                   in_=yrow.t[0:1, WB:2 * WB]),
                  reads=[yrow], writes=[yrow])
            P.add("vector", lambda e: e.tensor_scalar(out=yrow.t[0:1, 2 * WB:3 * WB],
                                                      in0=yrow.t[0:1, 2 * WB:3 * WB],
                                                      scalar1=127.0, scalar2=None, op0=ALU.mult),
                  reads=[yrow], writes=[yrow])
            P.add("vector", lambda e: e.tensor_scalar(out=yrow.t[0:1, 3 * WB:4 * WB],
                                                      in0=yrow.t[0:1, WB:2 * WB],
                                                      scalar1=1.0 / 127.0, scalar2=None,
                                                      op0=ALU.mult),
                  reads=[yrow], writes=[yrow])
            bcast_row(2 * WB, WB, dscb, WB)
            P.add("vector", lambda e: e.tensor_copy(
                out=yrow.t[0:1, 0:WB].rearrange("o (i b j) -> o i b j", b=2, j=64),
                in_=yrow.t[0:1, 3 * WB:4 * WB].rearrange("o (b i j) -> o i b j", b=2, j=64)),
                reads=[yrow], writes=[yrow])
            bcast_row(0, WB, dscb, 2 * WB)
            def emit_ldyf(tt):
                ls = (tt % 2) * 512
                def ldyf(e, tt=tt, ls=ls):
                    r0 = tt * 2 * 128
                    return [e.dma_start(out=yld.t[:, ls:ls + 256], in_=y_sc[r0:r0 + 128, :]),
                            e.dma_start(out=yld.t[:, ls + 256:ls + 512],
                                        in_=y_sc[r0 + 128:r0 + 256, :])]
                P.add("sync", dmas(ldyf, 2), dma="ld", n_dma=2, reads=[yscB4[tt % 4]],
                      writes=[yld_p[tt % 2]])

            def emit_q1(tt):
                ls = (tt % 2) * 512
                iv = ivh if tt % 2 == 0 else ivhb
                def q1(e, tt=tt, ls=ls, iv=iv):
                    return e.tensor_tensor(
                        out=iv.t[:],
                        in0=yld.t[:, ls:ls + 512],
                        in1=dscb.t[:, WB:2 * WB],
                        op=ALU.mult)
                P.add("vector", q1, reads=[yld_p[tt % 2], dscb], writes=[iv])

            def emit_q2(tt):
                iv = ivh if tt % 2 == 0 else ivhb
                def q2(e, tt=tt, iv=iv):
                    return e.tensor_copy(
                        out=hq.t[:, tt * WB:(tt + 1) * WB].rearrange(
                            "p (i b j) -> p i b j", b=2, j=64),
                        in_=iv.t[:].rearrange("p (b i j) -> p i b j", b=2, j=64))
                P.add("vector", q2, reads=[iv], writes=[hq_p[tt % 2]])

            emit_ldyf(0)
            for tt in range(32):
                if tt + 1 < 32:
                    emit_ldyf(tt + 1)
                emit_q1(tt)
                if tt >= 1:
                    emit_q2(tt - 1)
            emit_q2(31)
            P.add("vector", lambda e: e.tensor_scalar(out=dscb.t[:, 2 * WB:3 * WB],
                                                      in0=dscb.t[:, 2 * WB:3 * WB],
                                                      scalar1=swsc.t[:, 3:4], scalar2=None,
                                                      op0=ALU.mult),
                  reads=[dscb, swsc], writes=[dscb])
            def emit_ldw2(mo):
                wslot = (mo % 2) * 4096
                def ldw2(e, mo=mo, wslot=wslot):
                    return [e.dma_start(out=w2w.t[:, wslot:wslot + 4096],
                                        in_=w2q_sc[mo * 128:(mo + 1) * 128, :])]
                P.add("sync", dmas(ldw2, 1), dma="ld", n_dma=1, reads=[w2qB],
                      writes=[w2w_p[mo % 2]])

            emit_ldw2(0)
            for mo in range(8):
                if mo + 1 < 8:
                    emit_ldw2(mo + 1)
                wslot = (mo % 2) * 4096
                bank = pb[mo % 2]
                def mmw2(e, mo=mo, wslot=wslot, bank=bank):
                    last = None
                    for k in range(32):
                        last = e.matmul(bank.t[:, 0:WB],
                                        w2w.t[:, wslot + k * 128:wslot + (k + 1) * 128],
                                        hq.t[:, k * WB:(k + 1) * WB],
                                        start=(k == 0), stop=(k == 31))
                    return last
                P.add("tensor", mmw2, reads=[w2w_p[mo % 2], hq, hq_p[1]], writes=[bank])
                def evw2(e, mo=mo, bank=bank):
                    return e.tensor_tensor(out=outF.t[:, mo * WB:(mo + 1) * WB],
                                           in0=bank.t[:, 0:WB], in1=dscb.t[:, 2 * WB:3 * WB],
                                           op=ALU.mult)
                P.add("vector", evw2, reads=[bank, dscb], writes=[outF])
                def evw2f(e, mo=mo, bank=bank):
                    return e.tensor_tensor(out=outFf.t[:, mo * WB:(mo + 1) * WB],
                                           in0=bank.t[:, 0:WB], in1=dscb.t[:, 2 * WB:3 * WB],
                                           op=ALU.mult)
                P.add("vector", evw2f, reads=[bank, dscb], writes=[outFf])

            fm_ln_b(outF, outFf, onB, WB)
            for m in range(NTB):
                for k in range(8):
                    sl = k % 8
                    P.add("tensor", lambda e, m=m, k=k, sl=sl: e.transpose(
                        pT.t[:, sl * 128:(sl + 1) * 128],
                        onB.t[:, k * WB + m * 128:k * WB + (m + 1) * 128], c_eye.t[:]),
                        reads=[onB, c_eye], writes=pTB)
                def evd(e, m=m):
                    return e.tensor_copy(
                        out=d2TM.t[:].rearrange("p (m k c) -> p m k c", m=NTB, k=8)[:, m, :, :],
                        in_=pT.t[:].rearrange("p (a c) -> p a c", a=8))
                P.add("vector", evd, reads=pTB, writes=[d2TM])
            P.add("vector", lambda e: e.tensor_tensor(out=x2T.t[:], in0=x1s.t[:], in1=d2TM.t[:],
                                                      op=ALU.add),
                  reads=[x1s, d2TM], writes=[x2T])

            def sty(e, st=st, rb=rb):
                ds = []
                for b in range(NB):
                    p0 = b * 64
                    if st == 0:
                        ds.append(e.dma_start(out=y_out[b * T + 0:b * T + 63, :],
                                              in_=x2T.t[p0 + 1:p0 + 64, 0:D]))
                        ds.append(e.dma_start(
                            out=y_out[b * T + 63:b * T + 63 + (NTB - 1) * 64, :].rearrange(
                                "(t p) d -> p t d", p=64),
                            in_=x2T.t[p0:p0 + 64, D:NTB * D].rearrange(
                                "p (t d) -> p t d", t=NTB - 1)))
                    else:
                        r0 = b * T + rb - 1
                        ds.append(e.dma_start(
                            out=y_out[r0:r0 + NTB * 64, :].rearrange("(t p) d -> p t d", p=64),
                            in_=x2T.t[p0:p0 + 64, :].rearrange("p (t d) -> p t d", t=NTB)))
                return ds
            nst = 4 if st == 0 else 2
            P.add("gpsimd", dmas(sty, nst), dma="st", n_dma=nst, reads=[x2T])



        if stop_after is None:
            for st in range(NST_B):
                b_supertile(st)
        tapB("Bdscb", dscb, 128, 3 * WB)
        tapB("Bdrow", drow, 128, NTB * 2)
        tapB("Bhq", hq, 128, 32 * WB)
        tapB("BoutF", outF, 128, 8 * WB)
        tapB("BxqF", xqF, 128, 8 * WB)
        tapB("Byrow", yrow, 1, 4 * WB)
        tapB("Bswsc", swsc, 128, 4)

        def final_token():
            for tp2 in range(2):
                def cv(e, tp2=tp2):
                    return e.tensor_tensor(
                        out=ymx.t[:, 64 + tp2 * 64:128 + tp2 * 64].rearrange(
                            "p (t b) -> p t b", b=2),
                        in0=hprevF.t[:].rearrange("p (t b c) -> p t b c", b=2, c=2)[:, :, :, tp2],
                        in1=c_cw.t[:].rearrange("p (t k) -> p t k", k=3)[:, :, tp2:tp2 + 1]
                        .broadcast_to((128, 32, 2)),
                        op=ALU.mult)
                P.add("vector", cv, reads=[hprevF, c_cw], writes=[ymx])
            P.add("vector", lambda e: e.tensor_tensor(out=ymx.t[:, 0:64], in0=ymx.t[:, 64:128],
                                                      in1=ymx.t[:, 128:192], op=ALU.add),
                  reads=[ymx], writes=[ymx])
            P.add("vector", lambda e: e.tensor_tensor(
                out=ymx.t[:, 0:64].rearrange("p (t b) -> p t b", b=2),
                in0=ymx.t[:, 0:64].rearrange("p (t b) -> p t b", b=2),
                in1=c_cb.t[:, :, None].broadcast_to((128, 32, 2)), op=ALU.add),
                reads=[ymx, c_cb], writes=[ymx])
            P.add("scalar", lambda e: e.activation(out=ymx.t[:, 0:64], in_=ymx.t[:, 0:64],
                                                   func=AF.Silu),
                  reads=[ymx], writes=[ymx])
            P.add("vector", lambda e: e.reduce_max(
                out=drow.t[:, 0:2], in_=ymx.t[:, 0:64].rearrange("p (t b) -> p b t", b=2),
                axis=AX.X, apply_absolute_value=True),
                reads=[ymx], writes=[drow])
            row_from_cols(drow, 0, 1, 2, 0)
            P.add("vector", lambda e: e.reduce_max(out=yrow.t[0:1, 300:301],
                                                   in_=yrow.t[0:1, 0:128], axis=AX.X),
                  reads=[yrow], writes=[yrow])
            P.add("vector", lambda e: e.reduce_max(out=yrow.t[0:1, 301:302],
                                                   in_=yrow.t[0:1, 128:256], axis=AX.X),
                  reads=[yrow], writes=[yrow])
            P.add("vector", lambda e: e.tensor_scalar(out=yrow.t[0:1, 300:302],
                                                      in0=yrow.t[0:1, 300:302],
                                                      scalar1=1e-5, scalar2=None, op0=ALU.max),
                  reads=[yrow], writes=[yrow])
            P.add("vector", lambda e: e.reciprocal(out=yrow.t[0:1, 302:304],
                                                   in_=yrow.t[0:1, 300:302]),
                  reads=[yrow], writes=[yrow])
            P.add("vector", lambda e: e.tensor_scalar(out=yrow.t[0:1, 302:304],
                                                      in0=yrow.t[0:1, 302:304],
                                                      scalar1=127.0, scalar2=None, op0=ALU.mult),
                  reads=[yrow], writes=[yrow])

            def mmbf(e):
                e.matmul(pb[4].t[:, 0:2], c_onesrf.t[:], yrow.t[0:1, 302:304],
                         start=True, stop=True)
                return e.matmul(pb[4].t[:, 2:4], c_onesrf.t[:], yrow.t[0:1, 300:302],
                                start=True, stop=True)
            P.add("tensor", mmbf, reads=[yrow, c_onesrf], writes=[pb[4]])
            P.add("vector", lambda e: e.tensor_copy(out=drow.t[:, 2:6], in_=pb[4].t[:, 0:4]),
                  reads=[pb[4]], writes=[drow])
            P.add("vector", lambda e: e.tensor_tensor(
                out=ivh.t[:, 0:64].rearrange("p (t b) -> p t b", b=2),
                in0=ymx.t[:, 0:64].rearrange("p (t b) -> p t b", b=2),
                in1=drow.t[:, 2:4][:, None, :].broadcast_to((128, 32, 2)), op=ALU.mult),
                reads=[ymx, drow], writes=[ivh])
            P.add("vector", lambda e: e.tensor_copy(out=hq.t[:, 0:64], in_=ivh.t[:, 0:64]),
                  reads=[ivh], writes=[hq])
            for mo in range(8):
                wslot = (mo % 2) * 4096
                def ldw2(e, mo=mo, wslot=wslot):
                    return [e.dma_start(out=w2w.t[:, wslot:wslot + 4096],
                                        in_=w2q_sc[mo * 128:(mo + 1) * 128, :])]
                P.add("sync", dmas(ldw2, 1), dma="ld", n_dma=1, reads=[w2qB], writes=[w2w])
                bank = pb[mo % 2]
                def mmw2(e, mo=mo, wslot=wslot, bank=bank):
                    last = None
                    for k in range(32):
                        last = e.matmul(bank.t[:, 0:2],
                                        w2w.t[:, wslot + k * 128:wslot + (k + 1) * 128],
                                        hq.t[:, k * 2:(k + 1) * 2],
                                        start=(k == 0), stop=(k == 31))
                    return last
                P.add("tensor", mmw2, reads=[w2w, hq], writes=[bank])
                def evw2(e, mo=mo, bank=bank):
                    return e.tensor_tensor(out=outF.t[:, mo * 2:(mo + 1) * 2],
                                           in0=bank.t[:, 0:2], in1=drow.t[:, 4:6], op=ALU.mult)
                P.add("vector", evw2, reads=[bank, drow], writes=[outF])
            P.add("vector", lambda e: e.tensor_scalar(out=outF.t[:, 0:16], in0=outF.t[:, 0:16],
                                                      scalar1=1.0 / 127.0, scalar2=None,
                                                      op0=ALU.mult),
                  reads=[outF], writes=[outF])
            P.add("vector", lambda e: e.tensor_scalar(out=outF.t[:, 0:16], in0=outF.t[:, 0:16],
                                                      scalar1=swsc.t[:, 3:4], scalar2=None,
                                                      op0=ALU.mult),
                  reads=[outF, swsc], writes=[outF])
            tapB("FoutFpre", outF, 128, 16)
            def mmsum(e):
                last = None
                for k in range(8):
                    last = e.matmul(pb[5].t[0:1, 0:2], c_ones.t[:], outF.t[:, k * 2:(k + 1) * 2],
                                    start=(k == 0), stop=(k == 7))
                return last
            P.add("tensor", mmsum, reads=[outF, c_ones], writes=[pb[5]])
            for k in range(8):
                P.add("scalar", lambda e, k=k: e.activation(out=o2pB.t[:, 0:2],
                                                            in_=outF.t[:, k * 2:(k + 1) * 2],
                                                            func=AF.Square),
                      reads=[outF], writes=[o2pB])
                def mm2(e, k=k):
                    return e.matmul(pb[5].t[64:65, 0:2], c_ones.t[:], o2pB.t[:, 0:2],
                                    start=(k == 0), stop=(k == 7))
                P.add("tensor", mm2, reads=[o2pB, c_ones], writes=[pb[5]])
            P.add("vector", lambda e: e.tensor_scalar(out=yrow.t[0:1, 310:312],
                                                      in0=pb[5].t[0:1, 0:2],
                                                      scalar1=1.0 / D, scalar2=None, op0=ALU.mult),
                  reads=[pb[5]], writes=[yrow])
            P.add("vector", lambda e: e.tensor_scalar(out=yrow.t[0:1, 312:314],
                                                      in0=pb[5].t[64:65, 0:2],
                                                      scalar1=1.0 / D, scalar2=None, op0=ALU.mult),
                  reads=[pb[5]], writes=[yrow])
            P.add("vector", lambda e: e.tensor_tensor(out=yrow.t[0:1, 314:316],
                                                      in0=yrow.t[0:1, 310:312],
                                                      in1=yrow.t[0:1, 310:312], op=ALU.mult),
                  reads=[yrow], writes=[yrow])
            P.add("vector", lambda e: e.tensor_tensor(out=yrow.t[0:1, 312:314],
                                                      in0=yrow.t[0:1, 312:314],
                                                      in1=yrow.t[0:1, 314:316], op=ALU.subtract),
                  reads=[yrow], writes=[yrow])
            P.add("scalar", lambda e: e.activation(out=yrow.t[0:1, 312:314],
                                                   in_=yrow.t[0:1, 312:314], func=AF.Sqrt,
                                                   bias=cflt.t[0:1, 0:1]),
                  reads=[yrow], writes=[yrow])
            P.add("vector", lambda e: e.reciprocal(out=yrow.t[0:1, 312:314],
                                                   in_=yrow.t[0:1, 312:314]),
                  reads=[yrow], writes=[yrow])
            def mmbm(e):
                e.matmul(pb[4].t[:, 4:6], c_onesrf.t[:], yrow.t[0:1, 310:312],
                         start=True, stop=True)
                return e.matmul(pb[4].t[:, 6:8], c_onesrf.t[:], yrow.t[0:1, 312:314],
                                start=True, stop=True)
            P.add("tensor", mmbm, reads=[yrow, c_onesrf], writes=[pb[4]])
            P.add("vector", lambda e: e.tensor_copy(out=drow.t[:, 6:8], in_=pb[4].t[:, 4:6]),
                  reads=[pb[4]], writes=[drow])
            P.add("vector", lambda e: e.tensor_copy(out=drow.t[:, 0:2], in_=pb[4].t[:, 6:8]),
                  reads=[pb[4]], writes=[drow])
            for k in range(8):
                P.add("vector", lambda e, k=k: e.tensor_tensor(out=outF.t[:, k * 2:(k + 1) * 2],
                                                               in0=outF.t[:, k * 2:(k + 1) * 2],
                                                               in1=drow.t[:, 6:8], op=ALU.subtract),
                      reads=[outF, drow], writes=[outF])
                P.add("vector", lambda e, k=k: e.tensor_tensor(out=outF.t[:, k * 2:(k + 1) * 2],
                                                               in0=outF.t[:, k * 2:(k + 1) * 2],
                                                               in1=drow.t[:, 0:2], op=ALU.mult),
                      reads=[outF, drow], writes=[outF])
                P.add("tensor", lambda e, k=k: e.transpose(pT.t[0:2, k * 128:(k + 1) * 128],
                                                           outF.t[:, k * 2:(k + 1) * 2],
                                                           c_eye.t[:]),
                      reads=[outF, c_eye], writes=pTB)
            P.add("vector", lambda e: e.tensor_copy(out=x2T.t[0:2, 0:D], in_=pT.t[0:2, 0:1024]),
                  reads=pTB, writes=[x2T])
            def ldlast(e):
                return [e.dma_start(out=x1s.t[0:1, 0:D], in_=x1_sc[T - 1:T, :]),
                        e.dma_start(out=x1s.t[1:2, 0:D], in_=x1_sc[2 * T - 1:2 * T, :])]
            P.add("sync", dmas(ldlast, 2), dma="ld", reads=[x1B], writes=[x1s])
            P.add("vector", lambda e: e.tensor_tensor(out=x2T.t[0:2, 0:D], in0=x1s.t[0:2, 0:D],
                                                      in1=x2T.t[0:2, 0:D], op=ALU.add),
                  reads=[x1s, x2T], writes=[x2T])
            def styl(e):
                return [e.dma_start(out=y_out[T - 1:T, :], in_=x2T.t[0:1, 0:D]),
                        e.dma_start(out=y_out[2 * T - 1:2 * T, :], in_=x2T.t[1:2, 0:D])]
            P.add("gpsimd", dmas(styl, 2), dma="st", reads=[x2T])

        if stop_after is None:
            final_token()
        tapB("Fymx", ymx, 128, 192)
        tapB("Fhprevf", hprevF, 128, 32 * 2 * 2)
        tapB("Fhq", hq, 128, 64)
        tapB("FoutF", outF, 128, 16)
        tapB("Fyrow", yrow, 1, 320)
        tapB("Fdrow", drow, 128, 8)
        tapB("Fx2T", x1b, 128, D)

        P.add("vector", lambda e: e.tensor_copy(out=P.junk[:, 0:1], in_=P.junk[:, 1:2]))
        P.wait_all("gpsimd")

        if max_steps is not None:
            P.steps = P.steps[:max_steps]
        with nc.Block() as block:
            P.emit_into(block, sems)

    return nc


def _constants():
    e = np.eye(128, dtype=ml_dtypes.bfloat16)
    tri = np.triu(np.ones((C, C), np.float32))
    cum = np.zeros((128, 128), np.float32)
    cum[0:64, 0:64] = tri
    cum[64:128, 64:128] = tri
    mlow = np.tril(np.ones((C, C), np.float32), -1)
    mupp = np.triu(np.ones((C, C), np.float32), 1)
    mud = np.triu(np.ones((C, C), np.float32), 0)
    ip = np.eye(C, dtype=np.float32)
    ones1 = np.ones((128, 1), np.float32)
    stack2 = lambda m: np.concatenate([m, m], axis=0).astype(np.float32)
    return dict(eye128=e, eye128f=np.eye(128, dtype=np.float32),
                cumtri=cum.astype(ml_dtypes.bfloat16),
                mlow=stack2(mlow), mupp=stack2(mupp), mud=stack2(mud),
                ipair=np.concatenate([ip, ip], axis=0).astype(ml_dtypes.bfloat16),
                ones1=ones1.astype(ml_dtypes.bfloat16),
                onesr=np.ones((1, 128), ml_dtypes.bfloat16),
                onesrf=np.ones((1, 128), np.float32))


def kernel(L0_attn_x, L0_attn_Wq, L0_attn_Wk, L0_attn_Wv, L0_attn_Wgamma,
           L0_attn_Wf1, L0_attn_Wf2, L0_attn_Wog1, L0_attn_Wog2, L0_attn_norm_w,
           L0_attn_Wo, L0_ffn_w1, L0_ffn_conv_w, L0_ffn_conv_b, L0_ffn_w2):
    T = 4096
    x = np.ascontiguousarray(L0_attn_x, dtype=np.float32)
    consts = _constants()
    common = dict(Wq=np.ascontiguousarray(L0_attn_Wq),
                  Wk=np.ascontiguousarray(L0_attn_Wk),
                  Wv=np.ascontiguousarray(L0_attn_Wv),
                  Wgamma=np.ascontiguousarray(L0_attn_Wgamma),
                  Wf1=np.ascontiguousarray(L0_attn_Wf1),
                  Wf2=np.ascontiguousarray(L0_attn_Wf2),
                  Wog1=np.ascontiguousarray(L0_attn_Wog1),
                  Wog2=np.ascontiguousarray(L0_attn_Wog2),
                  norm_w=np.ascontiguousarray(L0_attn_norm_w).reshape(1, D),
                  Wo=np.ascontiguousarray(L0_attn_Wo),
                  w1=np.ascontiguousarray(L0_ffn_w1),
                  conv_w=np.ascontiguousarray(L0_ffn_conv_w).reshape(DH, 3),
                  conv_b=np.ascontiguousarray(L0_ffn_conv_b).reshape(1, DH),
                  w2=np.ascontiguousarray(L0_ffn_w2),
                  **consts)
    nc = build_program(T)
    in_maps = []
    for c in range(8):
        xs = x[2 * c:2 * c + 2].reshape(NB * T, D)
        in_maps.append(dict(x=np.ascontiguousarray(xs), **common))
    res = run_bass_kernel_spmd(nc, in_maps, core_ids=list(range(8)))
    out = np.empty((16, T, D), np.float32)
    for c in range(8):
        out[2 * c:2 * c + 2] = res.results[c]["y"].reshape(NB, T, D)
    return out


# version 09
# speedup vs baseline: 1.5530x; 1.0012x over previous
import contextlib
import numpy as np
import ml_dtypes
import concourse.bass as bass
import concourse.mybir as mybir
from concourse.bass_utils import run_bass_kernel_spmd

F32 = mybir.dt.float32
BF16 = mybir.dt.bfloat16
I32 = mybir.dt.int32
AF = mybir.ActivationFunctionType
ALU = mybir.AluOpType
AX = mybir.AxisListType

D = 1024
H = 16
HD = 64
DH = 4096
C = 64
NB = 2
HPG = 8


class Buf:
    def __init__(self, t, name=""):
        self.t = t
        self.name = name
        self.wtok = None
        self.rtoks = []


class Prog:
    DSEMS = [f"{q}{i}" for q in ("ld", "st") for i in range(4)]

    def __init__(self):
        self.steps = []
        self.cnt = {"t": 0, "v": 0, "s": 0, "g": 0}
        for k in self.DSEMS:
            self.cnt[k] = 0
        self.dseq = {"ld": 0, "st": 0}
        self.esem = {"tensor": "t", "vector": "v", "scalar": "s", "gpsimd": "g"}

    def add(self, eng, fn, reads=(), writes=(), dma=None, n_dma=1):
        waits = {}
        for b in reads:
            if b.wtok is not None:
                s, v = b.wtok
                waits[s] = max(waits.get(s, 0), v)
        for b in writes:
            toks = ([b.wtok] if b.wtok else []) + b.rtoks
            for (s, v) in toks:
                waits[s] = max(waits.get(s, 0), v)
        if dma:
            sem = f"{dma}{self.dseq[dma] % 4}"
            self.dseq[dma] += 1
            self.cnt[sem] += 16 * n_dma
        else:
            sem = self.esem[eng]
            self.cnt[sem] += 1
        self.steps.append((eng, waits, fn, sem,
                           frozenset(id(b) for b in reads),
                           frozenset(id(b) for b in writes)))
        tok = (sem, self.cnt[sem])
        for b in reads:
            b.rtoks.append(tok)
        for b in writes:
            b.wtok = tok
            b.rtoks = []
        return tok

    def wait_all(self, eng, skip=()):
        snap = dict(self.cnt)
        own = self.esem.get(eng)

        def fn(e):
            return None
        fn._waits = {s: v for s, v in snap.items() if s != own and s not in skip and v > 0}
        self.steps.append((eng, fn._waits, None, None, frozenset(), frozenset()))

    def emit_into(self, block, sems):
        def mk(en):
            def run(eng):
                own = self.esem.get(en)
                hw = {}
                prev_w = frozenset()
                for (e, waits, fn, sem, rds, wrs) in self.steps:
                    if e != en:
                        continue
                    for s, v in waits.items():
                        if s == own:
                            continue
                        if hw.get(s, 0) >= v:
                            continue
                        eng.wait_ge(sems[s], v)
                        hw[s] = v
                    if fn is None:
                        continue
                    if en == "vector" and (prev_w & rds):
                        eng.tensor_copy(out=self.junk[:, 0:1], in_=self.junk[:, 1:2])
                    if en == "vector":
                        prev_w = wrs
                    out = fn(eng)
                    if sem is not None and sem not in ("t", "v", "s", "g"):
                        for ins in (out if isinstance(out, (list, tuple)) else [out]):
                            ins.then_inc(sems[sem], 16)
                    else:
                        out.then_inc(sems[sem], 1)
            return run
        block.sync(mk("sync"))
        block.vector(mk("vector"))
        block.scalar(mk("scalar"))
        block.tensor(mk("tensor"))
        block.gpsimd(mk("gpsimd"))


def build_program(T, dbg=(), stop_after=None, max_steps=None):
    ST_A = 128
    ST_B = 256
    NCH = ST_A // C
    NST_A = T // ST_A
    NST_B = T // ST_B
    NTA = 2 * ST_A // 128
    NTB = 2 * ST_B // 128
    WA = 2 * ST_A
    WB = 2 * ST_B
    HBW = ST_B + 4

    nc = bass.Bass("TRN2", target_bir_lowering=False, debug=False)

    x_in = nc.dram_tensor("x", [NB * T, D], F32, kind="ExternalInput").ap()
    ws_in = {}
    for nm, shp in (("Wq", [D, D]), ("Wk", [D, D]), ("Wv", [D, D]), ("Wgamma", [D, H]),
                    ("Wf1", [D, HD]), ("Wf2", [HD, D]), ("Wog1", [D, HD]),
                    ("Wog2", [HD, D]), ("norm_w", [1, D]), ("Wo", [D, D]),
                    ("w1", [DH, D]), ("conv_w", [DH, 3]), ("conv_b", [1, DH]),
                    ("w2", [D, DH]), ("eye128", [128, 128]), ("cumtri", [128, 128]),
                    ("mlow", [128, C]), ("mupp", [128, C]), ("mud", [128, C]),
                    ("eye128f", [128, 128]),
                    ("ipair", [128, C]), ("ones1", [128, 1]), ("onesr", [1, 128]), ("onesrf", [1, 128])):
        dt = BF16 if nm in ("eye128", "cumtri", "ipair", "ones1", "onesr") else F32
        ws_in[nm] = nc.dram_tensor(nm, shp, dt, kind="ExternalInput").ap()

    y_out = nc.dram_tensor("y", [NB * T, D], F32, kind="ExternalOutput").ap()
    x1_sc = nc.dram_tensor("x1sc", [NB * T, D], F32, kind="Internal").ap()
    sc_sc = nc.dram_tensor("scsc", [8, 4], F32, kind="Internal").ap()
    row_sc = nc.dram_tensor("rowsc", [4, 512], F32, kind="Internal").ap()
    w1q_sc = nc.dram_tensor("w1qsc", [DH, D], BF16, kind="Internal").ap()
    y_sc = nc.dram_tensor("ysc", [32 * 2 * 128, 256], F32, kind="Internal").ap()
    w2q_sc = nc.dram_tensor("w2qsc", [D, DH], BF16, kind="Internal").ap()

    dbg_outs = {}
    for name, shape in dbg:
        dbg_outs[name] = nc.dram_tensor("dbg_" + name, list(shape), F32,
                                        kind="ExternalOutput").ap()

    P = Prog()

    rowB = Buf(None, "row_sc")
    scB = Buf(None, "sc_sc")
    woqB = Buf(None, "woq_sc")
    w1qB = Buf(None, "w1q_sc")
    w2qB = Buf(None, "w2q_sc")
    x1B = Buf(None, "x1_sc")
    yscB4 = [Buf(None, f"y_sc{i}") for i in range(4)]

    with contextlib.ExitStack() as ES:
        sems = {s: ES.enter_context(nc.semaphore("sem_" + s))
                for s in ["t", "v", "s", "g"] + Prog.DSEMS}

        def sb(name, p, f, dt):
            return Buf(ES.enter_context(nc.sbuf_tensor(name, [p, f], dt)), name)

        P.junk = ES.enter_context(nc.sbuf_tensor("jkraw", [128, 4], F32))

        pb = [Buf(ES.enter_context(nc.psum_tensor(f"pb{i}", [128, 512], F32)), f"pb{i}")
              for i in range(7)]
        pT = Buf(ES.enter_context(nc.psum_tensor("pT", [128, 1024], BF16)), "pT")
        pT_p = [pT, Buf(pT.t, "pT_odd")]
        pTB = pT_p

        c_eye = sb("c_eye", 128, 128, BF16)
        c_eyef = sb("c_eyef", 128, 128, F32)
        c_cum = sb("c_cum", 128, 128, BF16)
        c_mlow = sb("c_mlow", 128, C, F32)
        c_mupp = sb("c_mupp", 128, C, F32)
        c_mud = sb("c_mud", 128, C, F32)
        c_ipair = sb("c_ipair", 128, C, BF16)
        c_ones = sb("c_ones", 128, 1, BF16)
        c_onesr = sb("c_onesr", 1, 128, BF16)
        c_onesrf = sb("c_onesrf", 1, 128, F32)
        w_f1 = sb("w_f1", 128, 8 * HD, BF16)
        w_og1 = sb("w_og1", 128, 8 * HD, BF16)
        w_f2 = sb("w_f2", 128, D, BF16)
        w_og2 = sb("w_og2", 128, D, BF16)
        w_gam = sb("w_gam", 128, 8 * H, BF16)
        c_cb = sb("c_cb", 128, 32, F32)
        c_cw = sb("c_cw", 128, 96, F32)
        n_w = sb("n_w", 128, 8, F32)
        swsc = sb("swsc", 128, 4, F32)
        s_state = sb("s_state", 128, HPG * 2 * HD, BF16)
        x2f = sb("x2f", 128, D, F32)
        mu1 = sb("mu1", 128, 8, F32)
        cflt = sb("cflt", 128, 2, F32)
        WS = contextlib.ExitStack()

        def sbW(name, p, f, dt):
            return Buf(WS.enter_context(nc.sbuf_tensor(name, [p, f], dt)), name)

        w_q = sbW("w_q", 128, 8 * D, BF16)
        w_k = sbW("w_k", 128, 8 * D, BF16)
        w_v = sbW("w_v", 128, 8 * D, BF16)
        w_ow = sbW("w_ow", 128, 2 * 8 * 128, BF16)
        w_ow_p = [w_ow, Buf(w_ow.t, "w_ow_odd")]

        def dmas(fn, n):
            fn.n_dma = n
            return fn

        P.add("vector", lambda e: e.memset(cflt.t[:, 0:1], 1e-5), writes=[cflt])
        P.add("vector", lambda e: e.memset(cflt.t[:, 1:2], -1.0), writes=[cflt])
        P.add("vector", lambda e: e.memset(w_f2.t[64:128, :], 0.0), writes=[w_f2])
        P.add("vector", lambda e: e.memset(w_og2.t[64:128, :], 0.0), writes=[w_og2])

        def ld_const(e):
            return [e.dma_start(out=c_eye.t[:], in_=ws_in["eye128"]),
                    e.dma_start(out=c_eyef.t[:], in_=ws_in["eye128f"]),
                    e.dma_start(out=c_cum.t[:], in_=ws_in["cumtri"]),
                    e.dma_start(out=c_mlow.t[:], in_=ws_in["mlow"]),
                    e.dma_start(out=c_mupp.t[:], in_=ws_in["mupp"]),
                    e.dma_start(out=c_mud.t[:], in_=ws_in["mud"]),
                    e.dma_start(out=c_ipair.t[:], in_=ws_in["ipair"]),
                    e.dma_start(out=c_ones.t[:], in_=ws_in["ones1"]),
                    e.dma_start(out=c_onesr.t[:], in_=ws_in["onesr"]),
                    e.dma_start(out=c_onesrf.t[:], in_=ws_in["onesrf"]),
                    e.dma_start(out=c_cb.t[:].rearrange("p (a b) -> p a b", b=1),
                                in_=ws_in["conv_b"].rearrange("o (a p b) -> p a (o b)", p=128, b=1),
                                allow_slow_non_contiguous=True),
                    e.dma_start(out=c_cw.t[:].rearrange("p (a b) -> p a b", b=3),
                                in_=ws_in["conv_w"].rearrange("(a p) b -> p a b", p=128),
                                allow_slow_non_contiguous=True),
                    e.dma_start(out=n_w.t[:].rearrange("p (a b) -> p a b", b=1),
                                in_=ws_in["norm_w"].rearrange("o (a p b) -> p a (o b)", p=128, b=1),
                                allow_slow_non_contiguous=True)]
        P.add("sync", dmas(ld_const, 13), dma="ld", n_dma=13,
              writes=[c_eye, c_eyef, c_cum, c_mlow, c_mupp, c_mud, c_ipair, c_ones, c_onesr, c_onesrf, c_cb, c_cw, n_w])

        with contextlib.ExitStack() as PS:
            def psb(name, p, f, dt):
                return Buf(PS.enter_context(nc.sbuf_tensor(name, [p, f], dt)), name)

            stg = psb("stg", 128, 4096, F32)
            tq = psb("tq", 128, 4096, BF16)
            tqt = psb("tqt", 128, 1024, BF16)
            acc1 = psb("acc1", 128, 2, F32)
            accb = psb("accb", 128, 1, F32)
            tr1 = psb("tr1", 1, 128, F32)
            iv0 = psb("iv0", 128, 4096, I32)

            def cast_small(nm, dst, cols, kt):
                def ld(e):
                    return [e.dma_start(out=stg.t[:, k * cols:(k + 1) * cols],
                                        in_=ws_in[nm][k * 128:(k + 1) * 128, 0:cols])
                            for k in range(kt)]
                P.add("sync", dmas(ld, kt), dma="ld", n_dma=kt, writes=[stg])
                P.add("vector", lambda e, cols=cols, kt=kt, dst=dst: e.tensor_copy(
                    out=dst.t[:, 0:kt * cols], in_=stg.t[:, 0:kt * cols]),
                    reads=[stg], writes=[dst])

            cast_small("Wf1", w_f1, HD, 8)
            cast_small("Wog1", w_og1, HD, 8)
            cast_small("Wgamma", w_gam, H, 8)

            def ld_f2(e):
                return [e.dma_start(out=stg.t[0:64, 0:D], in_=ws_in["Wf2"]),
                        e.dma_start(out=stg.t[0:64, D:2 * D], in_=ws_in["Wog2"])]
            P.add("sync", dmas(ld_f2, 2), dma="ld", n_dma=2, writes=[stg])
            P.add("vector", lambda e: e.tensor_copy(out=w_f2.t[0:64, :], in_=stg.t[0:64, 0:D]),
                  reads=[stg], writes=[w_f2])
            P.add("vector", lambda e: e.tensor_copy(out=w_og2.t[0:64, :], in_=stg.t[0:64, D:2 * D]),
                  reads=[stg], writes=[w_og2])

            for (nm, dst) in (("Wq", w_q), ("Wk", w_k), ("Wv", w_v)):
                for k in range(8):
                    half = (k % 2) * 2048
                    def ld(e, nm=nm, k=k, half=half):
                        return [e.dma_start(out=stg.t[:, half:half + D],
                                            in_=ws_in[nm][k * 128:(k + 1) * 128, :])]
                    P.add("sync", dmas(ld, 1), dma="ld", writes=[stg])
                    P.add("vector", lambda e, dst=dst, k=k, half=half: e.tensor_copy(
                        out=dst.t[:, k * D:(k + 1) * D], in_=stg.t[:, half:half + D]),
                        reads=[stg], writes=[dst])

            woq_sc = nc.dram_tensor("woqsc", [D, D], BF16, kind="Internal").ap()
            for k in range(8):
                half = (k % 2) * 2048
                def ld(e, k=k, half=half):
                    return [e.dma_start(out=stg.t[:, half:half + D],
                                        in_=ws_in["Wo"][k * 128:(k + 1) * 128, :])]
                P.add("sync", dmas(ld, 1), dma="ld", writes=[stg])
                P.add("vector", lambda e, k=k, half=half: e.tensor_scalar(
                    out=tq.t[:, 0:D], in0=stg.t[:, half:half + D],
                    scalar1=n_w.t[:, k:k + 1], scalar2=None, op0=ALU.mult),
                    reads=[stg, n_w], writes=[tq])
                def stw(e, k=k):
                    return [e.dma_start(out=woq_sc[mo * 128:(mo + 1) * 128, k * 128:(k + 1) * 128],
                                        in_=tq.t[:, mo * 128:(mo + 1) * 128]) for mo in range(8)]
                P.add("gpsimd", dmas(stw, 8), dma="st", n_dma=8, reads=[tq], writes=[woqB])

            def wmean(nm, rows, cols, out_col):
                nt = rows // 128
                P.add("vector", lambda e: e.memset(acc1.t[:, 0:1], 0.0), writes=[acc1])
                for i in range(nt):
                    def ld(e, i=i):
                        return [e.dma_start(out=stg.t[:, 0:cols],
                                            in_=ws_in[nm][i * 128:(i + 1) * 128, :])]
                    P.add("sync", dmas(ld, 1), dma="ld", writes=[stg])
                    P.add("vector", lambda e: e.reduce_sum(out=acc1.t[:, 1:2], in_=stg.t[:, 0:cols],
                                                           axis=AX.X, apply_absolute_value=True),
                          reads=[stg], writes=[acc1])
                    P.add("vector", lambda e: e.tensor_add(out=acc1.t[:, 0:1], in0=acc1.t[:, 0:1],
                                                           in1=acc1.t[:, 1:2]),
                          reads=[acc1], writes=[acc1])
                P.add("vector", lambda e: e.tensor_copy(out=accb.t[:], in_=acc1.t[:, 0:1]),
                      reads=[acc1], writes=[accb])
                P.add("tensor", lambda e: e.transpose(pb[0].t[0:1, 0:128], accb.t[:], c_eyef.t[:]),
                      reads=[accb, c_eyef], writes=[pb[0]])
                P.add("vector", lambda e: e.tensor_copy(out=tr1.t[:], in_=pb[0].t[0:1, 0:128]),
                      reads=[pb[0]], writes=[tr1])
                P.add("vector", lambda e: e.reduce_sum(out=tr1.t[0:1, 0:1], in_=tr1.t[:], axis=AX.X),
                      reads=[tr1], writes=[tr1])
                P.add("vector", lambda e, rows=rows, cols=cols: e.tensor_scalar(
                    out=tr1.t[0:1, 1:2], in0=tr1.t[0:1, 0:1],
                    scalar1=1.0 / (rows * cols), scalar2=1e-5, op0=ALU.mult, op1=ALU.max),
                    reads=[tr1], writes=[tr1])
                P.add("vector", lambda e: e.reciprocal(out=tr1.t[0:1, 2:3], in_=tr1.t[0:1, 1:2]),
                      reads=[tr1], writes=[tr1])

                P.add("tensor", lambda e, out_col=out_col: e.matmul(
                    pT.t[:, 0:1] if False else pb[6].t[:, out_col:out_col + 1],
                    c_onesrf.t[:], tr1.t[0:1, 2:3],
                    start=True, stop=True), reads=[tr1, c_onesrf], writes=[pb[6]])
                P.add("vector", lambda e, out_col=out_col: e.tensor_copy(
                    out=swsc.t[:, out_col:out_col + 1],
                    in_=pb[6].t[:, out_col:out_col + 1]),
                    reads=[pb[6]], writes=[swsc])

            wmean("w1", DH, D, 0)
            wmean("w2", D, DH, 1)
            P.add("vector", lambda e: e.reciprocal(out=swsc.t[:, 2:3], in_=swsc.t[:, 0:1]),
                  reads=[swsc], writes=[swsc])
            P.add("vector", lambda e: e.reciprocal(out=swsc.t[:, 3:4], in_=swsc.t[:, 1:2]),
                  reads=[swsc], writes=[swsc])

            def ternarize(nm, dst_sc, dstB, rows, cols, scol):
                nt, nf = rows // 128, cols // 128
                for i in range(nt):
                    def ld(e, i=i):
                        return [e.dma_start(out=stg.t[:, 0:cols],
                                            in_=ws_in[nm][i * 128:(i + 1) * 128, :])]
                    P.add("sync", dmas(ld, 1), dma="ld", writes=[stg])
                    P.add("vector", lambda e, cols=cols, scol=scol: e.tensor_scalar(
                        out=stg.t[:, 0:cols], in0=stg.t[:, 0:cols],
                        scalar1=swsc.t[:, scol:scol + 1], scalar2=None, op0=ALU.mult),
                        reads=[stg, swsc], writes=[stg])
                    P.add("vector", lambda e, cols=cols: e.tensor_copy(out=iv0.t[:, 0:cols],
                                                                      in_=stg.t[:, 0:cols]),
                          reads=[stg], writes=[iv0])
                    P.add("vector", lambda e, cols=cols: e.tensor_scalar(
                        out=tq.t[:, 0:cols], in0=iv0.t[:, 0:cols],
                        scalar1=1.0, scalar2=-1.0, op0=ALU.min, op1=ALU.max),
                        reads=[iv0], writes=[tq])
                    for fj in range(nf):
                        sl = fj % 8
                        P.add("tensor", lambda e, fj=fj, sl=sl: e.transpose(
                            pT.t[:, sl * 128:(sl + 1) * 128],
                            tq.t[:, fj * 128:(fj + 1) * 128], c_eye.t[:]),
                            reads=[tq, c_eye], writes=pTB)
                        P.add("scalar", lambda e, fj=fj, sl=sl: e.copy(
                            out=tqt.t[:, sl * 128:(sl + 1) * 128],
                            in_=pT.t[:, sl * 128:(sl + 1) * 128]),
                            reads=pTB, writes=[tqt])
                        if sl == 7 or fj == nf - 1:
                            n = sl + 1
                            def stf(e, i=i, fj=fj, n=n):
                                return [e.dma_start(
                                    out=dst_sc[i * 128:(i + 1) * 128,
                                               (fj - n + 1) * 128:(fj + 1) * 128],
                                    in_=tqt.t[:, 0:n * 128])]
                            P.add("gpsimd", dmas(stf, 1), dma="st", n_dma=1, reads=[tqt],
                                  writes=[dstB])

            ternarize("w1", w1q_sc, w1qB, DH, D, 0)
            ternarize("w2", w2q_sc, w2qB, D, DH, 1)
        P.wait_all("vector")
        P.wait_all("scalar")
        P.wait_all("tensor")
        P.wait_all("sync")
        P.wait_all("gpsimd")

        AS = contextlib.ExitStack()

        def sbA(name, p, f, dt):
            return Buf(AS.enter_context(nc.sbuf_tensor(name, [p, f], dt)), name)

        xa = [sbA(f"xa{i}", 128, NTA * D, F32) for i in range(1)]
        lnT = sbA("lnT", 128, NTA * D, BF16)
        lnF = sbA("lnF", 128, 8 * WA, BF16)
        tfq = sbA("tfq", 128, D, F32)
        tfk = sbA("tfk", 128, D, F32)
        tfs = sbA("tfs", 128, D, F32)
        tfc = sbA("tfc", 128, D, F32)
        tfB = sbA("tfB", 128, D, F32)
        tfBi = sbA("tfBi", 128, D, F32)
        tfe = sbA("tfe", 128, D, F32)
        tfkn = sbA("tfkn", 128, D, F32)
        tf1 = sbA("tf1", 128, D, F32)
        qtm = sbA("qtm", 128, D, BF16)
        bhb = sbA("bhb", 128, D, BF16)
        rsq = sbA("rsq", 128, H, F32)
        gamT = sbA("gamT", 128, NCH * H, F32)
        zf1 = sbA("zf1", 128, WA, BF16)
        zg1 = sbA("zg1", 128, WA, BF16)
        tmVC = sbA("tmVC", 128, NCH * H * 128, BF16)
        tmW = sbA("tmW", 128, NCH * H * HD, BF16)
        hdq = sbA("hdq", 128, HPG * NCH * 128, BF16)
        hdw = sbA("hdw", 128, HPG * NCH * 128, BF16)
        hdc = sbA("hdc", 128, HPG * NCH * 128, BF16)
        hdb = sbA("hdb", 128, HPG * NCH * 128, BF16)
        hdq2 = sbA("hdq2", 128, HPG * NCH * 128, BF16)
        hdw2 = sbA("hdw2", 128, HPG * NCH * 128, BF16)
        hdc2 = sbA("hdc2", 128, HPG * NCH * 128, BF16)
        bCt = sbA("bCt", 128, H * NCH, BF16)
        sgog = sbA("sgog", 128, 8 * WA, BF16)
        sL = [sbA(f"sL_{pp}", 128, 2 * NCH * C, BF16) for pp in range(2)]
        sQl = [[sbA(f"sQl{i}_{pp}", 128, 2 * NCH * C, BF16) for i in range(6)]
               for pp in range(2)]
        sP = [[sbA(f"sP{i}_{pp}", 128, 2 * NCH * C, BF16) for i in range(2)]
              for pp in range(2)]
        sAq = [sbA(f"sAq_{pp}", 128, 2 * NCH * C, BF16) for pp in range(2)]
        sX = [[sbA(f"sX{i}_{pp}", 128, 2 * NCH * 128, BF16) for i in range(2)]
              for pp in range(2)]
        sXp = [sbA(f"sXp_{pp}", 128, 2 * NCH * HD, BF16) for pp in range(2)]
        sQE = [sbA(f"sQE_{pp}", 128, 2 * NCH * HD, BF16) for pp in range(2)]
        oStg = [sbA(f"oStg_{pp}", 128, NCH * 2 * HD, BF16) for pp in range(2)]
        oFM = sbA("oFM", 128, HPG * NCH * 128, BF16)
        oTM = lnT
        og2p = sbA("og2p", 128, WA, BF16)
        mrow = sbA("mrow", 1, 4 * WA, F32)
        onrm = sbA("onrm", 128, 8 * WA, BF16)
        aoF = sbA("aoF", 128, 8 * WA, BF16)
        dTM = sbA("dTM", 128, NTA * D, BF16)
        x1T = xa[0]

        ssB = [Buf(s_state.t, f"ss{h_}") for h_ in range(HPG)]

        def col_hd(hp, ch, b):
            return (hp * NCH + ch) * 128 + b * 64

        def ln_stats(xbuf, ti, center_dst=None):
            xt = lambda: xbuf.t[:, ti * D:(ti + 1) * D]
            P.add("vector", lambda e: e.reduce_sum(out=mu1.t[:, 0:1], in_=xt(), axis=AX.X),
                  reads=[xbuf], writes=[mu1])
            P.add("scalar", lambda e: e.activation(out=x2f.t[:], in_=xt(), func=AF.Square),
                  reads=[xbuf], writes=[x2f])
            P.add("vector", lambda e: e.reduce_sum(out=mu1.t[:, 1:2], in_=x2f.t[:], axis=AX.X),
                  reads=[x2f], writes=[mu1])
            P.add("vector", lambda e: e.tensor_scalar(out=mu1.t[:, 2:3], in0=mu1.t[:, 0:1],
                                                      scalar1=1.0 / D, scalar2=None, op0=ALU.mult),
                  reads=[mu1], writes=[mu1])
            P.add("vector", lambda e: e.tensor_tensor(out=mu1.t[:, 3:4], in0=mu1.t[:, 2:3],
                                                      in1=mu1.t[:, 2:3], op=ALU.mult),
                  reads=[mu1], writes=[mu1])
            P.add("vector", lambda e: e.tensor_scalar(out=mu1.t[:, 4:5], in0=mu1.t[:, 1:2],
                                                      scalar1=1.0 / D, scalar2=None, op0=ALU.mult),
                  reads=[mu1], writes=[mu1])
            P.add("vector", lambda e: e.tensor_tensor(out=mu1.t[:, 4:5], in0=mu1.t[:, 4:5],
                                                      in1=mu1.t[:, 3:4], op=ALU.subtract),
                  reads=[mu1], writes=[mu1])
            P.add("scalar", lambda e: e.activation(out=mu1.t[:, 5:6], in_=mu1.t[:, 4:5],
                                                   func=AF.Sqrt, bias=cflt.t[:, 0:1]),
                  reads=[mu1], writes=[mu1])
            P.add("vector", lambda e: e.reciprocal(out=mu1.t[:, 6:7], in_=mu1.t[:, 5:6]),
                  reads=[mu1], writes=[mu1])
            P.add("vector", lambda e: e.tensor_tensor(out=mu1.t[:, 7:8], in0=mu1.t[:, 2:3],
                                                      in1=mu1.t[:, 6:7], op=ALU.mult),
                  reads=[mu1], writes=[mu1])
            P.add("vector", lambda e: e.tensor_scalar(out=mu1.t[:, 7:8], in0=mu1.t[:, 7:8],
                                                      scalar1=-1.0, scalar2=None, op0=ALU.mult),
                  reads=[mu1], writes=[mu1])

        def tm_to_fm(srcbuf, src_off, dstbuf, W, ti):
            for k in range(8):
                sl = k % 8
                P.add("tensor", lambda e, k=k, sl=sl: e.transpose(
                    pT.t[:, sl * 128:(sl + 1) * 128],
                    srcbuf.t[:, src_off + k * 128:src_off + (k + 1) * 128], c_eye.t[:]),
                    reads=[srcbuf, c_eye], writes=pTB)
            def ev(e):
                return e.tensor_copy(
                    out=dstbuf.t[:].rearrange("p (k c) -> p k c", k=8)[:, :, ti * 128:(ti + 1) * 128],
                    in_=pT.t[:].rearrange("p (a c) -> p a c", a=8))
            P.add("vector", ev, reads=pTB, writes=[dstbuf])

        def fm_ln(srcbuf, dstbuf, W, eps=1e-5):
            def mmsum(e):
                last = None
                for k in range(8):
                    last = e.matmul(pb[5].t[0:1, 0:W], c_ones.t[:],
                                    srcbuf.t[:, k * W:(k + 1) * W],
                                    start=(k == 0), stop=(k == 7))
                return last
            P.add("tensor", mmsum, reads=[srcbuf, c_ones], writes=[pb[5]])
            for k in range(8):
                P.add("scalar", lambda e, k=k: e.activation(out=og2p.t[:, 0:W],
                                                            in_=srcbuf.t[:, k * W:(k + 1) * W],
                                                            func=AF.Square),
                      reads=[srcbuf], writes=[og2p])
                def mm2(e, k=k):
                    return e.matmul(pb[5].t[64:65, 0:W], c_ones.t[:], og2p.t[:, 0:W],
                                    start=(k == 0), stop=(k == 7))
                P.add("tensor", mm2, reads=[og2p, c_ones], writes=[pb[5]])
            P.add("vector", lambda e: e.tensor_scalar(out=mrow.t[0:1, 0:W], in0=pb[5].t[0:1, 0:W],
                                                      scalar1=1.0 / D, scalar2=None, op0=ALU.mult),
                  reads=[pb[5]], writes=[mrow])
            P.add("vector", lambda e: e.tensor_scalar(out=mrow.t[0:1, W:2 * W], in0=pb[5].t[64:65, 0:W],
                                                      scalar1=1.0 / D, scalar2=None, op0=ALU.mult),
                  reads=[pb[5]], writes=[mrow])
            P.add("vector", lambda e: e.tensor_tensor(out=mrow.t[0:1, 2 * W:3 * W],
                                                      in0=mrow.t[0:1, 0:W], in1=mrow.t[0:1, 0:W],
                                                      op=ALU.mult),
                  reads=[mrow], writes=[mrow])
            P.add("vector", lambda e: e.tensor_tensor(out=mrow.t[0:1, W:2 * W],
                                                      in0=mrow.t[0:1, W:2 * W],
                                                      in1=mrow.t[0:1, 2 * W:3 * W], op=ALU.subtract),
                  reads=[mrow], writes=[mrow])
            P.add("scalar", lambda e: e.activation(out=mrow.t[0:1, W:2 * W],
                                                   in_=mrow.t[0:1, W:2 * W], func=AF.Sqrt, bias=cflt.t[0:1, 0:1]),
                  reads=[mrow], writes=[mrow])
            P.add("vector", lambda e: e.reciprocal(out=mrow.t[0:1, W:2 * W],
                                                   in_=mrow.t[0:1, W:2 * W]),
                  reads=[mrow], writes=[mrow])
            def mmbc(e):
                e.matmul(pb[6].t[:, 0:W], c_onesrf.t[:], mrow.t[0:1, 0:W],
                         start=True, stop=True)
                return e.matmul(pb[6].t[:, W:2 * W], c_onesrf.t[:], mrow.t[0:1, W:2 * W],
                                start=True, stop=True)
            P.add("tensor", mmbc, reads=[mrow, c_onesrf], writes=[pb[6]])
            P.add("vector", lambda e: e.tensor_tensor(
                out=dstbuf.t[:].rearrange("p (k c) -> p k c", k=8),
                in0=srcbuf.t[:].rearrange("p (k c) -> p k c", k=8),
                in1=pb[6].t[:, 0:W][:, None, :].broadcast_to((128, 8, W)), op=ALU.subtract),
                reads=[srcbuf, pb[6]], writes=[dstbuf])
            P.add("vector", lambda e: e.tensor_tensor(
                out=dstbuf.t[:].rearrange("p (k c) -> p k c", k=8),
                in0=dstbuf.t[:].rearrange("p (k c) -> p k c", k=8),
                in1=pb[6].t[:, W:2 * W][:, None, :].broadcast_to((128, 8, W)), op=ALU.mult),
                reads=[dstbuf, pb[6]], writes=[dstbuf])

        def emit_recurrence():
            Wp = 2 * NCH * C
            HC = HPG * NCH * 128

            def mkswap(src, dst):
                def sw(e):
                    return [e.dma_start(out=dst.t[0:64, 0:HC], in_=src.t[64:128, 0:HC]),
                            e.dma_start(out=dst.t[64:128, 0:HC], in_=src.t[0:64, 0:HC])]
                P.add("gpsimd", dmas(sw, 2), dma="st", n_dma=2, reads=[src], writes=[dst])
            mkswap(hdq, hdq2)
            mkswap(hdw, hdw2)
            mkswap(hdc, hdc2)
            def bext(e):
                ds = []
                for s2 in range(2):
                    for b in range(2):
                        src = hdb.t[s2 * 64:(s2 + 1) * 64, :].rearrange(
                            "p (hp ch bb t) -> p hp ch bb t", hp=HPG, ch=NCH, bb=2)[
                            :, :, :, b, 63]
                        dst = bCt.t[b * 64:(b + 1) * 64,
                                    s2 * (HPG * NCH):(s2 + 1) * (HPG * NCH)].rearrange(
                            "p (hp ch) -> p hp ch", ch=NCH)
                        ds.append(e.dma_start(out=dst, in_=src,
                                              allow_slow_non_contiguous=True))
                return ds
            P.add("gpsimd", dmas(bext, 4), dma="st", n_dma=4, reads=[hdb], writes=[bCt])

            def hsel(base, swapped, b, s2):
                return base if (s2 == b) else swapped

            BK = [(pb[0], pb[1], pb[2]), (pb[3], pb[4], pb[5])]

            def st_mmx(hp):
                pp = hp % 2
                bkA, bkB, bkC = BK[pp]
                for ch in range(NCH):
                    for s2 in range(2):
                        uc = (ch * 2 + s2) * C
                        for (bank, lh, lh2, rh, rh2) in (
                                (bkA, hdc, hdc2, hdw, hdw2),
                                (bkB, hdw, hdw2, hdc, hdc2),
                                (bkC, hdw, hdw2, hdq, hdq2)):
                            def mmx(e, bank=bank, lh=lh, lh2=lh2, rh=rh, rh2=rh2,
                                    s2=s2, ch=ch, uc=uc, hp=hp):
                                last = None
                                for b in range(2):
                                    r0 = b * 64
                                    cb = col_hd(hp, ch, b)
                                    lt = hsel(lh, lh2, b, s2)
                                    rt = hsel(rh, rh2, b, s2)
                                    last = e.matmul(bank.t[r0:r0 + 64, uc:uc + C],
                                                    lt.t[r0:r0 + 64, cb:cb + 64],
                                                    rt.t[r0:r0 + 64, cb:cb + 64],
                                                    start=True, stop=True,
                                                    tile_position=(r0, r0))
                                return last
                            P.add("tensor", mmx, reads=[lh, lh2, rh, rh2], writes=[bank])

            def st_mask(hp):
                pp = hp % 2
                bkA, bkB, bkC = BK[pp]
                for (dst, bank, msk) in ((sL[pp], bkA, c_mlow), (sQl[pp][0], bkB, c_mupp),
                                         (sAq[pp], bkC, c_mud)):
                    P.add("vector", lambda e, dst=dst, bank=bank, msk=msk: e.tensor_tensor(
                        out=dst.t[:].rearrange("p (u c) -> p u c", c=C),
                        in0=bank.t[:, 0:Wp].rearrange("p (u c) -> p u c", c=C),
                        in1=msk.t[:][:, None, :].broadcast_to((128, Wp // C, C)), op=ALU.mult),
                        reads=[bank, msk], writes=[dst])

            def st_power(hp, lvl):
                pp = hp % 2
                bkA, bkB, bkC = BK[pp]
                curP = sL[pp] if lvl == 1 else sP[pp][(lvl - 1) % 2]
                curQ, dstQ, dstP = sQl[pp][lvl - 1], sQl[pp][lvl], sP[pp][lvl % 2]
                for u in range(2 * NCH):
                    uc = u * C
                    if lvl < 5:
                        def mmp(e, uc=uc, curP=curP, curQ=curQ, bkA=bkA):
                            e.matmul(bkA.t[0:64, uc:uc + C], curQ.t[0:64, uc:uc + C],
                                     curP.t[0:64, uc:uc + C], start=True, stop=True,
                                     tile_position=(0, 0))
                            return e.matmul(bkA.t[64:128, uc:uc + C],
                                            curQ.t[64:128, uc:uc + C],
                                            curP.t[64:128, uc:uc + C], start=True, stop=True,
                                            tile_position=(64, 64))
                        P.add("tensor", mmp, reads=[curP, curQ], writes=[bkA])
                    def mmq(e, uc=uc, curP=curP, curQ=curQ, bkB=bkB):
                        e.matmul(bkB.t[0:64, uc:uc + C], curP.t[0:64, uc:uc + C],
                                 curQ.t[0:64, uc:uc + C], start=True, stop=True,
                                 tile_position=(0, 0))
                        return e.matmul(bkB.t[64:128, uc:uc + C], curP.t[64:128, uc:uc + C],
                                        curQ.t[64:128, uc:uc + C], start=True, stop=True,
                                        tile_position=(64, 64))
                    P.add("tensor", mmq, reads=[curP, curQ], writes=[bkB])
                if lvl < 5:
                    P.add("scalar", lambda e, dstP=dstP, bkA=bkA: e.copy(
                        out=dstP.t[:, 0:Wp], in_=bkA.t[:, 0:Wp]),
                        reads=[bkA], writes=[dstP])
                P.add("vector", lambda e, dstQ=dstQ, bkB=bkB: e.tensor_copy(
                    out=dstQ.t[:, 0:Wp], in_=bkB.t[:, 0:Wp]),
                    reads=[bkB], writes=[dstQ])

            def st_apply(hp, gen):
                pp = hp % 2
                bkA, bkB, bkC = BK[pp]
                k = 5 - gen
                qk = sQl[pp][k]
                dstX = sX[pp][gen % 2]
                prevX = tmVC if gen == 0 else sX[pp][(gen - 1) % 2]
                for ch in range(NCH):
                    for s2 in range(2):
                        h = 2 * hp + s2
                        uc = (ch * 2 + s2) * C
                        xc = (ch * 2 + s2) * 128
                        def rap(gen=gen, ch=ch, h=h, xc=xc, pp=pp):
                            if gen == 0:
                                return tmVC.t[:, (ch * H + h) * 128:(ch * H + h + 1) * 128]
                            return sX[pp][(gen - 1) % 2].t[:, xc:xc + 128]
                        def mma(e, qk=qk, uc=uc, xc=xc, rap=rap, bkC=bkC):
                            e.matmul(bkC.t[0:64, xc:xc + 128], qk.t[0:64, uc:uc + C],
                                     rap()[0:64, :], start=True, stop=True,
                                     tile_position=(0, 0))
                            return e.matmul(bkC.t[64:128, xc:xc + 128],
                                            qk.t[64:128, uc:uc + C], rap()[64:128, :],
                                            start=True, stop=True, tile_position=(64, 64))
                        P.add("tensor", mma, reads=[qk, prevX], writes=[bkC])
                def evx(e, gen=gen, dstX=dstX, hp=hp, pp=pp, bkC=bkC):
                    if gen == 0:
                        src0 = tmVC.t[:].rearrange("p (ch h c) -> p ch h c", ch=NCH, h=H)[
                            :, :, 2 * hp:2 * hp + 2, :]
                    else:
                        src0 = sX[pp][(gen - 1) % 2].t[:, 0:2 * NCH * 128].rearrange(
                            "p (ch h c) -> p ch h c", ch=NCH, h=2)
                    return e.tensor_tensor(
                        out=dstX.t[:, 0:2 * NCH * 128].rearrange(
                            "p (ch h c) -> p ch h c", ch=NCH, h=2),
                        in0=src0,
                        in1=bkC.t[:, 0:2 * NCH * 128].rearrange(
                            "p (ch h c) -> p ch h c", ch=NCH, h=2),
                        op=ALU.add)
                P.add("vector", evx, reads=[prevX, bkC], writes=[dstX])

            def st_xprod(hp):
                pp = hp % 2
                bkA, bkB, bkC = BK[pp]
                xfin = sX[pp][1]
                for ch in range(NCH):
                    for s2 in range(2):
                        h = 2 * hp + s2
                        xc = (ch * 2 + s2) * 128
                        wc = (ch * H + h) * HD
                        oc = (ch * 2 + s2) * HD
                        def mmxp(e, xc=xc, wc=wc, oc=oc, xfin=xfin, bkA=bkA):
                            e.matmul(bkA.t[0:64, oc:oc + HD],
                                     xfin.t[0:64, xc + 64:xc + 128], tmW.t[0:64, wc:wc + HD],
                                     start=True, stop=True, tile_position=(0, 0))
                            return e.matmul(bkA.t[64:128, oc:oc + HD],
                                            xfin.t[64:128, xc + 64:xc + 128],
                                            tmW.t[64:128, wc:wc + HD],
                                            start=True, stop=True, tile_position=(64, 64))
                        P.add("tensor", mmxp, reads=[xfin, tmW], writes=[bkA])
                P.add("vector", lambda e, pp=pp, bkA=bkA: e.tensor_tensor(
                    out=sXp[pp].t[:].rearrange("p (u c) -> p u c", c=C),
                    in0=bkA.t[:, 0:2 * NCH * HD].rearrange("p (u c) -> p u c", c=C),
                    in1=c_ipair.t[:][:, None, :].broadcast_to((128, 2 * NCH, C)), op=ALU.add),
                    reads=[bkA, c_ipair], writes=[sXp[pp]])

            def st_state(hp, ch):
                pp = hp % 2
                bkA, bkB, bkC = BK[pp]
                xfin = sX[pp][1]
                ss = ssB[hp]
                for s2 in range(2):
                    h = 2 * hp + s2
                    uc = (ch * 2 + s2) * C
                    xc = (ch * 2 + s2) * 128
                    oc = (ch * 2 + s2) * HD
                    def mmE(e, uc=uc, xc=xc, oc=oc, xfin=xfin, bkA=bkA, pp=pp):
                        last = None
                        for b in range(2):
                            r0 = b * 64
                            last = e.matmul(bkA.t[r0:r0 + 64, oc:oc + HD],
                                            xfin.t[r0:r0 + 64, xc + 64:xc + 128],
                                            sAq[pp].t[r0:r0 + 64, uc:uc + C],
                                            start=True, stop=True, tile_position=(r0, r0))
                        return last
                    P.add("tensor", mmE, reads=[xfin, sAq[pp]], writes=[bkA])
                    for b in range(2):
                        r0 = b * 64
                        cb = col_hd(hp, ch, b)
                        qsrc = hsel(hdq, hdq2, b, s2)
                        def evqe(e, r0=r0, cb=cb, oc=oc, qsrc=qsrc, pp=pp, bkA=bkA):
                            return e.tensor_tensor(
                                out=sQE[pp].t[r0:r0 + 64, oc:oc + HD],
                                in0=qsrc.t[r0:r0 + 64, cb:cb + 64],
                                in1=bkA.t[r0:r0 + 64, oc:oc + HD], op=ALU.add)
                        P.add("vector", evqe, reads=[qsrc, bkA], writes=[sQE[pp]])
                    def mmF(e, uc=uc, xc=xc, oc=oc, xfin=xfin, bkB=bkB, pp=pp):
                        last = None
                        for b in range(2):
                            r0 = b * 64
                            last = e.matmul(bkB.t[r0:r0 + 64, oc:oc + HD],
                                            xfin.t[r0:r0 + 64, xc:xc + 64],
                                            sAq[pp].t[r0:r0 + 64, uc:uc + C],
                                            start=True, stop=False, tile_position=(r0, r0))
                        return last
                    P.add("tensor", mmF, reads=[xfin, sAq[pp]], writes=[bkB])
                    def mmO2(e, oc=oc, h=h, ss=ss, bkB=bkB, pp=pp):
                        last = None
                        for b in range(2):
                            r0 = b * 64
                            last = e.matmul(bkB.t[r0:r0 + 64, oc:oc + HD],
                                            s_state.t[r0:r0 + 64, h * HD:(h + 1) * HD],
                                            sQE[pp].t[r0:r0 + 64, oc:oc + HD],
                                            start=False, stop=True, tile_position=(r0, r0))
                        return last
                    P.add("tensor", mmO2, reads=[ss, sQE[pp]], writes=[bkB])
                    def mmQb(e, ch=ch, h=h, xc=xc, oc=oc, xfin=xfin, bkC=bkC):
                        last = None
                        for b in range(2):
                            r0 = b * 64
                            last = e.matmul(bkC.t[r0:r0 + 64, oc:oc + HD],
                                            tmW.t[r0:r0 + 64, (ch * H + h) * HD:
                                                  (ch * H + h + 1) * HD],
                                            xfin.t[r0:r0 + 64, xc:xc + 64],
                                            start=True, stop=False, tile_position=(r0, r0))
                        return last
                    P.add("tensor", mmQb, reads=[tmW, xfin], writes=[bkC])
                    def mmZ(e, oc=oc, h=h, ss=ss, bkC=bkC, pp=pp):
                        last = None
                        for b in range(2):
                            r0 = b * 64
                            last = e.matmul(bkC.t[r0:r0 + 64, oc:oc + HD],
                                            sXp[pp].t[r0:r0 + 64, oc:oc + HD],
                                            s_state.t[r0:r0 + 64, h * HD:(h + 1) * HD],
                                            start=False, stop=True, tile_position=(r0, r0))
                        return last
                    P.add("tensor", mmZ, reads=[sXp[pp], ss], writes=[bkC])
                def evo(e, ch=ch, pp=pp, bkB=bkB):
                    return e.tensor_copy(out=oStg[pp].t[:, ch * 128:(ch + 1) * 128],
                                         in_=bkB.t[:, ch * 128:(ch + 1) * 128])
                P.add("vector", evo, reads=[bkB], writes=[oStg[pp]])
                def evs(e, ch=ch, hp=hp, bkC=bkC):
                    return e.tensor_tensor(
                        out=s_state.t[:, 2 * hp * HD:(2 * hp + 2) * HD].rearrange(
                            "p (s d) -> p s d", s=2),
                        in0=bkC.t[:, ch * 128:(ch + 1) * 128].rearrange(
                            "p (s d) -> p s d", s=2),
                        in1=bCt.t[:].rearrange("p (s hp c) -> p s hp c", s=2, c=NCH)[
                            :, :, hp, ch][:, :, None].broadcast_to((128, 2, HD)),
                        op=ALU.mult)
                P.add("vector", evs, reads=[bkC, bCt], writes=[ss])
                for s2 in range(2):
                    for b in range(2):
                        r0 = b * 64
                        def mto(e, r0=r0, ch=ch, s2=s2, hp=hp, pp=pp):
                            return e.transpose(
                                pT.t[r0:r0 + 64, (2 * hp + s2) * 64:(2 * hp + s2 + 1) * 64],
                                oStg[pp].t[r0:r0 + 64,
                                           (ch * 2 + s2) * HD:(ch * 2 + s2 + 1) * HD],
                                c_eye.t[r0:r0 + 64, r0:r0 + 64])
                        P.add("tensor", mto, reads=[oStg[pp], c_eye], writes=[pT_p[pp]])
                def evoT(e, ch=ch, hp=hp, pp=pp):
                    return e.tensor_copy(
                        out=oTM.t[:, ch * D + 2 * hp * HD:ch * D + (2 * hp + 2) * HD],
                        in_=pT.t[:, (2 * hp) * 64:(2 * hp + 2) * 64])
                P.add("vector", evoT, reads=[pT_p[pp]], writes=[oTM])

            stages = ([st_mmx, st_mask]
                      + [lambda hp, lvl=lvl: st_power(hp, lvl) for lvl in range(1, 6)]
                      + [lambda hp, gen=gen: st_apply(hp, gen) for gen in range(6)]
                      + [st_xprod]
                      + [lambda hp, ch=ch: st_state(hp, ch) for ch in range(NCH)])
            import os as _os
            if _os.environ.get("REC_SERIAL", "0") == "1":
                for hp_ in range(HPG):
                    for stg in stages:
                        stg(hp_)
            else:
                for pair in range(HPG // 2):
                    for stg in stages:
                        stg(2 * pair)
                        stg(2 * pair + 1)
            for m in range(NTA):
                tm_to_fm(oTM, m * D, oFM, WA, m)
        P.add("vector", lambda e: e.memset(s_state.t[:], 0.0), writes=ssB)
        P.add("vector", lambda e: e.memset(zf1.t[64:128, :], 0.0), writes=[zf1])
        P.add("vector", lambda e: e.memset(zg1.t[64:128, :], 0.0), writes=[zg1])

        def a_supertile(st):
            xbuf = xa[0]
            rb = st * ST_A

            def ldx(e):
                ds = []
                for b in range(NB):
                    r0 = b * T + rb
                    ds.append(e.dma_start(
                        out=xbuf.t[b * 64:(b + 1) * 64, :].rearrange("p (t d) -> p t d", t=NTA),
                        in_=x_in[r0:r0 + NTA * 64, :].rearrange("(t p) d -> p t d", p=64)))
                return ds
            P.add("sync", dmas(ldx, NB), dma="ld", n_dma=NB, writes=[xbuf])

            for ti in range(NTA):
                ln_stats(xbuf, ti)
                P.add("vector", lambda e, ti=ti: e.tensor_scalar(
                    out=lnT.t[:, ti * D:(ti + 1) * D], in0=xbuf.t[:, ti * D:(ti + 1) * D],
                    scalar1=mu1.t[:, 6:7], scalar2=mu1.t[:, 7:8],
                    op0=ALU.mult, op1=ALU.add),
                    reads=[xbuf, mu1], writes=[lnT])
                tm_to_fm(lnT, ti * D, lnF, WA, ti)

            for (wb, dst) in ((w_f1, zf1), (w_og1, zg1)):
                def mmz(e, wb=wb):
                    last = None
                    for k in range(8):
                        last = e.matmul(pb[2].t[0:64, 0:WA], wb.t[:, k * HD:(k + 1) * HD],
                                        lnF.t[:, k * WA:(k + 1) * WA],
                                        start=(k == 0), stop=(k == 7))
                    return last
                P.add("tensor", mmz, reads=[lnF, wb], writes=[pb[2]])
                P.add("vector", lambda e, dst=dst: e.tensor_copy(out=dst.t[0:64, 0:WA],
                                                                 in_=pb[2].t[0:64, 0:WA]),
                      reads=[pb[2]], writes=[dst])

            for mo in range(8):
                bank = pb[mo % 2]
                def mmo(e, mo=mo, bank=bank):
                    return e.matmul(bank.t[:, 0:WA], w_og2.t[:, mo * 128:(mo + 1) * 128],
                                    zg1.t[:, 0:WA], start=True, stop=True)
                P.add("tensor", mmo, reads=[zg1, w_og2], writes=[bank])
                P.add("scalar", lambda e, mo=mo, bank=bank: e.activation(
                    out=sgog.t[:, mo * WA:(mo + 1) * WA], in_=bank.t[:, 0:WA], func=AF.Sigmoid),
                    reads=[bank], writes=[sgog])

            for m in range(NTA):
                ch = m
                for (wb, dst, act) in ((w_q, tfq, AF.Silu), (w_k, tfk, AF.Silu)):
                    for nn in range(2):
                        bank = pb[nn]
                        def mmp_(e, m=m, nn=nn, wb=wb, bank=bank):
                            last = None
                            for k in range(8):
                                last = e.matmul(bank.t[:, 0:512],
                                                lnF.t[:, k * WA + m * 128:k * WA + (m + 1) * 128],
                                                wb.t[:, k * D + nn * 512:k * D + (nn + 1) * 512],
                                                start=(k == 0), stop=(k == 7))
                            return last
                        P.add("tensor", mmp_, reads=[lnF, wb], writes=[bank])
                        P.add("scalar", lambda e, nn=nn, bank=bank, dst=dst, act=act: e.activation(
                            out=dst.t[:, nn * 512:(nn + 1) * 512], in_=bank.t[:, 0:512], func=act),
                            reads=[bank], writes=[dst])
                for nn in range(2):
                    bank = pb[nn]
                    def mmv(e, m=m, nn=nn, bank=bank):
                        last = None
                        for k in range(8):
                            last = e.matmul(bank.t[:, 0:512],
                                            lnF.t[:, k * WA + m * 128:k * WA + (m + 1) * 128],
                                            w_v.t[:, k * D + nn * 512:k * D + (nn + 1) * 512],
                                            start=(k == 0), stop=(k == 7))
                        return last
                    P.add("tensor", mmv, reads=[lnF, w_v], writes=[bank])
                    def evv(e, m=m, nn=nn, bank=bank):
                        return e.tensor_copy(
                            out=tmVC.t[:].rearrange("p (u c) -> p u c", c=128)[
                                :, m * H + nn * 8:m * H + (nn + 1) * 8, 0:64],
                            in_=bank.t[:, 0:512].rearrange("p (h c) -> p h c", c=64))
                    P.add("vector", evv, reads=[bank], writes=[tmVC])
                for nn in range(2):
                    bank = pb[nn]
                    def mmf2(e, m=m, nn=nn, bank=bank):
                        return e.matmul(bank.t[:, 0:512], zf1.t[:, m * 128:(m + 1) * 128],
                                        w_f2.t[:, nn * 512:(nn + 1) * 512],
                                        start=True, stop=True)
                    P.add("tensor", mmf2, reads=[zf1, w_f2], writes=[bank])
                    P.add("scalar", lambda e, nn=nn, bank=bank: e.activation(
                        out=tfe.t[:, nn * 512:(nn + 1) * 512], in_=bank.t[:, 0:512],
                        func=AF.Sigmoid),
                        reads=[bank], writes=[tfe])
                P.add("scalar", lambda e: e.activation(out=tfs.t[:], in_=tfe.t[:], func=AF.Ln),
                      reads=[tfe], writes=[tfs])
                bank = pb[2]
                def mmg(e, m=m, bank=bank):
                    last = None
                    for k in range(8):
                        last = e.matmul(bank.t[:, 0:H],
                                        lnF.t[:, k * WA + m * 128:k * WA + (m + 1) * 128],
                                        w_gam.t[:, k * H:(k + 1) * H],
                                        start=(k == 0), stop=(k == 7))
                    return last
                P.add("tensor", mmg, reads=[lnF, w_gam], writes=[bank])
                P.add("scalar", lambda e, m=m, bank=bank: e.activation(
                    out=gamT.t[:, m * H:(m + 1) * H], in_=bank.t[:, 0:H], func=AF.Sigmoid),
                    reads=[bank], writes=[gamT])
                P.add("vector", lambda e, m=m: e.tensor_scalar(
                    out=gamT.t[:, m * H:(m + 1) * H], in0=gamT.t[:, m * H:(m + 1) * H],
                    scalar1=-1.0, scalar2=None, op0=ALU.mult),
                    reads=[gamT], writes=[gamT])
                P.add("vector", lambda e: e.tensor_copy(out=qtm.t[:], in_=tfs.t[:]),
                      reads=[tfs], writes=[qtm])
                P.add("vector", lambda e: e.tensor_tensor(out=bhb.t[:], in0=tfs.t[:],
                                                          in1=qtm.t[:], op=ALU.subtract),
                      reads=[tfs, qtm], writes=[bhb])
                for nn in range(2):
                    bank = pb[nn]
                    def mmc(e, nn=nn, bank=bank):
                        e.matmul(bank.t[:, 0:512], c_cum.t[:], qtm.t[:, nn * 512:(nn + 1) * 512],
                                 start=True, stop=False)
                        return e.matmul(bank.t[:, 0:512], c_cum.t[:],
                                        bhb.t[:, nn * 512:(nn + 1) * 512], start=False, stop=True)
                    P.add("tensor", mmc, reads=[qtm, bhb, c_cum], writes=[bank])
                    P.add("vector", lambda e, nn=nn, bank=bank: e.tensor_copy(
                        out=tfc.t[:, nn * 512:(nn + 1) * 512], in_=bank.t[:, 0:512]),
                        reads=[bank], writes=[tfc])
                P.add("scalar", lambda e: e.activation(out=tfB.t[:], in_=tfc.t[:], func=AF.Exp),
                      reads=[tfc], writes=[tfB])
                P.add("scalar", lambda e: e.activation(out=tfBi.t[:], in_=tfc.t[:], func=AF.Exp,
                                                       scale=cflt.t[:, 1:2]),
                      reads=[tfc, cflt], writes=[tfBi])
                P.add("scalar", lambda e: e.activation(out=x2f.t[:], in_=tfk.t[:], func=AF.Square),
                      reads=[tfk], writes=[x2f])
                P.add("vector", lambda e: e.reduce_sum(
                    out=rsq.t[:], in_=x2f.t[:].rearrange("p (h d) -> p h d", h=H), axis=AX.X),
                    reads=[x2f], writes=[rsq])
                P.add("scalar", lambda e: e.activation(out=rsq.t[:], in_=rsq.t[:], func=AF.Sqrt),
                      reads=[rsq], writes=[rsq])
                P.add("vector", lambda e: e.tensor_scalar(out=rsq.t[:], in0=rsq.t[:],
                                                          scalar1=1e-12, scalar2=None, op0=ALU.max),
                      reads=[rsq], writes=[rsq])
                P.add("vector", lambda e: e.reciprocal(out=rsq.t[:], in_=rsq.t[:]),
                      reads=[rsq], writes=[rsq])
                P.add("vector", lambda e: e.tensor_tensor(
                    out=tfkn.t[:].rearrange("p (h d) -> p h d", h=H),
                    in0=tfk.t[:].rearrange("p (h d) -> p h d", h=H),
                    in1=rsq.t[:, :, None].broadcast_to((128, H, HD)), op=ALU.mult),
                    reads=[tfk, rsq], writes=[tfkn])
                P.add("vector", lambda e, ch=ch: e.tensor_tensor(
                    out=tmW.t[:].rearrange("p (u d) -> p u d", d=HD)[:, ch * H:(ch + 1) * H, :],
                    in0=tfkn.t[:].rearrange("p (h d) -> p h d", h=H),
                    in1=tfBi.t[:].rearrange("p (h d) -> p h d", h=H), op=ALU.mult),
                    reads=[tfkn, tfBi], writes=[tmW])
                P.add("vector", lambda e: e.tensor_tensor(out=tf1.t[:], in0=tfe.t[:], in1=tfB.t[:],
                                                          op=ALU.mult),
                      reads=[tfe, tfB], writes=[tf1])
                P.add("vector", lambda e: e.tensor_tensor(out=tf1.t[:], in0=tf1.t[:], in1=tfkn.t[:],
                                                          op=ALU.mult),
                      reads=[tf1, tfkn], writes=[tf1])
                P.add("vector", lambda e, ch=ch, m=m: e.tensor_tensor(
                    out=tmVC.t[:].rearrange("p (u c) -> p u c", c=128)[
                        :, ch * H:(ch + 1) * H, 64:128],
                    in0=tf1.t[:].rearrange("p (h d) -> p h d", h=H),
                    in1=gamT.t[:, m * H:(m + 1) * H][:, :, None].broadcast_to((128, H, HD)),
                    op=ALU.mult),
                    reads=[tf1, gamT], writes=[tmVC])
                P.add("vector", lambda e: e.tensor_tensor(out=qtm.t[:], in0=tfq.t[:], in1=tfB.t[:],
                                                          op=ALU.mult),
                      reads=[tfq, tfB], writes=[qtm])
                P.add("vector", lambda e: e.tensor_copy(out=bhb.t[:], in_=tfB.t[:]),
                      reads=[tfB], writes=[bhb])
                for (srcfn, srcbuf, dst) in (
                        (lambda hp: qtm.t[:, hp * 128:(hp + 1) * 128], qtm, hdq),
                        (lambda hp, ch=ch: tmW.t[:, (ch * H + 2 * hp) * HD:
                                                 (ch * H + 2 * hp + 2) * HD], tmW, hdw),
                        (lambda hp: bhb.t[:, hp * 128:(hp + 1) * 128], bhb, hdb)):
                    for hp in range(HPG):
                        sl = hp % 8
                        P.add("tensor", lambda e, srcfn=srcfn, hp=hp, sl=sl: e.transpose(
                            pT.t[:, sl * 128:(sl + 1) * 128], srcfn(hp), c_eye.t[:]),
                            reads=[srcbuf, c_eye], writes=pTB)
                    def evt(e, dst=dst, ch=ch):
                        return e.tensor_copy(
                            out=dst.t[:].rearrange("p (hp ch c) -> p hp ch c", hp=HPG, ch=NCH)[
                                :, :, ch, :],
                            in_=pT.t[:].rearrange("p (a c) -> p a c", a=8))
                    P.add("vector", evt, reads=pTB, writes=[dst])
                for hp in range(HPG):
                    for s2 in range(2):
                        hcm = 2 * hp + s2
                        P.add("tensor", lambda e, hp=hp, s2=s2, hcm=hcm, ch=ch: e.transpose(
                            pT.t[s2 * 64:s2 * 64 + 64, hp * 128:(hp + 1) * 128],
                            tmVC.t[:, (ch * H + hcm) * 128 + 64:(ch * H + hcm) * 128 + 128],
                            c_eye.t[:]),
                            reads=[tmVC, c_eye], writes=pTB)
                def evtc(e, ch=ch):
                    return e.tensor_copy(
                        out=hdc.t[:].rearrange("p (hp ch c) -> p hp ch c", hp=HPG, ch=NCH)[
                            :, :, ch, :],
                        in_=pT.t[:].rearrange("p (a c) -> p a c", a=8))
                P.add("vector", evtc, reads=pTB, writes=[hdc])

            emit_recurrence()

            P.add("vector", lambda e: e.tensor_tensor(out=oFM.t[:], in0=oFM.t[:], in1=sgog.t[:],
                                                      op=ALU.mult),
                  reads=[oFM, sgog], writes=[oFM])
            fm_ln(oFM, onrm, WA)
            def emit_ldo(mo):
                wslot = (mo % 2) * 1024
                def ldo(e, mo=mo, wslot=wslot):
                    return [e.dma_start(out=w_ow.t[:, wslot:wslot + 1024],
                                        in_=woq_sc[mo * 128:(mo + 1) * 128, :])]
                P.add("sync", dmas(ldo, 1), dma="ld", n_dma=1, reads=[woqB],
                      writes=[w_ow_p[mo % 2]])

            emit_ldo(0)
            for mo in range(8):
                if mo + 1 < 8:
                    emit_ldo(mo + 1)
                wslot = (mo % 2) * 1024
                bank = pb[mo % 2]
                def mmwo(e, mo=mo, wslot=wslot, bank=bank):
                    last = None
                    for k in range(8):
                        last = e.matmul(bank.t[:, 0:WA],
                                        w_ow.t[:, wslot + k * 128:wslot + (k + 1) * 128],
                                        onrm.t[:, k * WA:(k + 1) * WA],
                                        start=(k == 0), stop=(k == 7))
                    return last
                P.add("tensor", mmwo, reads=[onrm, w_ow_p[mo % 2]], writes=[bank])
                P.add("scalar", lambda e, mo=mo, bank=bank: e.activation(
                    out=aoF.t[:, mo * WA:(mo + 1) * WA], in_=bank.t[:, 0:WA], func=AF.Copy),
                    reads=[bank], writes=[aoF])
            fm_ln(aoF, onrm, WA)
            for m in range(NTA):
                for k in range(8):
                    sl = k % 8
                    P.add("tensor", lambda e, m=m, k=k, sl=sl: e.transpose(
                        pT.t[:, sl * 128:(sl + 1) * 128],
                        onrm.t[:, k * WA + m * 128:k * WA + (m + 1) * 128], c_eye.t[:]),
                        reads=[onrm, c_eye], writes=pTB)
                def evd(e, m=m):
                    return e.tensor_copy(
                        out=dTM.t[:].rearrange("p (m k c) -> p m k c", m=NTA, k=8)[:, m, :, :],
                        in_=pT.t[:].rearrange("p (a c) -> p a c", a=8))
                P.add("vector", evd, reads=pTB, writes=[dTM])
            P.add("vector", lambda e: e.tensor_tensor(out=x1T.t[:], in0=xbuf.t[:], in1=dTM.t[:],
                                                      op=ALU.add),
                  reads=[dTM], writes=[x1T])

            def stx1(e, rb=rb):
                ds = []
                for b in range(NB):
                    r0 = b * T + rb
                    ds.append(e.dma_start(
                        out=x1_sc[r0:r0 + NTA * 64, :].rearrange("(t p) d -> p t d", p=64),
                        in_=x1T.t[b * 64:(b + 1) * 64, :].rearrange("p (t d) -> p t d", t=NTA)))
                return ds
            P.add("gpsimd", dmas(stx1, NB), dma="st", n_dma=NB, reads=[x1T],
                  writes=[x1B])

        for st in range(NST_A):
            a_supertile(st)

        def tap(name, buf, rows, cols):
            if name in dbg_outs:
                def stp(e):
                    return [e.dma_start(out=dbg_outs[name], in_=buf.t[0:rows, 0:cols])]
                P.add("gpsimd", dmas(stp, 1), dma="st", reads=[buf])

        tap("lnT", lnT, 128, NTA * D)
        tap("lnF", lnF, 128, 8 * WA)
        tap("tfq", tfq, 128, D)
        tap("tfs", tfs, 128, D)
        tap("tfc", tfc, 128, D)
        tap("tmW", tmW, 128, NCH * H * HD)
        tap("tmVC", tmVC, 128, NCH * H * 128)
        tap("hdq", hdq, 128, HPG * NCH * 128)
        tap("hdw", hdw, 128, HPG * NCH * 128)
        tap("hdb", hdb, 128, HPG * NCH * 128)
        tap("oFM", oFM, 128, HPG * NCH * 128)
        tap("sgog", sgog, 128, 8 * WA)
        tap("aoF", aoF, 128, 8 * WA)
        tap("x1T", x1T, 128, NTA * D)
        tap("state", s_state, 128, HPG * 2 * HD)
        tap("sL", sL[1], 128, 2 * NCH * C)
        tap("sQ0", sQl[1][0], 128, 2 * NCH * C)
        tap("sQ5", sQl[1][5], 128, 2 * NCH * C)
        tap("sAq", sAq[1], 128, 2 * NCH * C)
        tap("xfin", sX[1][1], 128, 2 * NCH * 128)
        tap("sXp", sXp[1], 128, 2 * NCH * HD)
        tap("sQE", sQE[1], 128, 2 * NCH * HD)
        tap("oStg", oStg[1], 128, NCH * 2 * HD)
        tap("bCt", bCt, 128, H * NCH)
        tap("onrm", onrm, 128, 8 * WA)
        tap("hdc", hdc, 128, HPG * NCH * 128)

        def tapB(name, buf, rows, cols):
            if name in dbg_outs:
                def stp(e):
                    return [e.dma_start(out=dbg_outs[name], in_=buf.t[0:rows, 0:cols])]
                P.add("gpsimd", dmas(stp, 1), dma="st", reads=[buf])
            return name in dbg_outs

        for _eng in ("vector", "scalar", "tensor", "sync", "gpsimd"):
            P.wait_all(_eng)
        AS.close()
        WS.close()

        x1b = sb("x1b", 128, NTB * D, F32)
        x1s = sb("x1s", 128, NTB * D, F32)
        xqT = sb("xqT", 128, NTB * D, BF16)
        xcT = sb("xcT", 128, D, F32)
        drow = sb("drow", 128, NTB * 2, F32)
        xqF = sb("xqF", 128, 8 * WB, BF16)
        w1w = sb("w1w", 128, 2 * 8 * 128, BF16)
        w1w_p = [w1w, Buf(w1w.t, "w1w_odd")]
        w2w = sb("w2w", 128, 2 * 32 * 128, BF16)
        w2w_p = [w2w, Buf(w2w.t, "w2w_odd")]
        hF = sb("hF", 128, 2 * 2 * HBW, F32)
        hF_p = [hF, Buf(hF.t, "hF_odd")]
        hhalo = sb("hhalo", 128, 32 * 2 * 2, F32)
        hhalo_p = [hhalo, Buf(hhalo.t, "hhalo_odd")]
        ac0 = sb("ac0", 128, WB, F32)
        ac1 = sb("ac1", 128, WB, F32)
        ac0b = sb("ac0b", 128, WB, F32)
        ac1b = sb("ac1b", 128, WB, F32)
        yfb = sb("yfb", 128, WB, F32)
        mxp2 = sb("mxp2", 128, WB, F32)
        hprevF = sb("hprevF", 128, 32 * 2 * 2, F32)
        yf = sb("yf", 128, WB, F32)
        yld = sb("yld", 128, 2 * 512, F32)
        yld_p = [yld, Buf(yld.t, "yld_odd")]
        ivh = sb("ivh", 128, WB, I32)
        ivhb = sb("ivhb", 128, WB, I32)
        hq = sb("hq", 128, 32 * WB, BF16)
        hq_p = [hq, Buf(hq.t, "hq_odd")]
        ymx = sb("ymx", 128, WB, F32)
        mxp = sb("mxp", 128, WB, F32)
        outFf = sb("outFf", 128, 8 * WB, F32)
        yrow = sb("yrow", 1, 4 * WB, F32)
        dscb = sb("dscb", 128, 3 * WB, F32)
        outF = sb("outF", 128, 8 * WB, BF16)
        o2pB = sb("o2pB", 128, WB, BF16)
        mrowB = yrow
        bcA2 = ymx
        bcB2 = xcT
        onB = xqF
        d2TM = xqT
        x2T = x1b

        P.add("vector", lambda e: e.memset(hhalo.t[:], 0.0), writes=[hhalo])

        def fm_ln_b(srcbuf, srcnorm, dstbuf, W):
            def mmsum(e):
                last = None
                for k in range(8):
                    last = e.matmul(pb[5].t[0:1, 0:W], c_ones.t[:],
                                    srcbuf.t[:, k * W:(k + 1) * W], start=(k == 0), stop=(k == 7))
                return last
            P.add("tensor", mmsum, reads=[srcbuf, c_ones], writes=[pb[5]])
            for k in range(8):
                P.add("scalar", lambda e, k=k: e.activation(out=o2pB.t[:, 0:W],
                                                            in_=srcbuf.t[:, k * W:(k + 1) * W],
                                                            func=AF.Square),
                      reads=[srcbuf], writes=[o2pB])
                def mm2(e, k=k):
                    return e.matmul(pb[5].t[64:65, 0:W], c_ones.t[:], o2pB.t[:, 0:W],
                                    start=(k == 0), stop=(k == 7))
                P.add("tensor", mm2, reads=[o2pB, c_ones], writes=[pb[5]])
            P.add("vector", lambda e: e.tensor_scalar(out=mrowB.t[0:1, 0:W], in0=pb[5].t[0:1, 0:W],
                                                      scalar1=1.0 / D, scalar2=None, op0=ALU.mult),
                  reads=[pb[5]], writes=[mrowB])
            P.add("vector", lambda e: e.tensor_scalar(out=mrowB.t[0:1, W:2 * W],
                                                      in0=pb[5].t[64:65, 0:W],
                                                      scalar1=1.0 / D, scalar2=None, op0=ALU.mult),
                  reads=[pb[5]], writes=[mrowB])
            P.add("vector", lambda e: e.tensor_tensor(out=mrowB.t[0:1, 2 * W:3 * W],
                                                      in0=mrowB.t[0:1, 0:W], in1=mrowB.t[0:1, 0:W],
                                                      op=ALU.mult),
                  reads=[mrowB], writes=[mrowB])
            P.add("vector", lambda e: e.tensor_tensor(out=mrowB.t[0:1, W:2 * W],
                                                      in0=mrowB.t[0:1, W:2 * W],
                                                      in1=mrowB.t[0:1, 2 * W:3 * W],
                                                      op=ALU.subtract),
                  reads=[mrowB], writes=[mrowB])
            P.add("scalar", lambda e: e.activation(out=mrowB.t[0:1, W:2 * W],
                                                   in_=mrowB.t[0:1, W:2 * W], func=AF.Sqrt,
                                                   bias=cflt.t[0:1, 0:1]),
                  reads=[mrowB], writes=[mrowB])
            P.add("vector", lambda e: e.reciprocal(out=mrowB.t[0:1, W:2 * W],
                                                   in_=mrowB.t[0:1, W:2 * W]),
                  reads=[mrowB], writes=[mrowB])
            def mmbc(e):
                e.matmul(pb[6].t[:, 0:W], c_onesrf.t[:], mrowB.t[0:1, 0:W],
                         start=True, stop=True)
                return e.matmul(pb[4].t[:, 0:W], c_onesrf.t[:], mrowB.t[0:1, W:2 * W],
                                start=True, stop=True)
            P.add("tensor", mmbc, reads=[mrowB, c_onesrf], writes=[pb[6], pb[4]])
            P.add("vector", lambda e: e.tensor_tensor(
                out=dstbuf.t[:].rearrange("p (k c) -> p k c", k=8),
                in0=srcnorm.t[:].rearrange("p (k c) -> p k c", k=8),
                in1=pb[6].t[:, 0:W][:, None, :].broadcast_to((128, 8, W)), op=ALU.subtract),
                reads=[srcnorm, pb[6]], writes=[dstbuf])
            P.add("vector", lambda e: e.tensor_tensor(
                out=dstbuf.t[:].rearrange("p (k c) -> p k c", k=8),
                in0=dstbuf.t[:].rearrange("p (k c) -> p k c", k=8),
                in1=pb[4].t[:, 0:W][:, None, :].broadcast_to((128, 8, W)), op=ALU.mult),
                reads=[dstbuf, pb[4]], writes=[dstbuf])

        colbf = sb("colbf", 128, 8, F32)

        def row_from_cols(colbuf, coloff, colstride, n, dst_off):
            if colstride > 1:
                src_ap = colbuf.t[:].rearrange("p (a b) -> p a b", b=colstride)[:, 0:n, coloff]
            else:
                src_ap = colbuf.t[:, coloff:coloff + n]
            P.add("vector", lambda e, src_ap=src_ap: e.tensor_copy(
                out=colbf.t[:, 0:n], in_=src_ap),
                reads=[colbuf], writes=[colbf])
            for ti in range(n):
                P.add("tensor", lambda e, ti=ti: e.transpose(
                    pb[3].t[0:1, ti * 128:(ti + 1) * 128],
                    colbf.t[:, ti:ti + 1], c_eyef.t[:]),
                    reads=[colbf, c_eyef], writes=[pb[3]])
            P.add("vector", lambda e: e.tensor_copy(out=yrow.t[0:1, dst_off:dst_off + n * 128],
                                                    in_=pb[3].t[0:1, 0:n * 128]),
                  reads=[pb[3]], writes=[yrow])

        def bcast_row(src_off, W, dst, dst_off, scale_src=None):
            P.add("tensor", lambda e: e.matmul(
                pb[4].t[:, 0:W], c_onesrf.t[:], yrow.t[0:1, src_off:src_off + W],
                start=True, stop=True), reads=[yrow, c_onesrf], writes=[pb[4]])
            P.add("vector", lambda e: e.tensor_copy(out=dst.t[:, dst_off:dst_off + W],
                                                    in_=pb[4].t[:, 0:W]),
                  reads=[pb[4]], writes=[dst])

        def b_supertile(st):
            rb = st * ST_B

            def ldx1(e):
                ds = []
                for b in range(NB):
                    r0 = b * T + rb
                    ds.append(e.dma_start(
                        out=x1b.t[b * 64:(b + 1) * 64, :].rearrange("p (t d) -> p t d", t=NTB),
                        in_=x1_sc[r0:r0 + NTB * 64, :].rearrange("(t p) d -> p t d", p=64)))
                return ds
            P.add("sync", dmas(ldx1, NB), dma="ld", n_dma=NB, reads=[x1B],
                  writes=[x1b])
            def ldx1s(e, st=st):
                ds = []
                for b in range(NB):
                    p0 = b * 64
                    if st == 0:
                        ds.append(e.dma_start(out=x1s.t[p0 + 1:p0 + 64, 0:D],
                                              in_=x1_sc[b * T:b * T + 63, :]))
                        ds.append(e.dma_start(
                            out=x1s.t[p0:p0 + 64, D:NTB * D].rearrange(
                                "p (t d) -> p t d", t=NTB - 1),
                            in_=x1_sc[b * T + 63:b * T + 63 + (NTB - 1) * 64, :].rearrange(
                                "(t p) d -> p t d", p=64)))
                    else:
                        r0 = b * T + rb - 1
                        ds.append(e.dma_start(
                            out=x1s.t[p0:p0 + 64, :].rearrange("p (t d) -> p t d", t=NTB),
                            in_=x1_sc[r0:r0 + NTB * 64, :].rearrange("(t p) d -> p t d", p=64)))
                return ds
            nst = 4 if st == 0 else 2
            P.add("sync", dmas(ldx1s, nst), dma="ld", n_dma=nst, reads=[x1B],
                  writes=[x1s])

            for ti in range(NTB):
                xt = lambda ti=ti: x1b.t[:, ti * D:(ti + 1) * D]
                P.add("vector", lambda e, xt=xt: e.reduce_sum(out=mu1.t[:, 0:1], in_=xt(), axis=AX.X),
                      reads=[x1b], writes=[mu1])
                P.add("vector", lambda e: e.tensor_scalar(out=mu1.t[:, 2:3], in0=mu1.t[:, 0:1],
                                                          scalar1=1.0 / D, scalar2=None,
                                                          op0=ALU.mult),
                      reads=[mu1], writes=[mu1])
                P.add("vector", lambda e: e.tensor_scalar(out=mu1.t[:, 3:4], in0=mu1.t[:, 2:3],
                                                          scalar1=-1.0, scalar2=None, op0=ALU.mult),
                      reads=[mu1], writes=[mu1])
                P.add("vector", lambda e, xt=xt: e.tensor_scalar(
                    out=xcT.t[:], in0=xt(), scalar1=mu1.t[:, 3:4], scalar2=None, op0=ALU.add),
                    reads=[x1b, mu1], writes=[xcT])
                P.add("scalar", lambda e: e.activation(out=x2f.t[:], in_=xcT.t[:], func=AF.Square),
                      reads=[xcT], writes=[x2f])
                P.add("vector", lambda e: e.reduce_sum(out=mu1.t[:, 4:5], in_=x2f.t[:], axis=AX.X),
                      reads=[x2f], writes=[mu1])
                P.add("vector", lambda e: e.tensor_scalar(out=mu1.t[:, 4:5], in0=mu1.t[:, 4:5],
                                                          scalar1=1.0 / D, scalar2=None,
                                                          op0=ALU.mult),
                      reads=[mu1], writes=[mu1])
                P.add("scalar", lambda e: e.activation(out=mu1.t[:, 5:6], in_=mu1.t[:, 4:5],
                                                       func=AF.Sqrt, bias=cflt.t[:, 0:1]),
                      reads=[mu1], writes=[mu1])
                P.add("vector", lambda e: e.reciprocal(out=mu1.t[:, 6:7], in_=mu1.t[:, 5:6]),
                      reads=[mu1], writes=[mu1])
                P.add("vector", lambda e: e.reduce_max(out=mu1.t[:, 0:1], in_=xcT.t[:], axis=AX.X,
                                                       apply_absolute_value=True),
                      reads=[xcT], writes=[mu1])
                P.add("vector", lambda e: e.tensor_tensor(out=mu1.t[:, 1:2], in0=mu1.t[:, 0:1],
                                                          in1=mu1.t[:, 6:7], op=ALU.mult),
                      reads=[mu1], writes=[mu1])
                P.add("vector", lambda e: e.tensor_scalar(out=mu1.t[:, 1:2], in0=mu1.t[:, 1:2],
                                                          scalar1=1e-5, scalar2=None, op0=ALU.max),
                      reads=[mu1], writes=[mu1])
                P.add("vector", lambda e: e.reciprocal(out=mu1.t[:, 3:4], in_=mu1.t[:, 1:2]),
                      reads=[mu1], writes=[mu1])
                P.add("vector", lambda e: e.tensor_tensor(out=mu1.t[:, 3:4], in0=mu1.t[:, 3:4],
                                                          in1=mu1.t[:, 6:7], op=ALU.mult),
                      reads=[mu1], writes=[mu1])
                P.add("vector", lambda e, ti=ti: e.tensor_scalar(
                    out=drow.t[:, ti * 2:ti * 2 + 1], in0=mu1.t[:, 3:4],
                    scalar1=127.0, scalar2=None, op0=ALU.mult),
                    reads=[mu1], writes=[drow])
                P.add("vector", lambda e: e.tensor_scalar(
                    out=mu1.t[:, 4:5], in0=mu1.t[:, 1:2],
                    scalar1=1.0 / 127.0, scalar2=None, op0=ALU.mult),
                    reads=[mu1], writes=[mu1])
                P.add("vector", lambda e, ti=ti: e.tensor_tensor(
                    out=drow.t[:, ti * 2 + 1:ti * 2 + 2], in0=mu1.t[:, 4:5],
                    in1=swsc.t[:, 2:3], op=ALU.mult),
                    reads=[mu1, swsc], writes=[drow])
                P.add("vector", lambda e, ti=ti: e.tensor_scalar(
                    out=x2f.t[:], in0=xcT.t[:], scalar1=drow.t[:, ti * 2:ti * 2 + 1],
                    scalar2=None, op0=ALU.mult),
                    reads=[xcT, drow], writes=[x2f])
                for hh in range(2):
                    P.add("vector", lambda e, hh=hh: e.tensor_copy(
                        out=ivh.t[:], in_=x2f.t[:, hh * 512:(hh + 1) * 512]),
                        reads=[x2f], writes=[ivh])
                    P.add("vector", lambda e, ti=ti, hh=hh: e.tensor_copy(
                        out=xqT.t[:, ti * D + hh * 512:ti * D + (hh + 1) * 512],
                        in_=ivh.t[:]),
                        reads=[ivh], writes=[xqT])
                tm_to_fm(xqT, ti * D, xqF, WB, ti)

            row_from_cols(drow, 1, 2, NTB, 0)
            bcast_row(0, WB, dscb, 0)

            P.add("vector", lambda e: e.memset(mxp.t[:], 0.0), writes=[mxp])
            P.add("vector", lambda e: e.memset(mxp2.t[:], 0.0), writes=[mxp2])

            def emit_ldw1(tt):
                wslot = (tt % 2) * 1024
                def ldw1(e, tt=tt, wslot=wslot):
                    return [e.dma_start(out=w1w.t[:, wslot:wslot + 1024],
                                        in_=w1q_sc[tt * 128:(tt + 1) * 128, :])]
                P.add("sync", dmas(ldw1, 1), dma="ld", n_dma=1, reads=[w1qB],
                      writes=[w1w_p[tt % 2]])

            def emit_mmh(tt):
                wslot = (tt % 2) * 1024
                bank = pb[tt % 2]
                def mmh(e, tt=tt, wslot=wslot, bank=bank):
                    last = None
                    for k in range(8):
                        last = e.matmul(bank.t[:, 0:WB],
                                        w1w.t[:, wslot + k * 128:wslot + (k + 1) * 128],
                                        xqF.t[:, k * WB:(k + 1) * WB],
                                        start=(k == 0), stop=(k == 7))
                    return last
                P.add("tensor", mmh, reads=[w1w_p[tt % 2], xqF], writes=[bank])
                if st == NST_B - 1:
                    for b in range(2):
                        c0 = 3 * 128 + b * 64 + 62
                        def evhf(e, tt=tt, b=b, bank=bank, c0=c0):
                            return e.tensor_tensor(
                                out=hprevF.t[:, tt * 4 + b * 2:tt * 4 + b * 2 + 2],
                                in0=bank.t[:, c0:c0 + 2], in1=dscb.t[:, c0:c0 + 2],
                                op=ALU.mult)
                        P.add("vector", evhf, reads=[bank, dscb], writes=[hprevF])

            def emit_hev(tt):
                bank = pb[tt % 2]
                hFp, hhp = hF_p[tt % 2], hhalo_p[tt % 2]
                slot = (tt % 2) * (2 * HBW)
                def evhal(e, tt=tt, slot=slot):
                    return e.tensor_copy(
                        out=hF.t[:, slot:slot + 2 * HBW].rearrange(
                            "p (b w) -> p b w", b=2)[:, :, 0:2],
                        in_=hhalo.t[:, tt * 4:tt * 4 + 4].rearrange("p (b c) -> p b c", b=2))
                P.add("vector", evhal, reads=[hhp], writes=[hFp])
                def evh(e, tt=tt, bank=bank, slot=slot):
                    out_ap = hF.t[:, slot:slot + 2 * HBW].rearrange(
                        "p (b w) -> p b w", b=2)[:, :, 2:2 + ST_B].rearrange(
                        "p b (i j) -> p i b j", j=64)
                    return e.tensor_tensor(
                        out=out_ap,
                        in0=bank.t[:, 0:WB].rearrange("p (i b j) -> p i b j", b=2, j=64),
                        in1=dscb.t[:, 0:WB].rearrange("p (i b j) -> p i b j", b=2, j=64),
                        op=ALU.mult)
                P.add("vector", evh, reads=[bank, dscb], writes=[hFp])
                def evsav(e, tt=tt, slot=slot):
                    return e.tensor_copy(
                        out=hhalo.t[:, tt * 4:tt * 4 + 4].rearrange("p (b c) -> p b c", b=2),
                        in_=hF.t[:, slot:slot + 2 * HBW].rearrange(
                            "p (b w) -> p b w", b=2)[:, :, ST_B:ST_B + 2])
                P.add("vector", evsav, reads=[hFp], writes=[hhp])

            def emit_conv(tt):
                hFp = hF_p[tt % 2]
                a0, a1, yfs = (ac0, ac1, yf) if tt % 2 == 0 else (ac0b, ac1b, yfb)
                mxs = mxp if tt % 2 == 0 else mxp2
                slot = (tt % 2) * (2 * HBW)
                def hv(tp, slot=slot):
                    return hF.t[:, slot:slot + 2 * HBW].rearrange(
                        "p (b w) -> p b w", b=2)[:, :, tp:tp + ST_B]
                def cv0(e, tt=tt, slot=slot, a0=a0):
                    return e.tensor_scalar(
                        out=a0.t[:].rearrange("p (b j) -> p b j", b=2),
                        in0=hv(0, slot), scalar1=c_cw.t[:, tt * 3:tt * 3 + 1],
                        scalar2=None, op0=ALU.mult)
                P.add("vector", cv0, reads=[hFp, c_cw], writes=[a0])
                def cv1(e, tt=tt, slot=slot, a0=a0, a1=a1):
                    return e.scalar_tensor_tensor(
                        out=a1.t[:].rearrange("p (b j) -> p b j", b=2),
                        in0=hv(1, slot), scalar=c_cw.t[:, tt * 3 + 1:tt * 3 + 2],
                        in1=a0.t[:].rearrange("p (b j) -> p b j", b=2),
                        op0=ALU.mult, op1=ALU.add)
                P.add("vector", cv1, reads=[hFp, c_cw, a0], writes=[a1])
                def cv2(e, tt=tt, slot=slot, a0=a0, a1=a1):
                    return e.scalar_tensor_tensor(
                        out=a0.t[:].rearrange("p (b j) -> p b j", b=2),
                        in0=hv(2, slot), scalar=c_cw.t[:, tt * 3 + 2:tt * 3 + 3],
                        in1=a1.t[:].rearrange("p (b j) -> p b j", b=2),
                        op0=ALU.mult, op1=ALU.add)
                P.add("vector", cv2, reads=[hFp, c_cw, a1], writes=[a0])
                def evy(e, tt=tt, a0=a0, yfs=yfs):
                    return e.activation(out=yfs.t[:], in_=a0.t[:], func=AF.Silu,
                                        bias=c_cb.t[:, tt:tt + 1])
                P.add("scalar", evy, reads=[a0, c_cb], writes=[yfs])
                def styf(e, tt=tt, yfs=yfs):
                    r0 = tt * 2 * 128
                    return [e.dma_start(out=y_sc[r0:r0 + 128, :], in_=yfs.t[:, 0:256]),
                            e.dma_start(out=y_sc[r0 + 128:r0 + 256, :], in_=yfs.t[:, 256:512])]
                P.add("gpsimd", dmas(styf, 2), dma="st", n_dma=2, reads=[yfs],
                      writes=[yscB4[tt % 4]])
                def evab(e, yfs=yfs, a1=a1):
                    return e.scalar_tensor_tensor(out=a1.t[:], in0=yfs.t[:], scalar=-1.0,
                                                  in1=yfs.t[:], op0=ALU.mult, op1=ALU.max)
                P.add("vector", evab, reads=[yfs], writes=[a1])
                def evmx(e, a1=a1, mxs=mxs):
                    return e.tensor_tensor(out=mxs.t[:], in0=mxs.t[:], in1=a1.t[:],
                                           op=ALU.max)
                P.add("vector", evmx, reads=[a1, mxs], writes=[mxs])

            emit_ldw1(0)
            for tt in range(32):
                if tt + 1 < 32:
                    emit_ldw1(tt + 1)
                emit_mmh(tt)
                if tt >= 1:
                    emit_conv(tt - 1)
                emit_hev(tt)
            emit_conv(31)
            P.add("vector", lambda e: e.tensor_tensor(out=mxp.t[:], in0=mxp.t[:],
                                                      in1=mxp2.t[:], op=ALU.max),
                  reads=[mxp, mxp2], writes=[mxp])

            for blk in range(4):
                P.add("tensor", lambda e, blk=blk: e.transpose(
                    pb[2].t[:, blk * 128:(blk + 1) * 128],
                    mxp.t[:, blk * 128:(blk + 1) * 128], c_eyef.t[:]),
                    reads=[mxp, c_eyef], writes=[pb[2]])
            P.add("vector", lambda e: e.tensor_copy(out=dscb.t[:, WB:2 * WB],
                                                    in_=pb[2].t[:, 0:512]),
                  reads=[pb[2]], writes=[dscb])
            P.add("vector", lambda e: e.reduce_max(
                out=drow.t[:, 0:4], in_=dscb.t[:, WB:2 * WB].rearrange("p (a c) -> p a c", a=4),
                axis=AX.X, apply_absolute_value=True),
                reads=[dscb], writes=[drow])
            row_from_cols(drow, 0, 1, 4, WB)
            P.add("vector", lambda e: e.tensor_scalar(out=yrow.t[0:1, WB:2 * WB],
                                                      in0=yrow.t[0:1, WB:2 * WB],
                                                      scalar1=1e-5, scalar2=None, op0=ALU.max),
                  reads=[yrow], writes=[yrow])
            P.add("vector", lambda e: e.reciprocal(out=yrow.t[0:1, 2 * WB:3 * WB],
                                                   in_=yrow.t[0:1, WB:2 * WB]),
                  reads=[yrow], writes=[yrow])
            P.add("vector", lambda e: e.tensor_scalar(out=yrow.t[0:1, 2 * WB:3 * WB],
                                                      in0=yrow.t[0:1, 2 * WB:3 * WB],
                                                      scalar1=127.0, scalar2=None, op0=ALU.mult),
                  reads=[yrow], writes=[yrow])
            P.add("vector", lambda e: e.tensor_scalar(out=yrow.t[0:1, 3 * WB:4 * WB],
                                                      in0=yrow.t[0:1, WB:2 * WB],
                                                      scalar1=1.0 / 127.0, scalar2=None,
                                                      op0=ALU.mult),
                  reads=[yrow], writes=[yrow])
            bcast_row(2 * WB, WB, dscb, WB)
            P.add("vector", lambda e: e.tensor_copy(
                out=yrow.t[0:1, 0:WB].rearrange("o (i b j) -> o i b j", b=2, j=64),
                in_=yrow.t[0:1, 3 * WB:4 * WB].rearrange("o (b i j) -> o i b j", b=2, j=64)),
                reads=[yrow], writes=[yrow])
            bcast_row(0, WB, dscb, 2 * WB)
            def emit_ldyf(tt):
                ls = (tt % 2) * 512
                def ldyf(e, tt=tt, ls=ls):
                    r0 = tt * 2 * 128
                    return [e.dma_start(out=yld.t[:, ls:ls + 256], in_=y_sc[r0:r0 + 128, :]),
                            e.dma_start(out=yld.t[:, ls + 256:ls + 512],
                                        in_=y_sc[r0 + 128:r0 + 256, :])]
                P.add("sync", dmas(ldyf, 2), dma="ld", n_dma=2, reads=[yscB4[tt % 4]],
                      writes=[yld_p[tt % 2]])

            def emit_q1(tt):
                ls = (tt % 2) * 512
                iv = ivh if tt % 2 == 0 else ivhb
                def q1(e, tt=tt, ls=ls, iv=iv):
                    return e.tensor_tensor(
                        out=iv.t[:],
                        in0=yld.t[:, ls:ls + 512],
                        in1=dscb.t[:, WB:2 * WB],
                        op=ALU.mult)
                P.add("vector", q1, reads=[yld_p[tt % 2], dscb], writes=[iv])

            def emit_q2(tt):
                iv = ivh if tt % 2 == 0 else ivhb
                def q2(e, tt=tt, iv=iv):
                    return e.tensor_copy(
                        out=hq.t[:, tt * WB:(tt + 1) * WB].rearrange(
                            "p (i b j) -> p i b j", b=2, j=64),
                        in_=iv.t[:].rearrange("p (b i j) -> p i b j", b=2, j=64))
                P.add("vector", q2, reads=[iv], writes=[hq_p[tt % 2]])

            emit_ldyf(0)
            for tt in range(32):
                if tt + 1 < 32:
                    emit_ldyf(tt + 1)
                emit_q1(tt)
                if tt >= 1:
                    emit_q2(tt - 1)
            emit_q2(31)
            P.add("vector", lambda e: e.tensor_scalar(out=dscb.t[:, 2 * WB:3 * WB],
                                                      in0=dscb.t[:, 2 * WB:3 * WB],
                                                      scalar1=swsc.t[:, 3:4], scalar2=None,
                                                      op0=ALU.mult),
                  reads=[dscb, swsc], writes=[dscb])
            def emit_ldw2(mo):
                wslot = (mo % 2) * 4096
                def ldw2(e, mo=mo, wslot=wslot):
                    return [e.dma_start(out=w2w.t[:, wslot:wslot + 4096],
                                        in_=w2q_sc[mo * 128:(mo + 1) * 128, :])]
                P.add("sync", dmas(ldw2, 1), dma="ld", n_dma=1, reads=[w2qB],
                      writes=[w2w_p[mo % 2]])

            emit_ldw2(0)
            for mo in range(8):
                if mo + 1 < 8:
                    emit_ldw2(mo + 1)
                wslot = (mo % 2) * 4096
                bank = pb[mo % 2]
                def mmw2(e, mo=mo, wslot=wslot, bank=bank):
                    last = None
                    for k in range(32):
                        last = e.matmul(bank.t[:, 0:WB],
                                        w2w.t[:, wslot + k * 128:wslot + (k + 1) * 128],
                                        hq.t[:, k * WB:(k + 1) * WB],
                                        start=(k == 0), stop=(k == 31))
                    return last
                P.add("tensor", mmw2, reads=[w2w_p[mo % 2], hq, hq_p[1]], writes=[bank])
                def evw2(e, mo=mo, bank=bank):
                    return e.tensor_tensor(out=outF.t[:, mo * WB:(mo + 1) * WB],
                                           in0=bank.t[:, 0:WB], in1=dscb.t[:, 2 * WB:3 * WB],
                                           op=ALU.mult)
                P.add("vector", evw2, reads=[bank, dscb], writes=[outF])
                def evw2f(e, mo=mo, bank=bank):
                    return e.tensor_tensor(out=outFf.t[:, mo * WB:(mo + 1) * WB],
                                           in0=bank.t[:, 0:WB], in1=dscb.t[:, 2 * WB:3 * WB],
                                           op=ALU.mult)
                P.add("vector", evw2f, reads=[bank, dscb], writes=[outFf])

            fm_ln_b(outF, outFf, onB, WB)
            for m in range(NTB):
                for k in range(8):
                    sl = k % 8
                    P.add("tensor", lambda e, m=m, k=k, sl=sl: e.transpose(
                        pT.t[:, sl * 128:(sl + 1) * 128],
                        onB.t[:, k * WB + m * 128:k * WB + (m + 1) * 128], c_eye.t[:]),
                        reads=[onB, c_eye], writes=pTB)
                def evd(e, m=m):
                    return e.tensor_copy(
                        out=d2TM.t[:].rearrange("p (m k c) -> p m k c", m=NTB, k=8)[:, m, :, :],
                        in_=pT.t[:].rearrange("p (a c) -> p a c", a=8))
                P.add("vector", evd, reads=pTB, writes=[d2TM])
            P.add("vector", lambda e: e.tensor_tensor(out=x2T.t[:], in0=x1s.t[:], in1=d2TM.t[:],
                                                      op=ALU.add),
                  reads=[x1s, d2TM], writes=[x2T])

            def sty(e, st=st, rb=rb):
                ds = []
                for b in range(NB):
                    p0 = b * 64
                    if st == 0:
                        ds.append(e.dma_start(out=y_out[b * T + 0:b * T + 63, :],
                                              in_=x2T.t[p0 + 1:p0 + 64, 0:D]))
                        ds.append(e.dma_start(
                            out=y_out[b * T + 63:b * T + 63 + (NTB - 1) * 64, :].rearrange(
                                "(t p) d -> p t d", p=64),
                            in_=x2T.t[p0:p0 + 64, D:NTB * D].rearrange(
                                "p (t d) -> p t d", t=NTB - 1)))
                    else:
                        r0 = b * T + rb - 1
                        ds.append(e.dma_start(
                            out=y_out[r0:r0 + NTB * 64, :].rearrange("(t p) d -> p t d", p=64),
                            in_=x2T.t[p0:p0 + 64, :].rearrange("p (t d) -> p t d", t=NTB)))
                return ds
            nst = 4 if st == 0 else 2
            P.add("gpsimd", dmas(sty, nst), dma="st", n_dma=nst, reads=[x2T])



        if stop_after is None:
            for st in range(NST_B):
                b_supertile(st)
        tapB("Bdscb", dscb, 128, 3 * WB)
        tapB("Bdrow", drow, 128, NTB * 2)
        tapB("Bhq", hq, 128, 32 * WB)
        tapB("BoutF", outF, 128, 8 * WB)
        tapB("BxqF", xqF, 128, 8 * WB)
        tapB("Byrow", yrow, 1, 4 * WB)
        tapB("Bswsc", swsc, 128, 4)

        def final_token():
            for tp2 in range(2):
                def cv(e, tp2=tp2):
                    return e.tensor_tensor(
                        out=ymx.t[:, 64 + tp2 * 64:128 + tp2 * 64].rearrange(
                            "p (t b) -> p t b", b=2),
                        in0=hprevF.t[:].rearrange("p (t b c) -> p t b c", b=2, c=2)[:, :, :, tp2],
                        in1=c_cw.t[:].rearrange("p (t k) -> p t k", k=3)[:, :, tp2:tp2 + 1]
                        .broadcast_to((128, 32, 2)),
                        op=ALU.mult)
                P.add("vector", cv, reads=[hprevF, c_cw], writes=[ymx])
            P.add("vector", lambda e: e.tensor_tensor(out=ymx.t[:, 0:64], in0=ymx.t[:, 64:128],
                                                      in1=ymx.t[:, 128:192], op=ALU.add),
                  reads=[ymx], writes=[ymx])
            P.add("vector", lambda e: e.tensor_tensor(
                out=ymx.t[:, 0:64].rearrange("p (t b) -> p t b", b=2),
                in0=ymx.t[:, 0:64].rearrange("p (t b) -> p t b", b=2),
                in1=c_cb.t[:, :, None].broadcast_to((128, 32, 2)), op=ALU.add),
                reads=[ymx, c_cb], writes=[ymx])
            P.add("scalar", lambda e: e.activation(out=ymx.t[:, 0:64], in_=ymx.t[:, 0:64],
                                                   func=AF.Silu),
                  reads=[ymx], writes=[ymx])
            P.add("vector", lambda e: e.reduce_max(
                out=drow.t[:, 0:2], in_=ymx.t[:, 0:64].rearrange("p (t b) -> p b t", b=2),
                axis=AX.X, apply_absolute_value=True),
                reads=[ymx], writes=[drow])
            row_from_cols(drow, 0, 1, 2, 0)
            P.add("vector", lambda e: e.reduce_max(out=yrow.t[0:1, 300:301],
                                                   in_=yrow.t[0:1, 0:128], axis=AX.X),
                  reads=[yrow], writes=[yrow])
            P.add("vector", lambda e: e.reduce_max(out=yrow.t[0:1, 301:302],
                                                   in_=yrow.t[0:1, 128:256], axis=AX.X),
                  reads=[yrow], writes=[yrow])
            P.add("vector", lambda e: e.tensor_scalar(out=yrow.t[0:1, 300:302],
                                                      in0=yrow.t[0:1, 300:302],
                                                      scalar1=1e-5, scalar2=None, op0=ALU.max),
                  reads=[yrow], writes=[yrow])
            P.add("vector", lambda e: e.reciprocal(out=yrow.t[0:1, 302:304],
                                                   in_=yrow.t[0:1, 300:302]),
                  reads=[yrow], writes=[yrow])
            P.add("vector", lambda e: e.tensor_scalar(out=yrow.t[0:1, 302:304],
                                                      in0=yrow.t[0:1, 302:304],
                                                      scalar1=127.0, scalar2=None, op0=ALU.mult),
                  reads=[yrow], writes=[yrow])

            def mmbf(e):
                e.matmul(pb[4].t[:, 0:2], c_onesrf.t[:], yrow.t[0:1, 302:304],
                         start=True, stop=True)
                return e.matmul(pb[4].t[:, 2:4], c_onesrf.t[:], yrow.t[0:1, 300:302],
                                start=True, stop=True)
            P.add("tensor", mmbf, reads=[yrow, c_onesrf], writes=[pb[4]])
            P.add("vector", lambda e: e.tensor_copy(out=drow.t[:, 2:6], in_=pb[4].t[:, 0:4]),
                  reads=[pb[4]], writes=[drow])
            P.add("vector", lambda e: e.tensor_tensor(
                out=ivh.t[:, 0:64].rearrange("p (t b) -> p t b", b=2),
                in0=ymx.t[:, 0:64].rearrange("p (t b) -> p t b", b=2),
                in1=drow.t[:, 2:4][:, None, :].broadcast_to((128, 32, 2)), op=ALU.mult),
                reads=[ymx, drow], writes=[ivh])
            P.add("vector", lambda e: e.tensor_copy(out=hq.t[:, 0:64], in_=ivh.t[:, 0:64]),
                  reads=[ivh], writes=[hq])
            for mo in range(8):
                wslot = (mo % 2) * 4096
                def ldw2(e, mo=mo, wslot=wslot):
                    return [e.dma_start(out=w2w.t[:, wslot:wslot + 4096],
                                        in_=w2q_sc[mo * 128:(mo + 1) * 128, :])]
                P.add("sync", dmas(ldw2, 1), dma="ld", n_dma=1, reads=[w2qB], writes=[w2w])
                bank = pb[mo % 2]
                def mmw2(e, mo=mo, wslot=wslot, bank=bank):
                    last = None
                    for k in range(32):
                        last = e.matmul(bank.t[:, 0:2],
                                        w2w.t[:, wslot + k * 128:wslot + (k + 1) * 128],
                                        hq.t[:, k * 2:(k + 1) * 2],
                                        start=(k == 0), stop=(k == 31))
                    return last
                P.add("tensor", mmw2, reads=[w2w, hq], writes=[bank])
                def evw2(e, mo=mo, bank=bank):
                    return e.tensor_tensor(out=outF.t[:, mo * 2:(mo + 1) * 2],
                                           in0=bank.t[:, 0:2], in1=drow.t[:, 4:6], op=ALU.mult)
                P.add("vector", evw2, reads=[bank, drow], writes=[outF])
            P.add("vector", lambda e: e.tensor_scalar(out=outF.t[:, 0:16], in0=outF.t[:, 0:16],
                                                      scalar1=1.0 / 127.0, scalar2=None,
                                                      op0=ALU.mult),
                  reads=[outF], writes=[outF])
            P.add("vector", lambda e: e.tensor_scalar(out=outF.t[:, 0:16], in0=outF.t[:, 0:16],
                                                      scalar1=swsc.t[:, 3:4], scalar2=None,
                                                      op0=ALU.mult),
                  reads=[outF, swsc], writes=[outF])
            tapB("FoutFpre", outF, 128, 16)
            def mmsum(e):
                last = None
                for k in range(8):
                    last = e.matmul(pb[5].t[0:1, 0:2], c_ones.t[:], outF.t[:, k * 2:(k + 1) * 2],
                                    start=(k == 0), stop=(k == 7))
                return last
            P.add("tensor", mmsum, reads=[outF, c_ones], writes=[pb[5]])
            for k in range(8):
                P.add("scalar", lambda e, k=k: e.activation(out=o2pB.t[:, 0:2],
                                                            in_=outF.t[:, k * 2:(k + 1) * 2],
                                                            func=AF.Square),
                      reads=[outF], writes=[o2pB])
                def mm2(e, k=k):
                    return e.matmul(pb[5].t[64:65, 0:2], c_ones.t[:], o2pB.t[:, 0:2],
                                    start=(k == 0), stop=(k == 7))
                P.add("tensor", mm2, reads=[o2pB, c_ones], writes=[pb[5]])
            P.add("vector", lambda e: e.tensor_scalar(out=yrow.t[0:1, 310:312],
                                                      in0=pb[5].t[0:1, 0:2],
                                                      scalar1=1.0 / D, scalar2=None, op0=ALU.mult),
                  reads=[pb[5]], writes=[yrow])
            P.add("vector", lambda e: e.tensor_scalar(out=yrow.t[0:1, 312:314],
                                                      in0=pb[5].t[64:65, 0:2],
                                                      scalar1=1.0 / D, scalar2=None, op0=ALU.mult),
                  reads=[pb[5]], writes=[yrow])
            P.add("vector", lambda e: e.tensor_tensor(out=yrow.t[0:1, 314:316],
                                                      in0=yrow.t[0:1, 310:312],
                                                      in1=yrow.t[0:1, 310:312], op=ALU.mult),
                  reads=[yrow], writes=[yrow])
            P.add("vector", lambda e: e.tensor_tensor(out=yrow.t[0:1, 312:314],
                                                      in0=yrow.t[0:1, 312:314],
                                                      in1=yrow.t[0:1, 314:316], op=ALU.subtract),
                  reads=[yrow], writes=[yrow])
            P.add("scalar", lambda e: e.activation(out=yrow.t[0:1, 312:314],
                                                   in_=yrow.t[0:1, 312:314], func=AF.Sqrt,
                                                   bias=cflt.t[0:1, 0:1]),
                  reads=[yrow], writes=[yrow])
            P.add("vector", lambda e: e.reciprocal(out=yrow.t[0:1, 312:314],
                                                   in_=yrow.t[0:1, 312:314]),
                  reads=[yrow], writes=[yrow])
            def mmbm(e):
                e.matmul(pb[4].t[:, 4:6], c_onesrf.t[:], yrow.t[0:1, 310:312],
                         start=True, stop=True)
                return e.matmul(pb[4].t[:, 6:8], c_onesrf.t[:], yrow.t[0:1, 312:314],
                                start=True, stop=True)
            P.add("tensor", mmbm, reads=[yrow, c_onesrf], writes=[pb[4]])
            P.add("vector", lambda e: e.tensor_copy(out=drow.t[:, 6:8], in_=pb[4].t[:, 4:6]),
                  reads=[pb[4]], writes=[drow])
            P.add("vector", lambda e: e.tensor_copy(out=drow.t[:, 0:2], in_=pb[4].t[:, 6:8]),
                  reads=[pb[4]], writes=[drow])
            for k in range(8):
                P.add("vector", lambda e, k=k: e.tensor_tensor(out=outF.t[:, k * 2:(k + 1) * 2],
                                                               in0=outF.t[:, k * 2:(k + 1) * 2],
                                                               in1=drow.t[:, 6:8], op=ALU.subtract),
                      reads=[outF, drow], writes=[outF])
                P.add("vector", lambda e, k=k: e.tensor_tensor(out=outF.t[:, k * 2:(k + 1) * 2],
                                                               in0=outF.t[:, k * 2:(k + 1) * 2],
                                                               in1=drow.t[:, 0:2], op=ALU.mult),
                      reads=[outF, drow], writes=[outF])
                P.add("tensor", lambda e, k=k: e.transpose(pT.t[0:2, k * 128:(k + 1) * 128],
                                                           outF.t[:, k * 2:(k + 1) * 2],
                                                           c_eye.t[:]),
                      reads=[outF, c_eye], writes=pTB)
            P.add("vector", lambda e: e.tensor_copy(out=x2T.t[0:2, 0:D], in_=pT.t[0:2, 0:1024]),
                  reads=pTB, writes=[x2T])
            def ldlast(e):
                return [e.dma_start(out=x1s.t[0:1, 0:D], in_=x1_sc[T - 1:T, :]),
                        e.dma_start(out=x1s.t[1:2, 0:D], in_=x1_sc[2 * T - 1:2 * T, :])]
            P.add("sync", dmas(ldlast, 2), dma="ld", reads=[x1B], writes=[x1s])
            P.add("vector", lambda e: e.tensor_tensor(out=x2T.t[0:2, 0:D], in0=x1s.t[0:2, 0:D],
                                                      in1=x2T.t[0:2, 0:D], op=ALU.add),
                  reads=[x1s, x2T], writes=[x2T])
            def styl(e):
                return [e.dma_start(out=y_out[T - 1:T, :], in_=x2T.t[0:1, 0:D]),
                        e.dma_start(out=y_out[2 * T - 1:2 * T, :], in_=x2T.t[1:2, 0:D])]
            P.add("gpsimd", dmas(styl, 2), dma="st", reads=[x2T])

        if stop_after is None:
            final_token()
        tapB("Fymx", ymx, 128, 192)
        tapB("Fhprevf", hprevF, 128, 32 * 2 * 2)
        tapB("Fhq", hq, 128, 64)
        tapB("FoutF", outF, 128, 16)
        tapB("Fyrow", yrow, 1, 320)
        tapB("Fdrow", drow, 128, 8)
        tapB("Fx2T", x1b, 128, D)

        P.add("vector", lambda e: e.tensor_copy(out=P.junk[:, 0:1], in_=P.junk[:, 1:2]))
        P.wait_all("gpsimd")

        if max_steps is not None:
            P.steps = P.steps[:max_steps]
        with nc.Block() as block:
            P.emit_into(block, sems)

    return nc


def _constants():
    e = np.eye(128, dtype=ml_dtypes.bfloat16)
    tri = np.triu(np.ones((C, C), np.float32))
    cum = np.zeros((128, 128), np.float32)
    cum[0:64, 0:64] = tri
    cum[64:128, 64:128] = tri
    mlow = np.tril(np.ones((C, C), np.float32), -1)
    mupp = np.triu(np.ones((C, C), np.float32), 1)
    mud = np.triu(np.ones((C, C), np.float32), 0)
    ip = np.eye(C, dtype=np.float32)
    ones1 = np.ones((128, 1), np.float32)
    stack2 = lambda m: np.concatenate([m, m], axis=0).astype(np.float32)
    return dict(eye128=e, eye128f=np.eye(128, dtype=np.float32),
                cumtri=cum.astype(ml_dtypes.bfloat16),
                mlow=stack2(mlow), mupp=stack2(mupp), mud=stack2(mud),
                ipair=np.concatenate([ip, ip], axis=0).astype(ml_dtypes.bfloat16),
                ones1=ones1.astype(ml_dtypes.bfloat16),
                onesr=np.ones((1, 128), ml_dtypes.bfloat16),
                onesrf=np.ones((1, 128), np.float32))


def kernel(L0_attn_x, L0_attn_Wq, L0_attn_Wk, L0_attn_Wv, L0_attn_Wgamma,
           L0_attn_Wf1, L0_attn_Wf2, L0_attn_Wog1, L0_attn_Wog2, L0_attn_norm_w,
           L0_attn_Wo, L0_ffn_w1, L0_ffn_conv_w, L0_ffn_conv_b, L0_ffn_w2):
    T = 4096
    x = np.ascontiguousarray(L0_attn_x, dtype=np.float32)
    consts = _constants()
    common = dict(Wq=np.ascontiguousarray(L0_attn_Wq),
                  Wk=np.ascontiguousarray(L0_attn_Wk),
                  Wv=np.ascontiguousarray(L0_attn_Wv),
                  Wgamma=np.ascontiguousarray(L0_attn_Wgamma),
                  Wf1=np.ascontiguousarray(L0_attn_Wf1),
                  Wf2=np.ascontiguousarray(L0_attn_Wf2),
                  Wog1=np.ascontiguousarray(L0_attn_Wog1),
                  Wog2=np.ascontiguousarray(L0_attn_Wog2),
                  norm_w=np.ascontiguousarray(L0_attn_norm_w).reshape(1, D),
                  Wo=np.ascontiguousarray(L0_attn_Wo),
                  w1=np.ascontiguousarray(L0_ffn_w1),
                  conv_w=np.ascontiguousarray(L0_ffn_conv_w).reshape(DH, 3),
                  conv_b=np.ascontiguousarray(L0_ffn_conv_b).reshape(1, DH),
                  w2=np.ascontiguousarray(L0_ffn_w2),
                  **consts)
    nc = build_program(T)
    in_maps = []
    for c in range(8):
        xs = x[2 * c:2 * c + 2].reshape(NB * T, D)
        in_maps.append(dict(x=np.ascontiguousarray(xs), **common))
    res = run_bass_kernel_spmd(nc, in_maps, core_ids=list(range(8)))
    out = np.empty((16, T, D), np.float32)
    for c in range(8):
        out[2 * c:2 * c + 2] = res.results[c]["y"].reshape(NB, T, D)
    return out
